# version 16
# baseline (speedup 1.0000x reference)
import math
import os
import numpy as np
from contextlib import ExitStack

import concourse.bass as bass
import concourse.mybir as mybir
from concourse.bass_utils import run_bass_kernel_spmd

F32 = mybir.dt.float32
BF16 = mybir.dt.bfloat16
F16 = mybir.dt.float16
AF = mybir.ActivationFunctionType
ALU = mybir.AluOpType
AX = mybir.AxisListType

FULL = dict(B=2, S=2048, D=1024, H=16, E=8, HID=4096)
DECAY = 0.99
LN_EPS = 1e-5
N_CORES = 8


class Ctr:
    def __init__(self, es, nc, name, step=1):
        self.sem = es.enter_context(nc.semaphore(name))
        self.n = 0
        self.step = step

    def tick(self, instr):
        instr.then_inc(self.sem, self.step)
        self.n += self.step
        return self.n

    def reserve(self, k):
        base = self.n
        self.n += k * self.step
        return base

    def tick_raw(self, instr):
        instr.then_inc(self.sem, self.step)


class Waiter:

    def __init__(self):
        self.seen = {}

    def w(self, eng, key, ctr, tag):
        if tag <= 0:
            return
        k = (key, id(ctr))
        if tag <= self.seen.get(k, 0):
            return
        eng.wait_ge(ctr.sem, tag)
        self.seen[k] = tag


def build_program(P):
    B, S, D, H, E, HID = P["B"], P["S"], P["D"], P["H"], P["E"], P["HID"]
    T = B * S
    DC = D // 128
    NT = T // 128
    NTB = S // 128
    HP = H // 2
    JC = 512
    NJT = JC // 128
    NJC = HID // JC
    IC = 512
    NIC = S // IC
    IC2 = 256
    NIC2 = S // IC2
    TH = min(1024, S)
    NTH = S // TH
    NHT = HID // 128
    assert D == 1024 and DC == 8 and HP == 8 and S % IC == 0 and S % IC2 == 0

    nc = bass.Bass("TRN2", target_bir_lowering=False)

    x_ap = nc.dram_tensor("x", [B, S, D], F32, kind="ExternalInput").ap()
    wq_ap = nc.dram_tensor("wq", [D, D], F32, kind="ExternalInput").ap()
    bq_ap = nc.dram_tensor("bq", [D], F32, kind="ExternalInput").ap()
    wk_ap = nc.dram_tensor("wk", [D, D], F32, kind="ExternalInput").ap()
    bk_ap = nc.dram_tensor("bk", [D], F32, kind="ExternalInput").ap()
    wv_ap = nc.dram_tensor("wv", [D, D], F32, kind="ExternalInput").ap()
    bv_ap = nc.dram_tensor("bv", [D], F32, kind="ExternalInput").ap()
    wg_ap = nc.dram_tensor("wg", [D, D], F32, kind="ExternalInput").ap()
    bg_ap = nc.dram_tensor("bg", [D], F32, kind="ExternalInput").ap()
    wo_ap = nc.dram_tensor("wo", [D, D], F32, kind="ExternalInput").ap()
    bo_ap = nc.dram_tensor("bo", [D], F32, kind="ExternalInput").ap()
    gam_ap = nc.dram_tensor("gam", [D], F32, kind="ExternalInput").ap()
    bet_ap = nc.dram_tensor("bet", [D], F32, kind="ExternalInput").ap()
    w1_ap = nc.dram_tensor("w1", [E, D, HID], F32, kind="ExternalInput").ap()
    b1_ap = nc.dram_tensor("b1", [E, HID], F32, kind="ExternalInput").ap()
    w2_ap = nc.dram_tensor("w2", [E, HID, D], F32, kind="ExternalInput").ap()
    b2_ap = nc.dram_tensor("b2", [E, D], F32, kind="ExternalInput").ap()
    wr_ap = nc.dram_tensor("wr", [D, E], F32, kind="ExternalInput").ap()
    br_ap = nc.dram_tensor("br", [E], F32, kind="ExternalInput").ap()
    out_ap = nc.dram_tensor("out", [B, S, D], F32, kind="ExternalOutput").ap()
    DBG = bool(os.environ.get("KDBG"))
    if DBG:
        dbg_ret = nc.dram_tensor("dbg_ret", [128, DC * S], F32, kind="ExternalOutput").ap()
        dbg_qp = nc.dram_tensor("dbg_qp", [128, S], F32, kind="ExternalOutput").ap()
        dbg_kp = nc.dram_tensor("dbg_kp", [128, S], F32, kind="ExternalOutput").ap()
        dbg_cq = nc.dram_tensor("dbg_cq", [128, S], F32, kind="ExternalOutput").ap()
        dbg_ck = nc.dram_tensor("dbg_ck", [128, S], F32, kind="ExternalOutput").ap()
        dbg_xn = nc.dram_tensor("dbg_xn", [128, DC * T], F32, kind="ExternalOutput").ap()

    es = ExitStack()
    _uq = [0]

    def _un(n):
        _uq[0] += 1
        return f"{n}_{_uq[0]}"

    def u_sb(name, shape, dtype, **kw):
        return nc.sbuf_tensor(_un(name), shape, dtype, **kw)

    def u_ps(name, shape, dtype=F32, **kw):
        return nc.psum_tensor(_un(name), shape, dtype, **kw)

    def sb(name, shape, dtype, side=None):
        kw = {} if side is None else dict(side=side)
        return es.enter_context(u_sb(name, shape, dtype, **kw))

    xn_t = sb("xn_t", [128, DC * T], BF16)
    wrt = sb("wrt", [128, NT * 8], F32)
    cqrow = sb("cqrow", [1, S], BF16, side="right")
    ckcols = sb("ckcols", [128, S // 128], F32)
    iota_c = sb("iota_c", [128, S // 128], F32)
    bq_t = sb("bq_t", [128, DC], F32)
    bk_t = sb("bk_t", [128, DC], F32)
    bg_t = sb("bg_t", [128, DC], F32)
    gam_t = sb("gam_t", [128, DC], F32)
    bet_t = sb("bet_t", [128, DC], F32)
    bv_bf = sb("bv_bf", [1, D], BF16)
    bo_bf = sb("bo_bf", [1, D], BF16)
    br_bf = sb("br_bf", [1, E], BF16)
    b2_bf = sb("b2_bf", [E, D], BF16)
    id_bf = sb("id_bf", [128, 128], BF16)
    xo = [sb("xo0", [128, D], F32)]
    oo = [sb("oo0", [128, D], F32)]
    id_f = sb("id_f", [128, 128], F32)
    tri_bf = sb("tri_bf", [128, 128], BF16)
    bmask = sb("bmask", [128, 4 * 512], BF16)
    ones_r = sb("ones_r", [1, 128], BF16)
    ones_c = sb("ones_c", [128, 1], BF16)

    block = es.enter_context(nc.Block())
    CT = Ctr(es, nc, "ct")
    CV = Ctr(es, nc, "cv")
    CS = Ctr(es, nc, "cs")
    CG = Ctr(es, nc, "cg")
    DW = Ctr(es, nc, "dw", 16)
    DX = Ctr(es, nc, "dx", 16)
    DG = Ctr(es, nc, "dg", 16)
    W = Waiter()

    LN99 = math.log(DECAY)
    LN8I = math.log(1.0 / math.sqrt(D // H))

    def floor_now():
        return (CT.n, CV.n, CS.n, DW.n, DX.n, CG.n, DG.n)

    def wait_floor(eng, key, f):
        W.w(eng, key, CT, f[0])
        W.w(eng, key, CV, f[1])
        W.w(eng, key, CS, f[2])
        W.w(eng, key, DW, f[3])
        W.w(eng, key, DX, f[4])
        W.w(eng, key, CG, f[5])
        W.w(eng, key, DG, f[6])

    with es:
        p0b = ExitStack()
        def sb0(name, shape, dtype):
            return p0b.enter_context(u_sb(name, shape, dtype))
        iota_f = sb0("iota_f", [128, 128], F32)
        iota_row = sb0("iota_row", [1, S], F32)
        rows_b = [sb0(f"rows_{i}", [8, 128], F32) for i in range(5)]
        bv_row = sb0("bv_row", [1, D], F32)
        bo_row = sb0("bo_row", [1, D], F32)
        br_row = sb0("br_row", [1, E], F32)
        b2_row = sb0("b2_row", [E, D], F32)
        wr_st = sb0("wr_st", [128, DC * E], F32)
        wr_bf = sb0("wr_bf", [128, DC * E], BF16)
        xin = [sb0(f"xin{i}", [128, 4 * D], F32) for i in range(2)]
        xnk = [sb0(f"xnk{i}", [128, 4 * D], BF16) for i in range(2)]
        sqb = sb0("sqb", [128, 4 * D], BF16)
        st1 = sb0("st1", [128, 24], F32)
        ltb = sb0("ltb", [128, 8], F32)
        eqb = sb0("eqb", [128, 8], F32)
        t2b = sb0("t2b", [128, 8], F32)
        kpb = sb0("kpb", [128, 8], F32)
        exb = sb0("exb", [128, 8], F32)
        ekb = sb0("ekb", [128, 8], F32)
        c_ln8i = sb0("c_ln8i", [1, 1], F32)

        @block.sync
        def _(sy):
            W.w(sy, "sy", DW, DW.n)
            DW.tick(sy.dma_start(out=wr_st[:].rearrange("p (dc e) -> p dc e", dc=DC),
                                 in_=wr_ap.rearrange("(dc p) e -> p dc e", p=128)))
            W.w(sy, "sy", DW, DW.n)
            DW.tick(sy.dma_start(out=rows_b[0][:], in_=bq_ap.rearrange("(a c) -> a c", c=128)))
            W.w(sy, "sy", DW, DW.n)
            DW.tick(sy.dma_start(out=rows_b[1][:], in_=bk_ap.rearrange("(a c) -> a c", c=128)))
            W.w(sy, "sy", DW, DW.n)
            DW.tick(sy.dma_start(out=rows_b[2][:], in_=bg_ap.rearrange("(a c) -> a c", c=128)))
            W.w(sy, "sy", DW, DW.n)
            DW.tick(sy.dma_start(out=rows_b[3][:], in_=gam_ap.rearrange("(a c) -> a c", c=128)))
            W.w(sy, "sy", DW, DW.n)
            DW.tick(sy.dma_start(out=rows_b[4][:], in_=bet_ap.rearrange("(a c) -> a c", c=128)))
            W.w(sy, "sy", DW, DW.n)
            DW.tick(sy.dma_start(out=bv_row[:], in_=bv_ap.rearrange("(o d) -> o d", o=1)))
            W.w(sy, "sy", DW, DW.n)
            DW.tick(sy.dma_start(out=bo_row[:], in_=bo_ap.rearrange("(o d) -> o d", o=1)))
            W.w(sy, "sy", DW, DW.n)
            DW.tick(sy.dma_start(out=br_row[:], in_=br_ap.rearrange("(o d) -> o d", o=1)))
            W.w(sy, "sy", DW, DW.n)
            DW.tick(sy.dma_start(out=b2_row[:], in_=b2_ap))
        dma_const = DW.n

        @block.gpsimd
        def _(g):
            CG.tick(g.iota(iota_f[:], [[1, 128]], channel_multiplier=-1,
                           allow_small_or_imprecise_dtypes=True))
            CG.tick(g.iota(iota_row[:], [[1, S]], channel_multiplier=0,
                           allow_small_or_imprecise_dtypes=True))
            CG.tick(g.iota(iota_c[:], [[128, S // 128]], channel_multiplier=1,
                           allow_small_or_imprecise_dtypes=True))
        t_iota = CG.n

        @block.vector
        def _(v):
            W.w(v, "v", CG, t_iota)
            W.w(v, "v", CV, CV.n)
            CV.tick(v.tensor_scalar(id_bf[:], iota_f[:], 0.0, None, ALU.is_equal))
            W.w(v, "v", CV, CV.n)
            CV.tick(v.tensor_scalar(id_f[:], iota_f[:], 0.0, None, ALU.is_equal))
            W.w(v, "v", CV, CV.n)
            CV.tick(v.tensor_scalar(tri_bf[:], iota_f[:], 0.0, None, ALU.is_ge))
            for bp in range(4):
                for isub in range(4):
                    dst = bmask[:, bp * 512 + isub * 128: bp * 512 + (isub + 1) * 128]
                    W.w(v, "v", CV, CV.n)
                    if isub < bp:
                        CV.tick(v.memset(dst, 0.0))
                    elif isub == bp:
                        CV.tick(v.tensor_copy(dst, tri_bf[:]))
                    else:
                        CV.tick(v.memset(dst, 1.0))
            W.w(v, "v", CV, CV.n)
            CV.tick(v.memset(ones_r[:], 1.0))
            W.w(v, "v", CV, CV.n)
            CV.tick(v.memset(ones_c[:], 1.0))
            W.w(v, "v", DW, dma_const)
            W.w(v, "v", CV, CV.n)
            CV.tick(v.tensor_copy(wr_bf[:], wr_st[:]))
            W.w(v, "v", CV, CV.n)
            CV.tick(v.tensor_copy(bv_bf[:], bv_row[:]))
            W.w(v, "v", CV, CV.n)
            CV.tick(v.tensor_copy(bo_bf[:], bo_row[:]))
            W.w(v, "v", CV, CV.n)
            CV.tick(v.tensor_copy(br_bf[:], br_row[:]))
            W.w(v, "v", CV, CV.n)
            CV.tick(v.tensor_copy(b2_bf[:], b2_row[:]))
        @block.vector
        def _(v):
            W.w(v, "v", CV, CV.n)
            CV.tick(v.memset(c_ln8i[:], LN8I))
        t_c8 = CV.n
        t_consts_v = CV.n

        @block.scalar
        def _(s):
            W.w(s, "s", CG, t_iota)
            W.w(s, "s", CV, t_c8)
            W.w(s, "s", CS, CS.n)
            CS.tick(s.activation(cqrow[:], iota_row[:], AF.Exp,
                                 bias=c_ln8i[0:1, 0:1], scale=LN99))
            W.w(s, "s", CS, CS.n)
            CS.tick(s.activation(ckcols[:], iota_c[:], AF.Exp,
                                 bias=0.0, scale=-LN99))
        t_decay_s = CS.n

        with ExitStack() as p0s:
            ps_b = p0s.enter_context(u_ps("ps_b", [128, 512], F32))
            ps_t = p0s.enter_context(u_ps("ps_t", [128, 2048], BF16))
            ps_r = p0s.enter_context(u_ps("ps_r", [128, 512], F32))

            bias_t = [bq_t, bk_t, bg_t, gam_t, bet_t]

            @block.tensor
            def _(t):
                W.w(t, "t", CV, t_consts_v)
                for i in range(5):
                    CT.tick(t.transpose(ps_b[:, i * 8:(i + 1) * 8], rows_b[i][:],
                                        id_f[0:8, 0:8]))
            t_biasT = CT.n

            @block.vector
            def _(v):
                W.w(v, "v", CT, t_biasT)
                for i in range(5):
                    W.w(v, "v", CV, CV.n)
                    CV.tick(v.tensor_copy(bias_t[i][:], ps_b[:, i * 8:(i + 1) * 8]))
            t_biasT_v = CV.n

            NST = NT // 4
            apply_tag = [0] * NST
            stats_tag = [0] * NST
            trans_tag = [0] * NST
            xev_tag = [0] * NST
            rr_tag = [0] * NT
            w_tag = [0] * NT

            for st in range(NST):
                sl = st % 2
                b_i, r_i = divmod(st * 4, NTB)

                @block.scalar
                def _(s, st=st, sl=sl, b_i=b_i, r_i=r_i):
                    if st >= 2:
                        W.w(s, "s", CV, apply_tag[st - 2])
                        W.w(s, "s", CV, stats_tag[st - 2])
                    W.w(s, "s", DX, DX.n)
                    DX.tick(s.dma_start(
                        out=xin[sl][:].rearrange("p (a d) -> p a d", a=4),
                        in_=x_ap[b_i, r_i * 128:(r_i + 4) * 128, :]
                        .rearrange("(a p) d -> p a d", p=128)))
                dx_here = DX.n

                @block.vector
                def _(v, st=st, sl=sl):
                    W.w(v, "v", DX, dx_here)
                    xv = xin[sl][:].rearrange("p (a d) -> p a d", a=4)
                    W.w(v, "v", CV, CV.n)
                    CV.tick(v.tensor_mul(sqb[:], xin[sl][:], xin[sl][:]))
                    CV.tick(v.reduce_sum(st1[:, 0:4].rearrange("p (a o) -> p a o", o=1),
                                         xv, axis=AX.X))
                    W.w(v, "v", CV, CV.n)
                    CV.tick(v.reduce_sum(st1[:, 4:8].rearrange("p (a o) -> p a o", o=1),
                                         sqb[:].rearrange("p (a d) -> p a d", a=4),
                                         axis=AX.X))
                    W.w(v, "v", CV, CV.n)
                    CV.tick(v.tensor_scalar(st1[:, 8:12], st1[:, 0:4], 1.0 / D, None,
                                            ALU.mult))
                    CV.tick(v.tensor_scalar(st1[:, 12:16], st1[:, 4:8], 1.0 / D, None,
                                            ALU.mult))
                    W.w(v, "v", CV, CV.n)
                    CV.tick(v.tensor_mul(st1[:, 16:20], st1[:, 8:12], st1[:, 8:12]))
                    W.w(v, "v", CV, CV.n)
                    CV.tick(v.scalar_tensor_tensor(st1[:, 16:20], st1[:, 16:20], -1.0,
                                                   st1[:, 12:16], ALU.mult, ALU.add))
                    W.w(v, "v", CV, CV.n)
                    CV.tick(v.tensor_scalar(st1[:, 16:20], st1[:, 16:20], LN_EPS, None,
                                            ALU.add))
                vtag1 = CV.n

                @block.scalar
                def _(s, st=st, sl=sl):
                    W.w(s, "s", CS, CS.n)
                    W.w(s, "s", CV, vtag1)
                    CS.tick(s.activation(st1[:, 16:20], st1[:, 16:20], AF.Ln))
                    W.w(s, "s", CS, CS.n)
                    CS.tick(s.activation(st1[:, 20:24], st1[:, 16:20], AF.Exp,
                                         scale=-0.5))
                rstd_tag = CS.n

                @block.vector
                def _(v, st=st):
                    W.w(v, "v", CS, rstd_tag)
                    W.w(v, "v", CV, CV.n)
                    CV.tick(v.scalar_tensor_tensor(st1[:, 8:12], st1[:, 8:12], -1.0,
                                                   st1[:, 20:24], ALU.mult, ALU.mult))
                    stats_tag[st] = CV.n
                negm_tag = CV.n

                @block.scalar
                def _(s, st=st, sl=sl):
                    W.w(s, "s", CV, negm_tag)
                    if st >= 2:
                        W.w(s, "s", CT, trans_tag[st - 2])
                    for a in range(4):
                        W.w(s, "s", CS, CS.n)
                        CS.tick(s.activation(xnk[sl][:, a * D:(a + 1) * D],
                                             xin[sl][:, a * D:(a + 1) * D],
                                             AF.Identity, bias=st1[:, 8 + a:9 + a],
                                             scale=st1[:, 20 + a:21 + a]))
                    apply_tag[st] = CS.n

                sub_ev = [0] * 4
                for a in range(4):
                    @block.tensor
                    def _(t, st=st, sl=sl, a=a):
                        W.w(t, "t", CS, apply_tag[st])
                        if a >= 2:
                            W.w(t, "t", CV, sub_ev[a - 2])
                        elif st >= 1:
                            W.w(t, "t", CV, xev_tag[st - 1])
                        for dc in range(DC):
                            CT.tick(t.transpose(
                                ps_t[:, (a % 2) * 1024 + dc * 128:
                                     (a % 2) * 1024 + (dc + 1) * 128],
                                xnk[sl][:, a * D + dc * 128: a * D + (dc + 1) * 128],
                                id_bf[:]))
                    tr_here = CT.n

                    @block.vector
                    def _(v, st=st, a=a, tr_here=tr_here):
                        W.w(v, "v", CT, tr_here)
                        W.w(v, "v", CV, CV.n)
                        i = st * 4 + a
                        CV.tick(v.tensor_copy(
                            xn_t[:].rearrange("p (dc t) -> p dc t", dc=DC)
                            [:, :, i * 128:(i + 1) * 128],
                            ps_t[:, (a % 2) * 1024:(a % 2) * 1024 + 1024]
                            .rearrange("p (dc t) -> p dc t", dc=DC)))
                    sub_ev[a] = CV.n
                trans_tag[st] = CT.n
                xev_tag[st] = CV.n

                for a in range(4):
                    i = st * 4 + a

                    @block.tensor
                    def _(t, i=i, a=a):
                        W.w(t, "t", CV, sub_ev[a])
                        if i >= 1:
                            W.w(t, "t", CV, rr_tag[i - 1])
                        for dc in range(DC):
                            CT.tick(t.matmul(ps_r[:, 0:8],
                                             xn_t[:, dc * T + i * 128: dc * T + (i + 1) * 128],
                                             wr_bf[:, dc * 8:(dc + 1) * 8],
                                             start=(dc == 0), stop=False))
                        CT.tick(t.matmul(ps_r[:, 0:8], ones_r[0:1, :], br_bf[0:1, :],
                                         start=False, stop=True))
                    rmm_tag = CT.n

                    @block.vector
                    def _(v, i=i, rmm_tag=rmm_tag):
                        W.w(v, "v", CT, rmm_tag)
                        W.w(v, "v", CV, CV.n)
                        CV.tick(v.tensor_copy(ltb[:], ps_r[:, 0:8]))
                        rr_tag[i] = CV.n
                        W.w(v, "v", CV, CV.n)
                        CV.tick(v.reduce_max(eqb[:, 4:5], ltb[:], axis=AX.X))
                        W.w(v, "v", CV, CV.n)
                        CV.tick(v.tensor_scalar(eqb[:, 5:6], eqb[:, 4:5], -1.0, None, ALU.mult))
                        CV.tick(v.tensor_scalar(eqb[:, 6:7] if False else kpb[:], ltb[:],
                                                eqb[:, 4:5], None, ALU.is_equal))
                        W.w(v, "v", CV, CV.n)
                        CV.tick(v.scalar_tensor_tensor(t2b[:], kpb[:], -3.0e38, ltb[:],
                                                       ALU.mult, ALU.add))
                        W.w(v, "v", CV, CV.n)
                        CV.tick(v.reduce_max(eqb[:, 0:1], t2b[:], axis=AX.X))
                        W.w(v, "v", CV, CV.n)
                        CV.tick(v.tensor_scalar(kpb[:], ltb[:], eqb[:, 0:1], None, ALU.is_ge))
                    ex_tag = CV.n

                    @block.scalar
                    def _(s, i=i, ex_tag=ex_tag):
                        W.w(s, "s", CS, CS.n)
                        W.w(s, "s", CV, ex_tag)
                        CS.tick(s.activation(exb[:], ltb[:], AF.Exp, bias=eqb[:, 5:6]))
                    exs_tag = CS.n

                    @block.vector
                    def _(v, i=i, exs_tag=exs_tag):
                        W.w(v, "v", CS, exs_tag)
                        W.w(v, "v", CV, CV.n)
                        CV.tick(v.tensor_mul(ekb[:], exb[:], kpb[:]))
                        W.w(v, "v", CV, CV.n)
                        CV.tick(v.reduce_sum(eqb[:, 4:5], ekb[:], axis=AX.X))
                        W.w(v, "v", CV, CV.n)
                        CV.tick(v.reciprocal(eqb[:, 5:6], eqb[:, 4:5]))
                        W.w(v, "v", CV, CV.n)
                        CV.tick(v.tensor_scalar(wrt[:, i * 8:(i + 1) * 8], ekb[:],
                                                eqb[:, 5:6], None, ALU.mult))
                        w_tag[i] = CV.n

            p0_done_v = CV.n
        p0b.close()
        p0_floor = (CT.n, CV.n, CS.n, DW.n, DX.n)

        a2_done_s = 0
        a2_done_v = 0
        stt_tag = [0] * NTB
        prev_add_last = 0
        prev_odma = 0

        pre_out_floor = None
        sB_prev = [None]
        for b in range(B):
            tok0 = b * S
            sB = ExitStack()

            with ExitStack() as sA:
                retained = sA.enter_context(
                    u_sb("ret", [128, DC * S], BF16))
                s01 = sA.enter_context(ExitStack())
                Vt = s01.enter_context(u_sb("Vt", [128, NTB * D], BF16))

                ph_floor = pre_out_floor if pre_out_floor else floor_now()
                cs_floor = ph_floor[2]
                with ExitStack() as s0:
                    wv_bf = s0.enter_context(u_sb("wv_bf", [128, DC * D], BF16))
                    stgv = s0.enter_context(u_sb("stgv", [128, DC * 128], F32))
                    ps_v = [s0.enter_context(u_ps(f"ps_v{i}", [128, 1024], F32))
                            for i in range(2)]

                    vcast = []
                    for hf in range(8):
                        @block.sync
                        def _(sy, hf=hf):
                            wait_floor(sy, "sy", ph_floor)
                            if hf >= 1:
                                W.w(sy, "sy", CV, vcast[hf - 1])
                            W.w(sy, "sy", DW, DW.n)
                            DW.tick(sy.dma_start(
                                out=stgv[:].rearrange("p (dc n) -> p dc n", dc=DC),
                                in_=wv_ap[:, hf * 128:(hf + 1) * 128]
                                .rearrange("(dc p) n -> p dc n", p=128)))
                        dwh = DW.n

                        @block.vector
                        def _(v, hf=hf):
                            W.w(v, "v", DW, dwh)
                            W.w(v, "v", CV, CV.n)
                            CV.tick(v.tensor_copy(
                                wv_bf[:].rearrange("p (dc n) -> p dc n", dc=DC)
                                [:, :, hf * 128:(hf + 1) * 128],
                                stgv[:].rearrange("p (dc n) -> p dc n", dc=DC)))
                        vcast.append(CV.n)
                    wv_ready = vcast[-1]

                    cs_v0 = CS.reserve(NTB)
                    vmm = [0] * NTB
                    for tt in range(NTB):
                        par = tt % 2

                        @block.tensor
                        def _(t, tt=tt, par=par):
                            W.w(t, "t", CV, wv_ready)
                            W.w(t, "t", CS, cs_floor)
                            if tt >= 2:
                                W.w(t, "t", CS, cs_v0 + tt - 1)
                            for dc in range(DC):
                                for n2 in range(2):
                                    CT.tick(t.matmul(
                                        ps_v[par][:, n2 * 512:(n2 + 1) * 512],
                                        xn_t[:, dc * T + tok0 + tt * 128:
                                             dc * T + tok0 + (tt + 1) * 128],
                                        wv_bf[:, dc * D + n2 * 512: dc * D + (n2 + 1) * 512],
                                        start=(dc == 0), stop=False))
                            for n2 in range(2):
                                CT.tick(t.matmul(ps_v[par][:, n2 * 512:(n2 + 1) * 512],
                                                 ones_r[0:1, :],
                                                 bv_bf[0:1, n2 * 512:(n2 + 1) * 512],
                                                 start=False, stop=True))
                            vmm[tt] = CT.n

                    @block.scalar
                    def _(s):
                        for tt in range(NTB):
                            W.w(s, "s", CT, vmm[tt])
                            CS.tick_raw(s.activation(Vt[:, tt * D:(tt + 1) * D],
                                                     ps_v[tt % 2][:], AF.Copy))
                    assert CS.n == cs_v0 + NTB

                a1_floor = floor_now()
                with ExitStack() as s1:
                    qp = s1.enter_context(u_sb("qp", [128, S], BF16))
                    kp = s1.enter_context(u_sb("kp", [128, S], BF16))
                    cq = s1.enter_context(u_sb("cq", [128, S], BF16))
                    wqk_st = s1.enter_context(u_sb("wqk_st", [128, DC * 32], F32))
                    wq_bf = s1.enter_context(u_sb("wq_bf", [128, DC * 128], BF16))
                    wk_bf = s1.enter_context(u_sb("wk_bf", [128, DC * 128], BF16))
                    st_buf = s1.enter_context(u_sb("st_buf", [128, 3 * IC], BF16))
                    ps_qk = s1.enter_context(u_ps("ps_qk", [128, S], F32))
                    ps_s = [s1.enter_context(u_ps(f"ps_s{i}", [128, IC], F32))
                            for i in range(3)]
                    ps_ret = [s1.enter_context(u_ps(f"ps_r{i}", [64, IC], F32))
                              for i in range(1)]

                    dec_cp = [0, 0]
                    for which2, (dst2, rw) in enumerate(((cq, cqrow),)):
                        for c in range(S // 512):
                            spar = c % 2

                            @block.tensor
                            def _(t, which2=which2, dst2=dst2, rw=rw, c=c, spar=spar):
                                W.w(t, "t", CS, t_decay_s)
                                W.w(t, "t", CS, a1_floor[2])
                                W.w(t, "t", CV, dec_cp[spar])
                                W.w(t, "t", CV, a1_floor[1])
                                CT.tick(t.matmul(ps_s[spar][:],
                                                 ones_r[0:1, :],
                                                 rw[0:1, c * 512:(c + 1) * 512],
                                                 start=True, stop=True))
                            mm_here = CT.n

                            @block.vector
                            def _(v, dst2=dst2, c=c, spar=spar, mm_here=mm_here):
                                W.w(v, "v", CT, mm_here)
                                W.w(v, "v", CV, CV.n)
                                CV.tick(v.tensor_copy(dst2[:, c * 512:(c + 1) * 512],
                                                      ps_s[spar][:]))
                            dec_cp[spar] = CV.n
                    sev_init = list(dec_cp)

                    qcast = [0] * HP
                    kcast = [0] * HP
                    qmm = [0] * HP
                    kmm = [0] * HP
                    qev = [0] * HP
                    kev = [0] * HP
                    sev_last = sev_init + [0]
                    slot_av = [0, 0, 0]
                    ret_ev = [0, 0]
                    u_idx = 0

                    for hp in range(HP):
                        for which in range(2):
                            wap = wq_ap if which == 0 else wk_ap
                            wbf = wq_bf if which == 0 else wk_bf

                            for mh in range(4):
                                @block.sync
                                def _(sy, hp=hp, which=which, wap=wap, mh=mh):
                                    wait_floor(sy, "sy", a1_floor)
                                    prev = 0
                                    if mh >= 1:
                                        prev = half_cast
                                    elif which == 1:
                                        prev = qcast[hp]
                                    elif hp > 0:
                                        prev = kcast[hp - 1]
                                    W.w(sy, "sy", CV, prev)
                                    W.w(sy, "sy", DW, DW.n)
                                    DW.tick(sy.dma_start(
                                        out=wqk_st[:].rearrange("p (dc m) -> p dc m", dc=DC // 4),
                                        in_=wap[mh * 256:(mh + 1) * 256, hp * 128:(hp + 1) * 128]
                                        .rearrange("(dc p) m -> p dc m", p=128)))
                                dwh = DW.n

                                @block.vector
                                def _(v, hp=hp, which=which, wbf=wbf, mh=mh):
                                    W.w(v, "v", DW, dwh)
                                    if hp > 0:
                                        W.w(v, "v", CT, (qmm if which == 0 else kmm)[hp - 1])
                                    W.w(v, "v", CV, CV.n)
                                    CV.tick(v.tensor_copy(
                                        wbf[:].rearrange("p (dc m) -> p dc m", dc=DC)
                                        [:, mh * (DC // 4):(mh + 1) * (DC // 4), :],
                                        wqk_st[:].rearrange("p (dc m) -> p dc m", dc=DC // 4)))
                                half_cast = CV.n
                            if which == 0:
                                qcast[hp] = CV.n
                            else:
                                kcast[hp] = CV.n

                        @block.tensor
                        def _(t, hp=hp):
                            W.w(t, "t", CV, qcast[hp])
                            if hp > 0:
                                W.w(t, "t", CV, kev[hp - 1])
                            for dc in range(DC):
                                for ncs in range(S // 512):
                                    CT.tick(t.matmul(
                                        ps_qk[:, ncs * 512:(ncs + 1) * 512],
                                        wq_bf[:, dc * 128:(dc + 1) * 128],
                                        xn_t[:, dc * T + tok0 + ncs * 512:
                                             dc * T + tok0 + (ncs + 1) * 512],
                                        start=(dc == 0), stop=(dc == DC - 1)))
                            qmm[hp] = CT.n

                        @block.vector
                        def _(v, hp=hp):
                            W.w(v, "v", CT, qmm[hp])
                            W.w(v, "v", CV, CV.n)
                            CV.tick(v.scalar_tensor_tensor(qp[:], ps_qk[:],
                                                           bq_t[:, hp:hp + 1], cq[:],
                                                           ALU.add, ALU.mult))
                            qev[hp] = CV.n

                        @block.tensor
                        def _(t, hp=hp):
                            W.w(t, "t", CV, kcast[hp])
                            W.w(t, "t", CV, qev[hp])
                            for dc in range(DC):
                                for ncs in range(S // 512):
                                    CT.tick(t.matmul(
                                        ps_qk[:, ncs * 512:(ncs + 1) * 512],
                                        wk_bf[:, dc * 128:(dc + 1) * 128],
                                        xn_t[:, dc * T + tok0 + ncs * 512:
                                             dc * T + tok0 + (ncs + 1) * 512],
                                        start=(dc == 0), stop=(dc == DC - 1)))
                            kmm[hp] = CT.n

                        @block.vector
                        def _(v, hp=hp):
                            W.w(v, "v", CT, kmm[hp])
                            W.w(v, "v", CV, CV.n)
                            CV.tick(v.tensor_scalar(kp[:], ps_qk[:],
                                                    bk_t[:, hp:hp + 1], None, ALU.add))
                            kev[hp] = CV.n

                        for h2 in range(2):
                            po = 64 * h2
                            hgl = 2 * hp + h2
                            for ic in range(NIC):
                                njt = (ic + 1) * (IC // 128)
                                rpar = u_idx % 2
                                smm = [0] * njt
                                sev = [0] * njt
                                avt = [0] * njt

                                def S_sec(jt, hp=hp, po=po, ic=ic, smm=smm, sev=sev):
                                    @block.tensor
                                    def _(t, jt=jt):
                                        spar = jt % 3
                                        W.w(t, "t", CV,
                                            sev[jt - 3] if jt >= 3 else sev_last[spar])
                                        W.w(t, "t", CV, qev[hp])
                                        W.w(t, "t", CV, kev[hp])
                                        CT.tick(t.matmul(
                                            ps_s[spar][:],
                                            kp[po:po + 64, jt * 128:(jt + 1) * 128],
                                            qp[po:po + 64, ic * IC:(ic + 1) * IC],
                                            start=True, stop=True))
                                        smm[jt] = CT.n

                                def EV_sec(jt, ic=ic, smm=smm, sev=sev, avt=avt):
                                    @block.vector
                                    def _(v, jt=jt):
                                        spar = jt % 3
                                        W.w(v, "v", CT, smm[jt])
                                        W.w(v, "v", CT,
                                            avt[jt - 3] if jt >= 3 else slot_av[jt % 3])
                                        dst0 = (jt % 3) * IC
                                        band0 = ic * (IC // 128)
                                        ckc = ckcols[:, jt:jt + 1]
                                        W.w(v, "v", CV, CV.n)
                                        if jt < band0:
                                            CV.tick(v.tensor_scalar(
                                                st_buf[:, dst0:dst0 + IC],
                                                ps_s[spar][:], ckc, None, ALU.mult))
                                        else:
                                            bp = jt - band0
                                            CV.tick(v.scalar_tensor_tensor(
                                                st_buf[:, dst0:dst0 + IC],
                                                ps_s[spar][:], ckc,
                                                bmask[:, bp * 512:(bp + 1) * 512],
                                                ALU.mult, ALU.mult))
                                        sev[jt] = CV.n

                                def AV_sec(jt, hgl=hgl, njt=njt, rpar=rpar,
                                           sev=sev, avt=avt):
                                    @block.tensor
                                    def _(t, jt=jt):
                                        W.w(t, "t", CV, sev[jt])
                                        if jt == 0:
                                            W.w(t, "t", CV, ret_ev[0])
                                        CT.tick(t.matmul(
                                            ps_ret[0][:],
                                            Vt[:, jt * D + hgl * 64: jt * D + hgl * 64 + 64],
                                            st_buf[:, (jt % 3) * IC:(jt % 3 + 1) * IC],
                                            start=(jt == 0), stop=(jt == njt - 1)))
                                        avt[jt] = CT.n

                                S_sec(0)
                                if njt > 1:
                                    S_sec(1)
                                    EV_sec(0)
                                for jt in range(2, njt):
                                    S_sec(jt)
                                    EV_sec(jt - 1)
                                    AV_sec(jt - 2)
                                if njt > 1:
                                    EV_sec(njt - 1)
                                    AV_sec(njt - 2)
                                else:
                                    EV_sec(0)
                                AV_sec(njt - 1)

                                for jt in range(njt):
                                    slot_av[jt % 3] = max(slot_av[jt % 3], avt[jt])
                                for jt in range(njt):
                                    sev_last[jt % 2] = max(sev_last[jt % 2], sev[jt])
                                av_last = avt[njt - 1]

                                @block.vector
                                def _(v, hp=hp, po=po, ic=ic, av_last=av_last):
                                    W.w(v, "v", CT, av_last)
                                    W.w(v, "v", CV, CV.n)
                                    CV.tick(v.tensor_copy(
                                        retained[po:po + 64,
                                                 hp * S + ic * IC: hp * S + (ic + 1) * IC],
                                        ps_ret[0][:]))
                                    ret_ev[0] = CV.n
                                u_idx += 1
                    a1_done_v = CV.n
                    if DBG and b == 0:
                        dbf = s1.enter_context(u_sb("dbf", [128, S], F32))
                        dcp = [0]

                        @block.vector
                        def _(v):
                            W.w(v, "v", CV, a1_done_v)
                            W.w(v, "v", CV, CV.n)
                            CV.tick(v.tensor_copy(dbf[:], qp[:]))
                        dcp[0] = CV.n

                        @block.sync
                        def _(sy):
                            W.w(sy, "sy", CV, dcp[0])
                            DW.tick(sy.dma_start(out=dbg_qp, in_=dbf[:]))
                        dw_q = DW.n

                        @block.vector
                        def _(v):
                            W.w(v, "v", DW, dw_q)
                            W.w(v, "v", CV, CV.n)
                            CV.tick(v.tensor_copy(dbf[:], kp[:]))
                        dcp[0] = CV.n

                        @block.sync
                        def _(sy):
                            W.w(sy, "sy", CV, dcp[0])
                            DW.tick(sy.dma_start(out=dbg_kp, in_=dbf[:]))
                            sy.wait_ge(DW.sem, DW.n)

                        @block.vector
                        def _(v):
                            W.w(v, "v", CV, CV.n)
                            CV.tick(v.tensor_copy(dbf[:], cq[:]))
                        dcp[0] = CV.n

                        @block.sync
                        def _(sy):
                            W.w(sy, "sy", CV, dcp[0])
                            DW.tick(sy.dma_start(out=dbg_cq, in_=dbf[:]))
                            sy.wait_ge(DW.sem, DW.n)

                        @block.vector
                        def _(v):
                            W.w(v, "v", CV, CV.n)
                            CV.tick(v.tensor_copy(dbf[:], ck[:]))
                        dcp[0] = CV.n

                        @block.sync
                        def _(sy):
                            W.w(sy, "sy", CV, dcp[0])
                            DW.tick(sy.dma_start(out=dbg_ck, in_=dbf[:]))
                            sy.wait_ge(DW.sem, DW.n)

                        for dcx in range(DC):
                            @block.vector
                            def _(v, dcx=dcx):
                                W.w(v, "v", CV, CV.n)
                                CV.tick(v.tensor_copy(dbf[:, 0:S],
                                                      xn_t[:, dcx * T: dcx * T + S]))
                            dctx = CV.n

                            @block.sync
                            def _(sy, dcx=dcx):
                                W.w(sy, "sy", CV, dctx)
                                DW.tick(sy.dma_start(out=dbg_xn[:, dcx * T: dcx * T + S],
                                                     in_=dbf[:, 0:S]))
                                sy.wait_ge(DW.sem, DW.n)

                        for dc in range(DC):
                            @block.vector
                            def _(v, dc=dc):
                                W.w(v, "v", CV, CV.n)
                                CV.tick(v.tensor_copy(
                                    dbf[:].rearrange("p (a s) -> p a s", a=1) if False
                                    else dbf[:, 0:S],
                                    retained[:, dc * S:(dc + 1) * S]))
                            dct = CV.n

                            @block.sync
                            def _(sy, dc=dc):
                                W.w(sy, "sy", CV, dct)
                                DW.tick(sy.dma_start(out=dbg_ret[:, dc * S:(dc + 1) * S],
                                                     in_=dbf[:, 0:S]))
                                sy.wait_ge(DW.sem, DW.n)
                s01.close()

                a2_floor = floor_now()
                if sB_prev[0] is not None:
                    sB_prev[0].close()
                acc = sB.enter_context(
                    u_sb("acc", [128, NTB * D], F16, side="right"))
                with ExitStack() as s2:
                    wg_bf = s2.enter_context(u_sb("wg_bf", [128, DC * D], BF16))
                    wo_bf = s2.enter_context(u_sb("wo_bf", [128, DC * D], BF16))
                    Gc = s2.enter_context(u_sb("Gc", [128, DC * IC2], BF16))
                    ngb = s2.enter_context(u_sb("ngb", [128, DC * IC2], BF16))
                    t1b = s2.enter_context(u_sb("t1b", [128, IC2], BF16))
                    t2c = s2.enter_context(u_sb("t2c", [128, IC2], BF16))
                    rowf = s2.enter_context(u_sb("rowf", [1, 3 * IC2], F32))
                    rall = s2.enter_context(u_sb("rall", [1, NIC2 * 2 * IC2], BF16))
                    ps_st = s2.enter_context(u_ps("ps_st", [1, 512], F32))
                    ps_bc = s2.enter_context(u_ps("ps_bc", [128, 512], F32))
                    ps_g = [s2.enter_context(u_ps(f"ps_g{i}", [128, 512], F32))
                            for i in range(2)]
                    ps_o = [s2.enter_context(u_ps(f"ps_o{i}", [128, D], F32))
                            for i in range(2)]

                    s2s = s2.enter_context(ExitStack())
                    stg4 = s2s.enter_context(u_sb("stg4", [128, DC * 64], F32))
                    wcast = 0
                    for which, (wap2, wbf2) in enumerate(((wg_ap, wg_bf), (wo_ap, wo_bf))):
                        for q in range(16):
                            @block.sync
                            def _(sy, wap2=wap2, q=q):
                                wait_floor(sy, "sy", a2_floor)
                                W.w(sy, "sy", CV, wcast)
                                W.w(sy, "sy", DW, DW.n)
                                DW.tick(sy.dma_start(
                                    out=stg4[:].rearrange("p (dc n) -> p dc n", dc=DC),
                                    in_=wap2[:, q * 64:(q + 1) * 64]
                                    .rearrange("(dc p) n -> p dc n", p=128)))
                            dwh = DW.n

                            @block.vector
                            def _(v, which=which, wbf2=wbf2, q=q):
                                W.w(v, "v", DW, dwh)
                                W.w(v, "v", CV, CV.n)
                                if which == 0:
                                    gt, qc = divmod(q, 2)
                                    CV.tick(v.tensor_copy(
                                        wbf2[:, gt * D:(gt + 1) * D]
                                        .rearrange("p (dc c) -> p dc c", dc=DC)
                                        [:, :, qc * 64:(qc + 1) * 64],
                                        stg4[:].rearrange("p (dc n) -> p dc n", dc=DC)))
                                else:
                                    CV.tick(v.tensor_copy(
                                        wbf2[:].rearrange("p (dc n) -> p dc n", dc=DC)
                                        [:, :, q * 64:(q + 1) * 64],
                                        stg4[:].rearrange("p (dc n) -> p dc n", dc=DC)))
                            wcast = CV.n
                    wgo_ready = wcast

                    sq_read = 0
                    st_read = 0
                    sq_tagv = [0] * NIC2
                    stat_mmv = [0] * NIC2
                    for ic in range(NIC2):
                        @block.vector
                        def _(v, ic=ic):
                            W.w(v, "v", CV, a1_done_v)
                            W.w(v, "v", CT, stat_mmv[ic - 1] if ic >= 1 else 0)
                            for dc in range(DC):
                                W.w(v, "v", CV, CV.n)
                                CV.tick(v.tensor_mul(
                                    ngb[:, dc * IC2:(dc + 1) * IC2],
                                    retained[:, dc * S + ic * IC2: dc * S + (ic + 1) * IC2],
                                    retained[:, dc * S + ic * IC2: dc * S + (ic + 1) * IC2]))
                        sq_tagv[ic] = CV.n

                        @block.tensor
                        def _(t, ic=ic):
                            W.w(t, "t", CV, sq_tagv[ic])
                            W.w(t, "t", CV, st_read)
                            for dc in range(DC):
                                CT.tick(t.matmul(
                                    ps_st[0:1, 0:IC2], ones_c[:, 0:1],
                                    retained[:, dc * S + ic * IC2: dc * S + (ic + 1) * IC2],
                                    start=(dc == 0), stop=False))
                                CT.tick(t.matmul(
                                    ps_st[0:1, IC2:2 * IC2], ones_c[:, 0:1],
                                    ngb[:, dc * IC2:(dc + 1) * IC2],
                                    start=False, stop=(dc == DC - 1)))
                        stat_mmv[ic] = CT.n

                        @block.vector
                        def _(v, ic=ic):
                            W.w(v, "v", CT, stat_mmv[ic])
                            W.w(v, "v", CV, CV.n)
                            CV.tick(v.tensor_scalar(rowf[0:1, 0:IC2], ps_st[0:1, 0:IC2],
                                                    1.0 / D, None, ALU.mult))
                            CV.tick(v.tensor_scalar(rowf[0:1, IC2:2 * IC2],
                                                    ps_st[0:1, IC2:2 * IC2],
                                                    1.0 / D, None, ALU.mult))
                            W.w(v, "v", CV, CV.n)
                            CV.tick(v.tensor_mul(rowf[0:1, 2 * IC2:3 * IC2],
                                                 rowf[0:1, 0:IC2], rowf[0:1, 0:IC2]))
                            W.w(v, "v", CV, CV.n)
                            CV.tick(v.scalar_tensor_tensor(
                                rowf[0:1, 2 * IC2:3 * IC2], rowf[0:1, 2 * IC2:3 * IC2],
                                -1.0, rowf[0:1, IC2:2 * IC2], ALU.mult, ALU.add))
                            W.w(v, "v", CV, CV.n)
                            CV.tick(v.tensor_scalar(rowf[0:1, 2 * IC2:3 * IC2],
                                                    rowf[0:1, 2 * IC2:3 * IC2],
                                                    LN_EPS, None, ALU.add))
                        var_tag = CV.n
                        st_read = var_tag

                        @block.scalar
                        def _(s, ic=ic, var_tag=var_tag):
                            W.w(s, "s", CV, var_tag)
                            W.w(s, "s", CS, CS.n)
                            CS.tick(s.activation(rowf[0:1, 2 * IC2:3 * IC2],
                                                 rowf[0:1, 2 * IC2:3 * IC2], AF.Ln))
                            W.w(s, "s", CS, CS.n)
                            CS.tick(s.activation(rall[0:1, ic * 512: ic * 512 + IC2],
                                                 rowf[0:1, 2 * IC2:3 * IC2], AF.Exp,
                                                 scale=-0.5))
                        rstd_tag = CS.n

                        @block.vector
                        def _(v, ic=ic, rstd_tag=rstd_tag):
                            W.w(v, "v", CS, rstd_tag)
                            W.w(v, "v", CV, CV.n)
                            CV.tick(v.scalar_tensor_tensor(
                                rall[0:1, ic * 512 + IC2: ic * 512 + 2 * IC2],
                                rowf[0:1, 0:IC2], -1.0,
                                rall[0:1, ic * 512: ic * 512 + IC2],
                                ALU.mult, ALU.mult))
                    rall_done = CV.n

                    bc_read = 0
                    ng_used = 0
                    oev_tag = [0, 0]
                    gic_done = [0] * (NIC2 + 1)
                    gc_readv = [0] * (NIC2 + 1)

                    def emit_g(gi):
                        cs_sg0 = CS.reserve(DC)
                        gmm = [0] * DC
                        for gt in range(DC):
                            gpar = gt % 2

                            @block.tensor
                            def _(t, gi=gi, gt=gt, gpar=gpar):
                                W.w(t, "t", CV, wgo_ready)
                                if gt >= 2:
                                    W.w(t, "t", CS, cs_sg0 + gt - 1)
                                elif gi >= 1:
                                    W.w(t, "t", CS, gic_done[gi - 1])
                                for dc in range(DC):
                                    CT.tick(t.matmul(
                                        ps_g[gpar][:, 0:IC2],
                                        wg_bf[:, gt * D + dc * 128: gt * D + (dc + 1) * 128],
                                        xn_t[:, dc * T + tok0 + gi * IC2:
                                             dc * T + tok0 + (gi + 1) * IC2],
                                        start=(dc == 0), stop=(dc == DC - 1)))
                                gmm[gt] = CT.n

                        @block.scalar
                        def _(s, gi=gi):
                            W.w(s, "s", CV, gc_readv[gi - 1] if gi >= 1 else 0)
                            for gt in range(DC):
                                W.w(s, "s", CT, gmm[gt])
                                CS.tick_raw(s.activation(
                                    Gc[:, gt * IC2:(gt + 1) * IC2],
                                    ps_g[gt % 2][:, 0:IC2],
                                    AF.Sigmoid, bias=bg_t[:, gt:gt + 1]))
                        assert CS.n == cs_sg0 + DC
                        gic_done[gi] = cs_sg0 + DC

                    emit_g(0)
                    for ic in range(NIC2):
                        @block.tensor
                        def _(t, ic=ic):
                            W.w(t, "t", CV, rall_done)
                            W.w(t, "t", CV, bc_read)
                            CT.tick(t.matmul(ps_bc[:, 0:IC2], ones_r[0:1, :],
                                             rall[0:1, ic * 512: ic * 512 + IC2],
                                             start=True, stop=False))
                            CT.tick(t.matmul(ps_bc[:, IC2:2 * IC2], ones_r[0:1, :],
                                             rall[0:1, ic * 512 + IC2: ic * 512 + 2 * IC2],
                                             start=False, stop=True))
                        bc_mm = CT.n

                        @block.vector
                        def _(v, ic=ic, bc_mm=bc_mm):
                            W.w(v, "v", CT, bc_mm)
                            W.w(v, "v", CT, ng_used)
                            W.w(v, "v", CS, gic_done[ic])
                            for dc in range(DC):
                                W.w(v, "v", CV, CV.n)
                                CV.tick(v.scalar_tensor_tensor(
                                    t1b[:],
                                    retained[:, dc * S + ic * IC2: dc * S + (ic + 1) * IC2],
                                    gam_t[:, dc:dc + 1], ps_bc[:, 0:IC2],
                                    ALU.mult, ALU.mult))
                                W.w(v, "v", CV, CV.n)
                                CV.tick(v.scalar_tensor_tensor(
                                    t2c[:], ps_bc[:, IC2:2 * IC2], gam_t[:, dc:dc + 1],
                                    t1b[:], ALU.mult, ALU.add))
                                W.w(v, "v", CV, CV.n)
                                CV.tick(v.scalar_tensor_tensor(
                                    ngb[:, dc * IC2:(dc + 1) * IC2], t2c[:],
                                    bet_t[:, dc:dc + 1],
                                    Gc[:, dc * IC2:(dc + 1) * IC2],
                                    ALU.add, ALU.mult))
                        ng_tag = CV.n
                        bc_read = ng_tag
                        gc_readv[ic] = ng_tag

                        if ic + 1 < NIC2:
                            emit_g(ic + 1)

                        for tsub in range(IC2 // 128):
                            opar = tsub % 2
                            tt_loc = ic * (IC2 // 128) + tsub

                            @block.tensor
                            def _(t, ic=ic, tsub=tsub, opar=opar, ng_tag=ng_tag):
                                W.w(t, "t", CV, ng_tag)
                                W.w(t, "t", CV, wgo_ready)
                                W.w(t, "t", CS, oev_tag[opar])
                                for dc in range(DC):
                                    for n2 in range(2):
                                        CT.tick(t.matmul(
                                            ps_o[opar][:, n2 * 512:(n2 + 1) * 512],
                                            ngb[:, dc * IC2 + tsub * 128:
                                                dc * IC2 + (tsub + 1) * 128],
                                            wo_bf[:, dc * D + n2 * 512:
                                                  dc * D + (n2 + 1) * 512],
                                            start=(dc == 0), stop=False))
                                for n2 in range(2):
                                    CT.tick(t.matmul(
                                        ps_o[opar][:, n2 * 512:(n2 + 1) * 512],
                                        ones_r[0:1, :],
                                        bo_bf[0:1, n2 * 512:(n2 + 1) * 512],
                                        start=False, stop=True))
                            omm = CT.n
                            ng_used = max(ng_used, omm)

                            @block.scalar
                            def _(s, tt_loc=tt_loc, opar=opar, omm=omm):
                                W.w(s, "s", CT, omm)
                                W.w(s, "s", CS, CS.n)
                                CS.tick(s.activation(acc[:, tt_loc * D:(tt_loc + 1) * D],
                                                     ps_o[opar][:], AF.Copy))
                                oev_tag[opar] = CS.n
                    a2_done_s = CS.n
                    a2_done_v = CV.n

            f_floor = floor_now()
            stt_tag = [0] * NTB
            with ExitStack() as sF:
                w1c = [sF.enter_context(u_sb(f"w1c{i}", [128, DC * 512], BF16))
                       for i in range(2)]
                w2c = [sF.enter_context(u_sb(f"w2c{i}", [128, NJT * D], BF16))
                       for i in range(2)]
                stg1 = sF.enter_context(u_sb("stg1", [128, DC * 512], F32))
                stg2 = sF.enter_context(u_sb("stg2", [128, NJT * D], F32))
                hb = sF.enter_context(u_sb("hb", [128, NJT * S], BF16))
                b1t = sF.enter_context(u_sb("b1t", [128, NHT], F32))
                b1r = sF.enter_context(u_sb("b1r", [NHT, 128], F32))
                wT_bf = sF.enter_context(u_sb("wT_bf", [8, S], BF16))
                ps_h = [sF.enter_context(u_ps(f"ps_h{i}", [128, TH], F32))
                        for i in range(2)]
                ps_o2 = [sF.enter_context(u_ps(f"ps_o2{i}", [128, D], F32))
                         for i in range(2)]

                wT_last = 0
                for tt in range(NTB):
                    @block.tensor
                    def _(t, tt=tt, wT_last=wT_last):
                        W.w(t, "t", CV, p0_done_v)
                        W.w(t, "t", CS, a2_done_s)
                        W.w(t, "t", CV, a2_done_v)
                        W.w(t, "t", CV, wT_last)
                        CT.tick(t.transpose(ps_h[0][0:8, 0:128],
                                            wrt[:, (b * NTB + tt) * 8:(b * NTB + tt + 1) * 8],
                                            id_f[:]))
                    trt = CT.n

                    @block.vector
                    def _(v, tt=tt, trt=trt):
                        W.w(v, "v", CT, trt)
                        W.w(v, "v", CV, CV.n)
                        CV.tick(v.tensor_copy(wT_bf[0:8, tt * 128:(tt + 1) * 128],
                                              ps_h[0][0:8, 0:128]))
                    wT_last = CV.n
                wT_done = wT_last

                cast1 = {}
                cast2 = {}
                w1mm_last = {}
                w2mm_last = {}
                pso2_read = [0, 0]
                w2_pend = None
                hfree = [0, 0]
                hread = {}
                b1tr = [0] * E
                b1cp = [0] * E

                for e in range(E):
                    @block.scalar
                    def _(s, e=e):
                        wait_floor(s, "s", f_floor)
                        if e >= 1:
                            W.w(s, "s", CT, b1tr[e - 1])
                        W.w(s, "s", DX, DX.n)
                        DX.tick(s.dma_start(out=b1r[:],
                                            in_=b1_ap[e].rearrange("(a c) -> a c", c=128)))
                    dxh = DX.n

                    @block.tensor
                    def _(t, e=e, dxh=dxh):
                        W.w(t, "t", DX, dxh)
                        W.w(t, "t", CV, wT_done)
                        W.w(t, "t", CS, hfree[1])
                        if e >= 1:
                            W.w(t, "t", CV, b1cp[e - 1])
                        CT.tick(t.transpose(ps_h[1][:, 0:NHT], b1r[:], id_f[0:NHT, 0:NHT]))
                        b1tr[e] = CT.n

                    @block.vector
                    def _(v, e=e):
                        W.w(v, "v", CT, b1tr[e])
                        W.w(v, "v", CS, max(hfree))
                        W.w(v, "v", CV, CV.n)
                        CV.tick(v.tensor_copy(b1t[:], ps_h[1][:, 0:NHT]))
                        b1cp[e] = CV.n

                    for jc in range(NJC):
                        k = e * NJC + jc
                        par = k % 2

                        @block.sync
                        def _(sy, e=e, jc=jc, k=k):
                            wait_floor(sy, "sy", f_floor)
                            W.w(sy, "sy", CV, cast1.get(k - 1, 0))
                            W.w(sy, "sy", DW, DW.n)
                            DW.tick(sy.dma_start(
                                out=stg1[:].rearrange("p (dc j) -> p dc j", dc=DC),
                                in_=w1_ap[e][:, jc * JC:(jc + 1) * JC]
                                .rearrange("(dc p) j -> p dc j", p=128)))
                        dw1 = DW.n

                        @block.vector
                        def _(v, k=k, par=par, dw1=dw1):
                            W.w(v, "v", DW, dw1)
                            W.w(v, "v", CT, w1mm_last.get(k - 2, 0))
                            W.w(v, "v", CV, CV.n)
                            CV.tick(v.tensor_copy(w1c[par][:], stg1[:]))
                        cast1[k] = CV.n

                        @block.sync
                        def _(sy, e=e, jc=jc, k=k):
                            wait_floor(sy, "sy", f_floor)
                            W.w(sy, "sy", CV, cast2.get(k - 1, 0))
                            W.w(sy, "sy", DW, DW.n)
                            DW.tick(sy.dma_start(
                                out=stg2[:].rearrange("p (jt d) -> p jt d", jt=NJT),
                                in_=w2_ap[e][jc * JC:(jc + 1) * JC, :]
                                .rearrange("(jt p) d -> p jt d", p=128)))
                        dw2 = DW.n

                        @block.vector
                        def _(v, k=k, par=par, dw2=dw2):
                            W.w(v, "v", DW, dw2)
                            W.w(v, "v", CT, w2mm_last.get(k - 2, 0))
                            W.w(v, "v", CV, CV.n)
                            CV.tick(v.tensor_copy(w2c[par][:], stg2[:]))
                        cast2[k] = CV.n

                        NW1 = NJT * NTH
                        cs_g0 = CS.reserve(NW1)
                        w1_here = [0] * NW1

                        def emit_w1(idx, e=e, jc=jc, k=k, par=par, cs_g0=cs_g0):
                            jt, th = divmod(idx, NTH)
                            hpar = idx % 2
                            if True:

                                @block.tensor
                                def _(t, e=e, jc=jc, jt=jt, th=th, k=k, par=par,
                                      hpar=hpar, idx=idx):
                                    W.w(t, "t", CV, cast1[k])
                                    W.w(t, "t", CV, b1cp[e])
                                    if idx >= 2:
                                        W.w(t, "t", CS, cs_g0 + idx - 1)
                                    else:
                                        W.w(t, "t", CS, hfree[hpar])
                                    for dc in range(DC):
                                        for n2 in range(TH // 512):
                                            CT.tick(t.matmul(
                                                ps_h[hpar][:, n2 * 512:(n2 + 1) * 512],
                                                w1c[par][:, dc * 512 + jt * 128:
                                                     dc * 512 + (jt + 1) * 128],
                                                xn_t[:, dc * T + tok0 + th * TH + n2 * 512:
                                                     dc * T + tok0 + th * TH + (n2 + 1) * 512],
                                                start=(dc == 0), stop=(dc == DC - 1)))
                                    w1mm_last[k] = CT.n
                                    w1_here[idx] = CT.n

                        emit_w1(0)
                        emit_w1(1)
                        if w2_pend is not None:
                            emit_w2(*w2_pend)
                            w2_pend = None
                        for _i in range(2, NW1):
                            emit_w1(_i)

                        @block.scalar
                        def _(s, e=e, jc=jc, k=k):
                            for idx in range(NW1):
                                jt, th = divmod(idx, NTH)
                                ht = jc * NJT + jt
                                W.w(s, "s", CT, w1_here[idx])
                                if idx == 0:
                                    W.w(s, "s", CT, w2mm_last.get(k - 1, 0))
                                CS.tick_raw(s.activation(
                                    hb[:, jt * S + th * TH: jt * S + (th + 1) * TH],
                                    ps_h[idx % 2][:], AF.Gelu, bias=b1t[:, ht:ht + 1]))
                        assert CS.n == cs_g0 + NW1
                        hfree[0] = cs_g0 + NW1 - 1
                        hfree[1] = cs_g0 + NW1
                        if NW1 >= 2 and (NW1 - 2) % 2 == 1:
                            hfree[0], hfree[1] = cs_g0 + NW1, cs_g0 + NW1 - 1
                        gelu_all = cs_g0 + NW1

                        def emit_w2(e2, jc2, k2, par2, gelu_all2):
                            for tt in range(NTB):
                                opar = tt % 2

                                @block.tensor
                                def _(t, e=e2, jc=jc2, tt=tt, k=k2, par=par2, opar=opar,
                                      gelu_all=gelu_all2):
                                    W.w(t, "t", CS, gelu_all)
                                    W.w(t, "t", CV, cast2[k])
                                    W.w(t, "t", CV, pso2_read[opar])
                                    for jt in range(NJT):
                                        last = (jt == NJT - 1) and not (e == 0 and jc == 0)
                                        for n2 in range(2):
                                            CT.tick(t.matmul(
                                                ps_o2[opar][:, n2 * 512:(n2 + 1) * 512],
                                                hb[:, jt * S + tt * 128: jt * S + (tt + 1) * 128],
                                                w2c[par][:, jt * D + n2 * 512:
                                                     jt * D + (n2 + 1) * 512],
                                                start=(jt == 0), stop=last))
                                    if e == 0 and jc == 0:
                                        for n2 in range(2):
                                            CT.tick(t.matmul(
                                                ps_o2[opar][:, n2 * 512:(n2 + 1) * 512],
                                                wT_bf[0:8, tt * 128:(tt + 1) * 128],
                                                b2_bf[0:8, n2 * 512:(n2 + 1) * 512],
                                                start=False, stop=True))
                                    w2mm_last[k] = CT.n
                                    hread[k] = CT.n
                                w2_here = CT.n

                                @block.vector
                                def _(v, e=e2, tt=tt, opar=opar, w2_here=w2_here):
                                    W.w(v, "v", CT, w2_here)
                                    W.w(v, "v", CS, a2_done_s)
                                    W.w(v, "v", CV, stt_tag[tt])
                                    W.w(v, "v", CV, CV.n)
                                    CV.tick(v.scalar_tensor_tensor(
                                        acc[:, tt * D:(tt + 1) * D], ps_o2[opar][:],
                                        wrt[:, (b * NTB + tt) * 8 + e:
                                            (b * NTB + tt) * 8 + e + 1],
                                        acc[:, tt * D:(tt + 1) * D], ALU.mult, ALU.add))
                                    stt_tag[tt] = CV.n
                                    pso2_read[opar] = CV.n

                        w2_pend = (e, jc, k, par, gelu_all)

                if w2_pend is not None:
                    emit_w2(*w2_pend)
                    w2_pend = None

            pre_out_floor = floor_now()
            with ExitStack() as sO:
                add_tag = [0] * NTB
                odma_tag = [0] * NTB
                for tt in range(NTB):
                    @block.scalar
                    def _(s, tt=tt):
                        if tt >= 1:
                            W.w(s, "s", CV, add_tag[tt - 1])
                        if prev_add_last:
                            W.w(s, "s", CV, prev_add_last)
                        W.w(s, "s", DX, DX.n)
                        DX.tick(s.dma_start(out=xo[0][:],
                                            in_=x_ap[b, tt * 128:(tt + 1) * 128, :]))
                    dxh = DX.n

                    @block.vector
                    def _(v, tt=tt, dxh=dxh):
                        W.w(v, "v", DX, dxh)
                        W.w(v, "v", CV, stt_tag[tt])
                        if tt >= 1:
                            W.w(v, "v", DX, odma_tag[tt - 1])
                        elif prev_odma:
                            W.w(v, "v", DX, prev_odma)
                        W.w(v, "v", CV, CV.n)
                        CV.tick(v.tensor_add(oo[0][:], xo[0][:],
                                             acc[:, tt * D:(tt + 1) * D]))
                        add_tag[tt] = CV.n

                    @block.scalar
                    def _(s, tt=tt):
                        W.w(s, "s", CV, add_tag[tt])
                        W.w(s, "s", DX, DX.n)
                        DX.tick(s.dma_start(out=out_ap[b, tt * 128:(tt + 1) * 128, :],
                                            in_=oo[0][:]))
                        odma_tag[tt] = DX.n
                prev_add_last = add_tag[-1]
                prev_odma = odma_tag[-1]
            sB_prev[0] = sB

        if sB_prev[0] is not None:
            sB_prev[0].close()

        @block.scalar
        def _(s):
            s.wait_ge(DX.sem, DX.n)
            s.wait_ge(DG.sem, DG.n)

    return nc


def kernel(**inputs):
    P = dict(FULL)
    x = np.ascontiguousarray(np.asarray(inputs["L0_attn_x"], dtype=np.float32))
    nb = x.shape[0] // N_CORES
    base = {
        "wq": np.ascontiguousarray(np.asarray(inputs["L0_attn_Wq"], dtype=np.float32)),
        "bq": np.ascontiguousarray(np.asarray(inputs["L0_attn_bq"], dtype=np.float32)),
        "wk": np.ascontiguousarray(np.asarray(inputs["L0_attn_Wk"], dtype=np.float32)),
        "bk": np.ascontiguousarray(np.asarray(inputs["L0_attn_bk"], dtype=np.float32)),
        "wv": np.ascontiguousarray(np.asarray(inputs["L0_attn_Wv"], dtype=np.float32)),
        "bv": np.ascontiguousarray(np.asarray(inputs["L0_attn_bv"], dtype=np.float32)),
        "wg": np.ascontiguousarray(np.asarray(inputs["L0_attn_Wg"], dtype=np.float32)),
        "bg": np.ascontiguousarray(np.asarray(inputs["L0_attn_bg"], dtype=np.float32)),
        "wo": np.ascontiguousarray(np.asarray(inputs["L0_attn_Wo"], dtype=np.float32)),
        "bo": np.ascontiguousarray(np.asarray(inputs["L0_attn_bo"], dtype=np.float32)),
        "gam": np.ascontiguousarray(np.asarray(inputs["L0_attn_gamma"], dtype=np.float32)),
        "bet": np.ascontiguousarray(np.asarray(inputs["L0_attn_beta"], dtype=np.float32)),
        "w1": np.ascontiguousarray(np.asarray(inputs["L0_ffn_W1"], dtype=np.float32)),
        "b1": np.ascontiguousarray(np.asarray(inputs["L0_ffn_b1"], dtype=np.float32)),
        "w2": np.ascontiguousarray(np.asarray(inputs["L0_ffn_W2"], dtype=np.float32)),
        "b2": np.ascontiguousarray(np.asarray(inputs["L0_ffn_b2"], dtype=np.float32)),
        "wr": np.ascontiguousarray(np.asarray(inputs["L0_ffn_Wr"], dtype=np.float32)),
        "br": np.ascontiguousarray(np.asarray(inputs["L0_ffn_br"], dtype=np.float32)),
    }
    nc = build_program(P)
    in_maps = []
    for c in range(N_CORES):
        m = dict(base)
        m["x"] = np.ascontiguousarray(x[c * nb:(c + 1) * nb])
        in_maps.append(m)
    res = run_bass_kernel_spmd(nc, in_maps, core_ids=list(range(N_CORES)))
    return np.concatenate([r["out"] for r in res.results], axis=0)


# version 17
# speedup vs baseline: 1.0823x; 1.0823x over previous
import math
import os
import numpy as np
from contextlib import ExitStack

import concourse.bass as bass
import concourse.mybir as mybir
from concourse.bass_utils import run_bass_kernel_spmd

F32 = mybir.dt.float32
BF16 = mybir.dt.bfloat16
F16 = mybir.dt.float16
AF = mybir.ActivationFunctionType
ALU = mybir.AluOpType
AX = mybir.AxisListType

FULL = dict(B=2, S=2048, D=1024, H=16, E=8, HID=4096)
DECAY = 0.99
LN_EPS = 1e-5
N_CORES = 8


class Ctr:
    def __init__(self, es, nc, name, step=1):
        self.sem = es.enter_context(nc.semaphore(name))
        self.n = 0
        self.step = step

    def tick(self, instr):
        instr.then_inc(self.sem, self.step)
        self.n += self.step
        return self.n

    def reserve(self, k):
        base = self.n
        self.n += k * self.step
        return base

    def tick_raw(self, instr):
        instr.then_inc(self.sem, self.step)


class Waiter:

    def __init__(self):
        self.seen = {}

    def w(self, eng, key, ctr, tag):
        if tag <= 0:
            return
        k = (key, id(ctr))
        if tag <= self.seen.get(k, 0):
            return
        eng.wait_ge(ctr.sem, tag)
        self.seen[k] = tag


def build_program(P):
    B, S, D, H, E, HID = P["B"], P["S"], P["D"], P["H"], P["E"], P["HID"]
    T = B * S
    DC = D // 128
    NT = T // 128
    NTB = S // 128
    HP = H // 2
    JC = 512
    NJT = JC // 128
    NJC = HID // JC
    IC = 512
    NIC = S // IC
    IC2 = 256
    NIC2 = S // IC2
    TH = min(1024, S)
    NTH = S // TH
    NHT = HID // 128
    assert D == 1024 and DC == 8 and HP == 8 and S % IC == 0 and S % IC2 == 0

    nc = bass.Bass("TRN2", target_bir_lowering=False)

    x_ap = nc.dram_tensor("x", [B, S, D], F32, kind="ExternalInput").ap()
    wq_ap = nc.dram_tensor("wq", [D, D], F32, kind="ExternalInput").ap()
    bq_ap = nc.dram_tensor("bq", [D], F32, kind="ExternalInput").ap()
    wk_ap = nc.dram_tensor("wk", [D, D], F32, kind="ExternalInput").ap()
    bk_ap = nc.dram_tensor("bk", [D], F32, kind="ExternalInput").ap()
    wv_ap = nc.dram_tensor("wv", [D, D], F32, kind="ExternalInput").ap()
    bv_ap = nc.dram_tensor("bv", [D], F32, kind="ExternalInput").ap()
    wg_ap = nc.dram_tensor("wg", [D, D], F32, kind="ExternalInput").ap()
    bg_ap = nc.dram_tensor("bg", [D], F32, kind="ExternalInput").ap()
    wo_ap = nc.dram_tensor("wo", [D, D], F32, kind="ExternalInput").ap()
    bo_ap = nc.dram_tensor("bo", [D], F32, kind="ExternalInput").ap()
    gam_ap = nc.dram_tensor("gam", [D], F32, kind="ExternalInput").ap()
    bet_ap = nc.dram_tensor("bet", [D], F32, kind="ExternalInput").ap()
    w1_ap = nc.dram_tensor("w1", [E, D, HID], F32, kind="ExternalInput").ap()
    b1_ap = nc.dram_tensor("b1", [E, HID], F32, kind="ExternalInput").ap()
    w2_ap = nc.dram_tensor("w2", [E, HID, D], F32, kind="ExternalInput").ap()
    b2_ap = nc.dram_tensor("b2", [E, D], F32, kind="ExternalInput").ap()
    wr_ap = nc.dram_tensor("wr", [D, E], F32, kind="ExternalInput").ap()
    br_ap = nc.dram_tensor("br", [E], F32, kind="ExternalInput").ap()
    out_ap = nc.dram_tensor("out", [B, S, D], F32, kind="ExternalOutput").ap()
    DBG = bool(os.environ.get("KDBG"))
    if DBG:
        dbg_ret = nc.dram_tensor("dbg_ret", [128, DC * S], F32, kind="ExternalOutput").ap()
        dbg_qp = nc.dram_tensor("dbg_qp", [128, S], F32, kind="ExternalOutput").ap()
        dbg_kp = nc.dram_tensor("dbg_kp", [128, S], F32, kind="ExternalOutput").ap()
        dbg_cq = nc.dram_tensor("dbg_cq", [128, S], F32, kind="ExternalOutput").ap()
        dbg_ck = nc.dram_tensor("dbg_ck", [128, S], F32, kind="ExternalOutput").ap()
        dbg_xn = nc.dram_tensor("dbg_xn", [128, DC * T], F32, kind="ExternalOutput").ap()

    es = ExitStack()
    _uq = [0]

    def _un(n):
        _uq[0] += 1
        return f"{n}_{_uq[0]}"

    def u_sb(name, shape, dtype, **kw):
        return nc.sbuf_tensor(_un(name), shape, dtype, **kw)

    def u_ps(name, shape, dtype=F32, **kw):
        return nc.psum_tensor(_un(name), shape, dtype, **kw)

    def sb(name, shape, dtype, side=None):
        kw = {} if side is None else dict(side=side)
        return es.enter_context(u_sb(name, shape, dtype, **kw))

    xn_t = sb("xn_t", [128, DC * T], BF16)
    wrt = sb("wrt", [128, NT * 8], F32)
    cqrow = sb("cqrow", [1, S], BF16, side="right")
    ckcols = sb("ckcols", [128, S // 128], F32)
    iota_c = sb("iota_c", [128, S // 128], F32)
    bq_t = sb("bq_t", [128, DC], F32)
    bk_t = sb("bk_t", [128, DC], F32)
    bg_t = sb("bg_t", [128, DC], F32)
    gam_t = sb("gam_t", [128, DC], F32)
    bet_t = sb("bet_t", [128, DC], F32)
    bv_bf = sb("bv_bf", [1, D], BF16)
    bo_bf = sb("bo_bf", [1, D], BF16)
    br_bf = sb("br_bf", [1, E], BF16)
    b2_bf = sb("b2_bf", [E, D], BF16)
    id_bf = sb("id_bf", [128, 128], BF16)
    xo = [sb("xo0", [128, D], F32)]
    oo = [sb("oo0", [128, D], F32)]
    id_f = sb("id_f", [128, 128], F32)
    tri_bf = sb("tri_bf", [128, 128], BF16)
    bmask = sb("bmask", [128, 4 * 512], BF16)
    ones_r = sb("ones_r", [1, 128], BF16)
    ones_c = sb("ones_c", [128, 1], BF16)

    block = es.enter_context(nc.Block())
    CT = Ctr(es, nc, "ct")
    CV = Ctr(es, nc, "cv")
    CS = Ctr(es, nc, "cs")
    CG = Ctr(es, nc, "cg")
    DW = Ctr(es, nc, "dw", 16)
    DX = Ctr(es, nc, "dx", 16)
    DG = Ctr(es, nc, "dg", 16)
    W = Waiter()

    LN99 = math.log(DECAY)
    LN8I = math.log(1.0 / math.sqrt(D // H))

    def floor_now():
        return (CT.n, CV.n, CS.n, DW.n, DX.n, CG.n, DG.n)

    def wait_floor(eng, key, f):
        W.w(eng, key, CT, f[0])
        W.w(eng, key, CV, f[1])
        W.w(eng, key, CS, f[2])
        W.w(eng, key, DW, f[3])
        W.w(eng, key, DX, f[4])
        W.w(eng, key, CG, f[5])
        W.w(eng, key, DG, f[6])

    with es:
        p0b = ExitStack()
        def sb0(name, shape, dtype):
            return p0b.enter_context(u_sb(name, shape, dtype))
        iota_f = sb0("iota_f", [128, 128], F32)
        iota_row = sb0("iota_row", [1, S], F32)
        rows_b = [sb0(f"rows_{i}", [8, 128], F32) for i in range(5)]
        bv_row = sb0("bv_row", [1, D], F32)
        bo_row = sb0("bo_row", [1, D], F32)
        br_row = sb0("br_row", [1, E], F32)
        b2_row = sb0("b2_row", [E, D], F32)
        wr_st = sb0("wr_st", [128, DC * E], F32)
        wr_bf = sb0("wr_bf", [128, DC * E], BF16)
        xin = [sb0(f"xin{i}", [128, 4 * D], F32) for i in range(2)]
        xnk = [sb0(f"xnk{i}", [128, 4 * D], BF16) for i in range(2)]
        sqb = sb0("sqb", [128, 4 * D], BF16)
        st1 = sb0("st1", [128, 24], F32)
        ltb = sb0("ltb", [128, 8], F32)
        eqb = sb0("eqb", [128, 8], F32)
        t2b = sb0("t2b", [128, 8], F32)
        kpb = sb0("kpb", [128, 8], F32)
        exb = sb0("exb", [128, 8], F32)
        ekb = sb0("ekb", [128, 8], F32)
        c_ln8i = sb0("c_ln8i", [1, 1], F32)

        @block.sync
        def _(sy):
            W.w(sy, "sy", DW, DW.n)
            DW.tick(sy.dma_start(out=wr_st[:].rearrange("p (dc e) -> p dc e", dc=DC),
                                 in_=wr_ap.rearrange("(dc p) e -> p dc e", p=128)))
            W.w(sy, "sy", DW, DW.n)
            DW.tick(sy.dma_start(out=rows_b[0][:], in_=bq_ap.rearrange("(a c) -> a c", c=128)))
            W.w(sy, "sy", DW, DW.n)
            DW.tick(sy.dma_start(out=rows_b[1][:], in_=bk_ap.rearrange("(a c) -> a c", c=128)))
            W.w(sy, "sy", DW, DW.n)
            DW.tick(sy.dma_start(out=rows_b[2][:], in_=bg_ap.rearrange("(a c) -> a c", c=128)))
            W.w(sy, "sy", DW, DW.n)
            DW.tick(sy.dma_start(out=rows_b[3][:], in_=gam_ap.rearrange("(a c) -> a c", c=128)))
            W.w(sy, "sy", DW, DW.n)
            DW.tick(sy.dma_start(out=rows_b[4][:], in_=bet_ap.rearrange("(a c) -> a c", c=128)))
            W.w(sy, "sy", DW, DW.n)
            DW.tick(sy.dma_start(out=bv_row[:], in_=bv_ap.rearrange("(o d) -> o d", o=1)))
            W.w(sy, "sy", DW, DW.n)
            DW.tick(sy.dma_start(out=bo_row[:], in_=bo_ap.rearrange("(o d) -> o d", o=1)))
            W.w(sy, "sy", DW, DW.n)
            DW.tick(sy.dma_start(out=br_row[:], in_=br_ap.rearrange("(o d) -> o d", o=1)))
            W.w(sy, "sy", DW, DW.n)
            DW.tick(sy.dma_start(out=b2_row[:], in_=b2_ap))
        dma_const = DW.n

        @block.gpsimd
        def _(g):
            CG.tick(g.iota(iota_f[:], [[1, 128]], channel_multiplier=-1,
                           allow_small_or_imprecise_dtypes=True))
            CG.tick(g.iota(iota_row[:], [[1, S]], channel_multiplier=0,
                           allow_small_or_imprecise_dtypes=True))
            CG.tick(g.iota(iota_c[:], [[128, S // 128]], channel_multiplier=1,
                           allow_small_or_imprecise_dtypes=True))
        t_iota = CG.n

        @block.vector
        def _(v):
            W.w(v, "v", CG, t_iota)
            W.w(v, "v", CV, CV.n)
            CV.tick(v.tensor_scalar(id_bf[:], iota_f[:], 0.0, None, ALU.is_equal))
            W.w(v, "v", CV, CV.n)
            CV.tick(v.tensor_scalar(id_f[:], iota_f[:], 0.0, None, ALU.is_equal))
            W.w(v, "v", CV, CV.n)
            CV.tick(v.tensor_scalar(tri_bf[:], iota_f[:], 0.0, None, ALU.is_ge))
            for bp in range(4):
                for isub in range(4):
                    dst = bmask[:, bp * 512 + isub * 128: bp * 512 + (isub + 1) * 128]
                    W.w(v, "v", CV, CV.n)
                    if isub < bp:
                        CV.tick(v.memset(dst, 0.0))
                    elif isub == bp:
                        CV.tick(v.tensor_copy(dst, tri_bf[:]))
                    else:
                        CV.tick(v.memset(dst, 1.0))
            W.w(v, "v", CV, CV.n)
            CV.tick(v.memset(ones_r[:], 1.0))
            W.w(v, "v", CV, CV.n)
            CV.tick(v.memset(ones_c[:], 1.0))
            W.w(v, "v", DW, dma_const)
            W.w(v, "v", CV, CV.n)
            CV.tick(v.tensor_copy(wr_bf[:], wr_st[:]))
            W.w(v, "v", CV, CV.n)
            CV.tick(v.tensor_copy(bv_bf[:], bv_row[:]))
            W.w(v, "v", CV, CV.n)
            CV.tick(v.tensor_copy(bo_bf[:], bo_row[:]))
            W.w(v, "v", CV, CV.n)
            CV.tick(v.tensor_copy(br_bf[:], br_row[:]))
            W.w(v, "v", CV, CV.n)
            CV.tick(v.tensor_copy(b2_bf[:], b2_row[:]))
        @block.vector
        def _(v):
            W.w(v, "v", CV, CV.n)
            CV.tick(v.memset(c_ln8i[:], LN8I))
        t_c8 = CV.n
        t_consts_v = CV.n

        @block.scalar
        def _(s):
            W.w(s, "s", CG, t_iota)
            W.w(s, "s", CV, t_c8)
            W.w(s, "s", CS, CS.n)
            CS.tick(s.activation(cqrow[:], iota_row[:], AF.Exp,
                                 bias=c_ln8i[0:1, 0:1], scale=LN99))
            W.w(s, "s", CS, CS.n)
            CS.tick(s.activation(ckcols[:], iota_c[:], AF.Exp,
                                 bias=0.0, scale=-LN99))
        t_decay_s = CS.n

        with ExitStack() as p0s:
            ps_b = p0s.enter_context(u_ps("ps_b", [128, 512], F32))
            ps_t = p0s.enter_context(u_ps("ps_t", [128, 2048], BF16))
            ps_r = p0s.enter_context(u_ps("ps_r", [128, 512], F32))

            bias_t = [bq_t, bk_t, bg_t, gam_t, bet_t]

            @block.tensor
            def _(t):
                W.w(t, "t", CV, t_consts_v)
                for i in range(5):
                    CT.tick(t.transpose(ps_b[:, i * 8:(i + 1) * 8], rows_b[i][:],
                                        id_f[0:8, 0:8]))
            t_biasT = CT.n

            @block.vector
            def _(v):
                W.w(v, "v", CT, t_biasT)
                for i in range(5):
                    W.w(v, "v", CV, CV.n)
                    CV.tick(v.tensor_copy(bias_t[i][:], ps_b[:, i * 8:(i + 1) * 8]))
            t_biasT_v = CV.n

            NST = NT // 4
            apply_tag = [0] * NST
            stats_tag = [0] * NST
            trans_tag = [0] * NST
            xev_tag = [0] * NST
            rr_tag = [0] * NT
            w_tag = [0] * NT

            for st in range(NST):
                sl = st % 2
                b_i, r_i = divmod(st * 4, NTB)

                @block.scalar
                def _(s, st=st, sl=sl, b_i=b_i, r_i=r_i):
                    if st >= 2:
                        W.w(s, "s", CV, apply_tag[st - 2])
                        W.w(s, "s", CV, stats_tag[st - 2])
                    W.w(s, "s", DX, DX.n)
                    DX.tick(s.dma_start(
                        out=xin[sl][:].rearrange("p (a d) -> p a d", a=4),
                        in_=x_ap[b_i, r_i * 128:(r_i + 4) * 128, :]
                        .rearrange("(a p) d -> p a d", p=128)))
                dx_here = DX.n

                @block.vector
                def _(v, st=st, sl=sl):
                    W.w(v, "v", DX, dx_here)
                    xv = xin[sl][:].rearrange("p (a d) -> p a d", a=4)
                    W.w(v, "v", CV, CV.n)
                    CV.tick(v.tensor_mul(sqb[:], xin[sl][:], xin[sl][:]))
                    CV.tick(v.reduce_sum(st1[:, 0:4].rearrange("p (a o) -> p a o", o=1),
                                         xv, axis=AX.X))
                    W.w(v, "v", CV, CV.n)
                    CV.tick(v.reduce_sum(st1[:, 4:8].rearrange("p (a o) -> p a o", o=1),
                                         sqb[:].rearrange("p (a d) -> p a d", a=4),
                                         axis=AX.X))
                    W.w(v, "v", CV, CV.n)
                    CV.tick(v.tensor_scalar(st1[:, 8:12], st1[:, 0:4], 1.0 / D, None,
                                            ALU.mult))
                    CV.tick(v.tensor_scalar(st1[:, 12:16], st1[:, 4:8], 1.0 / D, None,
                                            ALU.mult))
                    W.w(v, "v", CV, CV.n)
                    CV.tick(v.tensor_mul(st1[:, 16:20], st1[:, 8:12], st1[:, 8:12]))
                    W.w(v, "v", CV, CV.n)
                    CV.tick(v.scalar_tensor_tensor(st1[:, 16:20], st1[:, 16:20], -1.0,
                                                   st1[:, 12:16], ALU.mult, ALU.add))
                    W.w(v, "v", CV, CV.n)
                    CV.tick(v.tensor_scalar(st1[:, 16:20], st1[:, 16:20], LN_EPS, None,
                                            ALU.add))
                vtag1 = CV.n

                @block.scalar
                def _(s, st=st, sl=sl):
                    W.w(s, "s", CS, CS.n)
                    W.w(s, "s", CV, vtag1)
                    CS.tick(s.activation(st1[:, 16:20], st1[:, 16:20], AF.Ln))
                    W.w(s, "s", CS, CS.n)
                    CS.tick(s.activation(st1[:, 20:24], st1[:, 16:20], AF.Exp,
                                         scale=-0.5))
                rstd_tag = CS.n

                @block.vector
                def _(v, st=st):
                    W.w(v, "v", CS, rstd_tag)
                    W.w(v, "v", CV, CV.n)
                    CV.tick(v.scalar_tensor_tensor(st1[:, 8:12], st1[:, 8:12], -1.0,
                                                   st1[:, 20:24], ALU.mult, ALU.mult))
                    stats_tag[st] = CV.n
                negm_tag = CV.n

                @block.scalar
                def _(s, st=st, sl=sl):
                    W.w(s, "s", CV, negm_tag)
                    if st >= 2:
                        W.w(s, "s", CT, trans_tag[st - 2])
                    for a in range(4):
                        W.w(s, "s", CS, CS.n)
                        CS.tick(s.activation(xnk[sl][:, a * D:(a + 1) * D],
                                             xin[sl][:, a * D:(a + 1) * D],
                                             AF.Identity, bias=st1[:, 8 + a:9 + a],
                                             scale=st1[:, 20 + a:21 + a]))
                    apply_tag[st] = CS.n

                sub_ev = [0] * 4
                for a in range(4):
                    @block.tensor
                    def _(t, st=st, sl=sl, a=a):
                        W.w(t, "t", CS, apply_tag[st])
                        if a >= 2:
                            W.w(t, "t", CV, sub_ev[a - 2])
                        elif st >= 1:
                            W.w(t, "t", CV, xev_tag[st - 1])
                        for dc in range(DC):
                            CT.tick(t.transpose(
                                ps_t[:, (a % 2) * 1024 + dc * 128:
                                     (a % 2) * 1024 + (dc + 1) * 128],
                                xnk[sl][:, a * D + dc * 128: a * D + (dc + 1) * 128],
                                id_bf[:]))
                    tr_here = CT.n

                    @block.vector
                    def _(v, st=st, a=a, tr_here=tr_here):
                        W.w(v, "v", CT, tr_here)
                        W.w(v, "v", CV, CV.n)
                        i = st * 4 + a
                        CV.tick(v.tensor_copy(
                            xn_t[:].rearrange("p (dc t) -> p dc t", dc=DC)
                            [:, :, i * 128:(i + 1) * 128],
                            ps_t[:, (a % 2) * 1024:(a % 2) * 1024 + 1024]
                            .rearrange("p (dc t) -> p dc t", dc=DC)))
                    sub_ev[a] = CV.n
                trans_tag[st] = CT.n
                xev_tag[st] = CV.n

                for a in range(4):
                    i = st * 4 + a

                    @block.tensor
                    def _(t, i=i, a=a):
                        W.w(t, "t", CV, sub_ev[a])
                        if i >= 1:
                            W.w(t, "t", CV, rr_tag[i - 1])
                        for dc in range(DC):
                            CT.tick(t.matmul(ps_r[:, 0:8],
                                             xn_t[:, dc * T + i * 128: dc * T + (i + 1) * 128],
                                             wr_bf[:, dc * 8:(dc + 1) * 8],
                                             start=(dc == 0), stop=False))
                        CT.tick(t.matmul(ps_r[:, 0:8], ones_r[0:1, :], br_bf[0:1, :],
                                         start=False, stop=True))
                    rmm_tag = CT.n

                    @block.vector
                    def _(v, i=i, rmm_tag=rmm_tag):
                        W.w(v, "v", CT, rmm_tag)
                        W.w(v, "v", CV, CV.n)
                        CV.tick(v.tensor_copy(ltb[:], ps_r[:, 0:8]))
                        rr_tag[i] = CV.n
                        W.w(v, "v", CV, CV.n)
                        CV.tick(v.reduce_max(eqb[:, 4:5], ltb[:], axis=AX.X))
                        W.w(v, "v", CV, CV.n)
                        CV.tick(v.tensor_scalar(eqb[:, 5:6], eqb[:, 4:5], -1.0, None, ALU.mult))
                        CV.tick(v.tensor_scalar(eqb[:, 6:7] if False else kpb[:], ltb[:],
                                                eqb[:, 4:5], None, ALU.is_equal))
                        W.w(v, "v", CV, CV.n)
                        CV.tick(v.scalar_tensor_tensor(t2b[:], kpb[:], -3.0e38, ltb[:],
                                                       ALU.mult, ALU.add))
                        W.w(v, "v", CV, CV.n)
                        CV.tick(v.reduce_max(eqb[:, 0:1], t2b[:], axis=AX.X))
                        W.w(v, "v", CV, CV.n)
                        CV.tick(v.tensor_scalar(kpb[:], ltb[:], eqb[:, 0:1], None, ALU.is_ge))
                    ex_tag = CV.n

                    @block.scalar
                    def _(s, i=i, ex_tag=ex_tag):
                        W.w(s, "s", CS, CS.n)
                        W.w(s, "s", CV, ex_tag)
                        CS.tick(s.activation(exb[:], ltb[:], AF.Exp, bias=eqb[:, 5:6]))
                    exs_tag = CS.n

                    @block.vector
                    def _(v, i=i, exs_tag=exs_tag):
                        W.w(v, "v", CS, exs_tag)
                        W.w(v, "v", CV, CV.n)
                        CV.tick(v.tensor_mul(ekb[:], exb[:], kpb[:]))
                        W.w(v, "v", CV, CV.n)
                        CV.tick(v.reduce_sum(eqb[:, 4:5], ekb[:], axis=AX.X))
                        W.w(v, "v", CV, CV.n)
                        CV.tick(v.reciprocal(eqb[:, 5:6], eqb[:, 4:5]))
                        W.w(v, "v", CV, CV.n)
                        CV.tick(v.tensor_scalar(wrt[:, i * 8:(i + 1) * 8], ekb[:],
                                                eqb[:, 5:6], None, ALU.mult))
                        w_tag[i] = CV.n

            p0_done_v = CV.n
        p0b.close()
        p0_floor = (CT.n, CV.n, CS.n, DW.n, DX.n)

        a2_done_s = 0
        a2_done_v = 0
        stt_tag = [0] * NTB
        prev_add_last = 0
        prev_odma = 0

        pre_out_floor = None
        sB_prev = [None]
        for b in range(B):
            tok0 = b * S
            sB = ExitStack()

            with ExitStack() as sA:
                retained = sA.enter_context(
                    u_sb("ret", [128, DC * S], BF16))
                s01 = sA.enter_context(ExitStack())
                Vt = s01.enter_context(u_sb("Vt", [128, NTB * D], BF16))

                ph_floor = pre_out_floor if pre_out_floor else floor_now()
                cs_floor = ph_floor[2]
                with ExitStack() as s0:
                    wv_bf = s0.enter_context(u_sb("wv_bf", [128, DC * D], BF16))
                    stgv = s0.enter_context(u_sb("stgv", [128, DC * 128], F32))
                    ps_v = [s0.enter_context(u_ps(f"ps_v{i}", [128, 1024], F32))
                            for i in range(2)]

                    vcast = []
                    for hf in range(8):
                        @block.sync
                        def _(sy, hf=hf):
                            wait_floor(sy, "sy", ph_floor)
                            if hf >= 1:
                                W.w(sy, "sy", CV, vcast[hf - 1])
                            W.w(sy, "sy", DW, DW.n)
                            DW.tick(sy.dma_start(
                                out=stgv[:].rearrange("p (dc n) -> p dc n", dc=DC),
                                in_=wv_ap[:, hf * 128:(hf + 1) * 128]
                                .rearrange("(dc p) n -> p dc n", p=128)))
                        dwh = DW.n

                        @block.vector
                        def _(v, hf=hf):
                            W.w(v, "v", DW, dwh)
                            W.w(v, "v", CV, CV.n)
                            CV.tick(v.tensor_copy(
                                wv_bf[:].rearrange("p (dc n) -> p dc n", dc=DC)
                                [:, :, hf * 128:(hf + 1) * 128],
                                stgv[:].rearrange("p (dc n) -> p dc n", dc=DC)))
                        vcast.append(CV.n)
                    wv_ready = vcast[-1]

                    cs_v0 = CS.reserve(NTB)
                    vmm = [0] * NTB
                    for tt in range(NTB):
                        par = tt % 2

                        @block.tensor
                        def _(t, tt=tt, par=par):
                            W.w(t, "t", CV, wv_ready)
                            W.w(t, "t", CS, cs_floor)
                            if tt >= 2:
                                W.w(t, "t", CS, cs_v0 + tt - 1)
                            for dc in range(DC):
                                for n2 in range(2):
                                    CT.tick(t.matmul(
                                        ps_v[par][:, n2 * 512:(n2 + 1) * 512],
                                        xn_t[:, dc * T + tok0 + tt * 128:
                                             dc * T + tok0 + (tt + 1) * 128],
                                        wv_bf[:, dc * D + n2 * 512: dc * D + (n2 + 1) * 512],
                                        start=(dc == 0), stop=False))
                            for n2 in range(2):
                                CT.tick(t.matmul(ps_v[par][:, n2 * 512:(n2 + 1) * 512],
                                                 ones_r[0:1, :],
                                                 bv_bf[0:1, n2 * 512:(n2 + 1) * 512],
                                                 start=False, stop=True))
                            vmm[tt] = CT.n

                    @block.scalar
                    def _(s):
                        for tt in range(NTB):
                            W.w(s, "s", CT, vmm[tt])
                            CS.tick_raw(s.activation(Vt[:, tt * D:(tt + 1) * D],
                                                     ps_v[tt % 2][:], AF.Copy))
                    assert CS.n == cs_v0 + NTB

                a1_floor = floor_now()
                with ExitStack() as s1:
                    qp = s1.enter_context(u_sb("qp", [128, S], BF16))
                    kp = s1.enter_context(u_sb("kp", [128, S], BF16))
                    cq = s1.enter_context(u_sb("cq", [128, S], BF16))
                    wqk_st = s1.enter_context(u_sb("wqk_st", [128, DC * 32], F32))
                    wq_bf = s1.enter_context(u_sb("wq_bf", [128, DC * 128], BF16))
                    wk_bf = s1.enter_context(u_sb("wk_bf", [128, DC * 128], BF16))
                    st_buf = s1.enter_context(u_sb("st_buf", [128, 3 * IC], BF16))
                    ps_qk = s1.enter_context(u_ps("ps_qk", [128, S], F32))
                    ps_s = [s1.enter_context(u_ps(f"ps_s{i}", [128, IC], F32))
                            for i in range(3)]
                    ps_ret = [s1.enter_context(u_ps(f"ps_r{i}", [64, IC], F32))
                              for i in range(1)]

                    dec_cp = [0, 0]
                    for which2, (dst2, rw) in enumerate(((cq, cqrow),)):
                        for c in range(S // 512):
                            spar = c % 2

                            @block.tensor
                            def _(t, which2=which2, dst2=dst2, rw=rw, c=c, spar=spar):
                                W.w(t, "t", CS, t_decay_s)
                                W.w(t, "t", CS, a1_floor[2])
                                W.w(t, "t", CV, dec_cp[spar])
                                W.w(t, "t", CV, a1_floor[1])
                                CT.tick(t.matmul(ps_s[spar][:],
                                                 ones_r[0:1, :],
                                                 rw[0:1, c * 512:(c + 1) * 512],
                                                 start=True, stop=True))
                            mm_here = CT.n

                            @block.vector
                            def _(v, dst2=dst2, c=c, spar=spar, mm_here=mm_here):
                                W.w(v, "v", CT, mm_here)
                                W.w(v, "v", CV, CV.n)
                                CV.tick(v.tensor_copy(dst2[:, c * 512:(c + 1) * 512],
                                                      ps_s[spar][:]))
                            dec_cp[spar] = CV.n
                    sev_init = list(dec_cp)

                    qcast = [0] * HP
                    kcast = [0] * HP
                    qmm = [0] * HP
                    kmm = [0] * HP
                    qev = [0] * HP
                    kev = [0] * HP
                    sev_last = sev_init + [0]
                    slot_av = [0, 0, 0]
                    ret_ev = [0, 0]
                    u_idx = 0

                    for hp in range(HP):
                        for which in range(2):
                            wap = wq_ap if which == 0 else wk_ap
                            wbf = wq_bf if which == 0 else wk_bf

                            for mh in range(4):
                                @block.sync
                                def _(sy, hp=hp, which=which, wap=wap, mh=mh):
                                    wait_floor(sy, "sy", a1_floor)
                                    prev = 0
                                    if mh >= 1:
                                        prev = half_cast
                                    elif which == 1:
                                        prev = qcast[hp]
                                    elif hp > 0:
                                        prev = kcast[hp - 1]
                                    W.w(sy, "sy", CV, prev)
                                    W.w(sy, "sy", DW, DW.n)
                                    DW.tick(sy.dma_start(
                                        out=wqk_st[:].rearrange("p (dc m) -> p dc m", dc=DC // 4),
                                        in_=wap[mh * 256:(mh + 1) * 256, hp * 128:(hp + 1) * 128]
                                        .rearrange("(dc p) m -> p dc m", p=128)))
                                dwh = DW.n

                                @block.vector
                                def _(v, hp=hp, which=which, wbf=wbf, mh=mh):
                                    W.w(v, "v", DW, dwh)
                                    if hp > 0:
                                        W.w(v, "v", CT, (qmm if which == 0 else kmm)[hp - 1])
                                    W.w(v, "v", CV, CV.n)
                                    CV.tick(v.tensor_copy(
                                        wbf[:].rearrange("p (dc m) -> p dc m", dc=DC)
                                        [:, mh * (DC // 4):(mh + 1) * (DC // 4), :],
                                        wqk_st[:].rearrange("p (dc m) -> p dc m", dc=DC // 4)))
                                half_cast = CV.n
                            if which == 0:
                                qcast[hp] = CV.n
                            else:
                                kcast[hp] = CV.n

                        @block.tensor
                        def _(t, hp=hp):
                            W.w(t, "t", CV, qcast[hp])
                            if hp > 0:
                                W.w(t, "t", CV, kev[hp - 1])
                            for dc in range(DC):
                                for ncs in range(S // 512):
                                    CT.tick(t.matmul(
                                        ps_qk[:, ncs * 512:(ncs + 1) * 512],
                                        wq_bf[:, dc * 128:(dc + 1) * 128],
                                        xn_t[:, dc * T + tok0 + ncs * 512:
                                             dc * T + tok0 + (ncs + 1) * 512],
                                        start=(dc == 0), stop=(dc == DC - 1)))
                            qmm[hp] = CT.n

                        @block.vector
                        def _(v, hp=hp):
                            W.w(v, "v", CT, qmm[hp])
                            W.w(v, "v", CV, CV.n)
                            CV.tick(v.scalar_tensor_tensor(qp[:], ps_qk[:],
                                                           bq_t[:, hp:hp + 1], cq[:],
                                                           ALU.add, ALU.mult))
                            qev[hp] = CV.n

                        @block.tensor
                        def _(t, hp=hp):
                            W.w(t, "t", CV, kcast[hp])
                            W.w(t, "t", CV, qev[hp])
                            for dc in range(DC):
                                for ncs in range(S // 512):
                                    CT.tick(t.matmul(
                                        ps_qk[:, ncs * 512:(ncs + 1) * 512],
                                        wk_bf[:, dc * 128:(dc + 1) * 128],
                                        xn_t[:, dc * T + tok0 + ncs * 512:
                                             dc * T + tok0 + (ncs + 1) * 512],
                                        start=(dc == 0), stop=(dc == DC - 1)))
                            kmm[hp] = CT.n

                        @block.vector
                        def _(v, hp=hp):
                            W.w(v, "v", CT, kmm[hp])
                            W.w(v, "v", CV, CV.n)
                            CV.tick(v.tensor_scalar(kp[:], ps_qk[:],
                                                    bk_t[:, hp:hp + 1], None, ALU.add))
                            kev[hp] = CV.n

                        for h2 in range(2):
                            po = 64 * h2
                            hgl = 2 * hp + h2
                            for ic in range(NIC):
                                njt = (ic + 1) * (IC // 128)
                                rpar = u_idx % 2
                                smm = [0] * njt
                                sev = [0] * njt
                                avt = [0] * njt

                                def S_sec(jt, hp=hp, po=po, ic=ic, smm=smm, sev=sev):
                                    @block.tensor
                                    def _(t, jt=jt):
                                        spar = jt % 3
                                        W.w(t, "t", CV,
                                            sev[jt - 3] if jt >= 3 else sev_last[spar])
                                        W.w(t, "t", CV, qev[hp])
                                        W.w(t, "t", CV, kev[hp])
                                        CT.tick(t.matmul(
                                            ps_s[spar][:],
                                            kp[po:po + 64, jt * 128:(jt + 1) * 128],
                                            qp[po:po + 64, ic * IC:(ic + 1) * IC],
                                            start=True, stop=True))
                                        smm[jt] = CT.n

                                def EV_sec(jt, ic=ic, smm=smm, sev=sev, avt=avt):
                                    @block.vector
                                    def _(v, jt=jt):
                                        spar = jt % 3
                                        W.w(v, "v", CT, smm[jt])
                                        W.w(v, "v", CT,
                                            avt[jt - 3] if jt >= 3 else slot_av[jt % 3])
                                        dst0 = (jt % 3) * IC
                                        band0 = ic * (IC // 128)
                                        ckc = ckcols[:, jt:jt + 1]
                                        W.w(v, "v", CV, CV.n)
                                        if jt < band0:
                                            CV.tick(v.tensor_scalar(
                                                st_buf[:, dst0:dst0 + IC],
                                                ps_s[spar][:], ckc, None, ALU.mult))
                                        else:
                                            bp = jt - band0
                                            CV.tick(v.scalar_tensor_tensor(
                                                st_buf[:, dst0:dst0 + IC],
                                                ps_s[spar][:], ckc,
                                                bmask[:, bp * 512:(bp + 1) * 512],
                                                ALU.mult, ALU.mult))
                                        sev[jt] = CV.n

                                def AV_sec(jt, hgl=hgl, njt=njt, rpar=rpar,
                                           sev=sev, avt=avt):
                                    @block.tensor
                                    def _(t, jt=jt):
                                        W.w(t, "t", CV, sev[jt])
                                        if jt == 0:
                                            W.w(t, "t", CV, ret_ev[0])
                                        CT.tick(t.matmul(
                                            ps_ret[0][:],
                                            Vt[:, jt * D + hgl * 64: jt * D + hgl * 64 + 64],
                                            st_buf[:, (jt % 3) * IC:(jt % 3 + 1) * IC],
                                            start=(jt == 0), stop=(jt == njt - 1)))
                                        avt[jt] = CT.n

                                S_sec(0)
                                if njt > 1:
                                    S_sec(1)
                                    EV_sec(0)
                                for jt in range(2, njt):
                                    S_sec(jt)
                                    EV_sec(jt - 1)
                                    AV_sec(jt - 2)
                                if njt > 1:
                                    EV_sec(njt - 1)
                                    AV_sec(njt - 2)
                                else:
                                    EV_sec(0)
                                AV_sec(njt - 1)

                                for jt in range(njt):
                                    slot_av[jt % 3] = max(slot_av[jt % 3], avt[jt])
                                for jt in range(njt):
                                    sev_last[jt % 2] = max(sev_last[jt % 2], sev[jt])
                                av_last = avt[njt - 1]

                                @block.vector
                                def _(v, hp=hp, po=po, ic=ic, av_last=av_last):
                                    W.w(v, "v", CT, av_last)
                                    W.w(v, "v", CV, CV.n)
                                    CV.tick(v.tensor_copy(
                                        retained[po:po + 64,
                                                 hp * S + ic * IC: hp * S + (ic + 1) * IC],
                                        ps_ret[0][:]))
                                    ret_ev[0] = CV.n
                                u_idx += 1
                    a1_done_v = CV.n
                    if DBG and b == 0:
                        dbf = s1.enter_context(u_sb("dbf", [128, S], F32))
                        dcp = [0]

                        @block.vector
                        def _(v):
                            W.w(v, "v", CV, a1_done_v)
                            W.w(v, "v", CV, CV.n)
                            CV.tick(v.tensor_copy(dbf[:], qp[:]))
                        dcp[0] = CV.n

                        @block.sync
                        def _(sy):
                            W.w(sy, "sy", CV, dcp[0])
                            DW.tick(sy.dma_start(out=dbg_qp, in_=dbf[:]))
                        dw_q = DW.n

                        @block.vector
                        def _(v):
                            W.w(v, "v", DW, dw_q)
                            W.w(v, "v", CV, CV.n)
                            CV.tick(v.tensor_copy(dbf[:], kp[:]))
                        dcp[0] = CV.n

                        @block.sync
                        def _(sy):
                            W.w(sy, "sy", CV, dcp[0])
                            DW.tick(sy.dma_start(out=dbg_kp, in_=dbf[:]))
                            sy.wait_ge(DW.sem, DW.n)

                        @block.vector
                        def _(v):
                            W.w(v, "v", CV, CV.n)
                            CV.tick(v.tensor_copy(dbf[:], cq[:]))
                        dcp[0] = CV.n

                        @block.sync
                        def _(sy):
                            W.w(sy, "sy", CV, dcp[0])
                            DW.tick(sy.dma_start(out=dbg_cq, in_=dbf[:]))
                            sy.wait_ge(DW.sem, DW.n)

                        @block.vector
                        def _(v):
                            W.w(v, "v", CV, CV.n)
                            CV.tick(v.tensor_copy(dbf[:], ck[:]))
                        dcp[0] = CV.n

                        @block.sync
                        def _(sy):
                            W.w(sy, "sy", CV, dcp[0])
                            DW.tick(sy.dma_start(out=dbg_ck, in_=dbf[:]))
                            sy.wait_ge(DW.sem, DW.n)

                        for dcx in range(DC):
                            @block.vector
                            def _(v, dcx=dcx):
                                W.w(v, "v", CV, CV.n)
                                CV.tick(v.tensor_copy(dbf[:, 0:S],
                                                      xn_t[:, dcx * T: dcx * T + S]))
                            dctx = CV.n

                            @block.sync
                            def _(sy, dcx=dcx):
                                W.w(sy, "sy", CV, dctx)
                                DW.tick(sy.dma_start(out=dbg_xn[:, dcx * T: dcx * T + S],
                                                     in_=dbf[:, 0:S]))
                                sy.wait_ge(DW.sem, DW.n)

                        for dc in range(DC):
                            @block.vector
                            def _(v, dc=dc):
                                W.w(v, "v", CV, CV.n)
                                CV.tick(v.tensor_copy(
                                    dbf[:].rearrange("p (a s) -> p a s", a=1) if False
                                    else dbf[:, 0:S],
                                    retained[:, dc * S:(dc + 1) * S]))
                            dct = CV.n

                            @block.sync
                            def _(sy, dc=dc):
                                W.w(sy, "sy", CV, dct)
                                DW.tick(sy.dma_start(out=dbg_ret[:, dc * S:(dc + 1) * S],
                                                     in_=dbf[:, 0:S]))
                                sy.wait_ge(DW.sem, DW.n)
                s01.close()

                a2_floor = floor_now()
                if sB_prev[0] is not None:
                    sB_prev[0].close()
                acc = sB.enter_context(
                    u_sb("acc", [128, NTB * D], F16, side="right"))
                with ExitStack() as s2:
                    wg_bf = s2.enter_context(u_sb("wg_bf", [128, DC * D], BF16))
                    wo_bf = s2.enter_context(u_sb("wo_bf", [128, DC * D], BF16))
                    Gc = s2.enter_context(u_sb("Gc", [128, DC * IC2], BF16))
                    ngb = s2.enter_context(u_sb("ngb", [128, DC * IC2], BF16))
                    t1b = s2.enter_context(u_sb("t1b", [128, IC2], BF16))
                    t2c = s2.enter_context(u_sb("t2c", [128, IC2], BF16))
                    rowf = s2.enter_context(u_sb("rowf", [1, 3 * IC2], F32))
                    rall = s2.enter_context(u_sb("rall", [1, NIC2 * 2 * IC2], BF16))
                    ps_st = s2.enter_context(u_ps("ps_st", [1, 512], F32))
                    ps_bc = s2.enter_context(u_ps("ps_bc", [128, 512], F32))
                    ps_g = [s2.enter_context(u_ps(f"ps_g{i}", [128, 512], F32))
                            for i in range(2)]
                    ps_o = [s2.enter_context(u_ps(f"ps_o{i}", [128, D], F32))
                            for i in range(2)]

                    s2s = s2.enter_context(ExitStack())
                    stg4 = s2s.enter_context(u_sb("stg4", [128, DC * 64], F32))
                    wcast = 0
                    for which, (wap2, wbf2) in enumerate(((wg_ap, wg_bf), (wo_ap, wo_bf))):
                        for q in range(16):
                            @block.sync
                            def _(sy, wap2=wap2, q=q):
                                wait_floor(sy, "sy", a2_floor)
                                W.w(sy, "sy", CV, wcast)
                                W.w(sy, "sy", DW, DW.n)
                                DW.tick(sy.dma_start(
                                    out=stg4[:].rearrange("p (dc n) -> p dc n", dc=DC),
                                    in_=wap2[:, q * 64:(q + 1) * 64]
                                    .rearrange("(dc p) n -> p dc n", p=128)))
                            dwh = DW.n

                            @block.vector
                            def _(v, which=which, wbf2=wbf2, q=q):
                                W.w(v, "v", DW, dwh)
                                W.w(v, "v", CV, CV.n)
                                if which == 0:
                                    gt, qc = divmod(q, 2)
                                    CV.tick(v.tensor_copy(
                                        wbf2[:, gt * D:(gt + 1) * D]
                                        .rearrange("p (dc c) -> p dc c", dc=DC)
                                        [:, :, qc * 64:(qc + 1) * 64],
                                        stg4[:].rearrange("p (dc n) -> p dc n", dc=DC)))
                                else:
                                    CV.tick(v.tensor_copy(
                                        wbf2[:].rearrange("p (dc n) -> p dc n", dc=DC)
                                        [:, :, q * 64:(q + 1) * 64],
                                        stg4[:].rearrange("p (dc n) -> p dc n", dc=DC)))
                            wcast = CV.n
                    wgo_ready = wcast

                    sq_read = 0
                    st_read = 0
                    sq_tagv = [0] * NIC2
                    stat_mmv = [0] * NIC2
                    for ic in range(NIC2):
                        @block.vector
                        def _(v, ic=ic):
                            W.w(v, "v", CV, a1_done_v)
                            W.w(v, "v", CT, stat_mmv[ic - 1] if ic >= 1 else 0)
                            for dc in range(DC):
                                W.w(v, "v", CV, CV.n)
                                CV.tick(v.tensor_mul(
                                    ngb[:, dc * IC2:(dc + 1) * IC2],
                                    retained[:, dc * S + ic * IC2: dc * S + (ic + 1) * IC2],
                                    retained[:, dc * S + ic * IC2: dc * S + (ic + 1) * IC2]))
                        sq_tagv[ic] = CV.n

                        @block.tensor
                        def _(t, ic=ic):
                            W.w(t, "t", CV, sq_tagv[ic])
                            W.w(t, "t", CV, st_read)
                            for dc in range(DC):
                                CT.tick(t.matmul(
                                    ps_st[0:1, 0:IC2], ones_c[:, 0:1],
                                    retained[:, dc * S + ic * IC2: dc * S + (ic + 1) * IC2],
                                    start=(dc == 0), stop=False))
                                CT.tick(t.matmul(
                                    ps_st[0:1, IC2:2 * IC2], ones_c[:, 0:1],
                                    ngb[:, dc * IC2:(dc + 1) * IC2],
                                    start=False, stop=(dc == DC - 1)))
                        stat_mmv[ic] = CT.n

                        @block.vector
                        def _(v, ic=ic):
                            W.w(v, "v", CT, stat_mmv[ic])
                            W.w(v, "v", CV, CV.n)
                            CV.tick(v.tensor_scalar(rowf[0:1, 0:IC2], ps_st[0:1, 0:IC2],
                                                    1.0 / D, None, ALU.mult))
                            CV.tick(v.tensor_scalar(rowf[0:1, IC2:2 * IC2],
                                                    ps_st[0:1, IC2:2 * IC2],
                                                    1.0 / D, None, ALU.mult))
                            W.w(v, "v", CV, CV.n)
                            CV.tick(v.tensor_mul(rowf[0:1, 2 * IC2:3 * IC2],
                                                 rowf[0:1, 0:IC2], rowf[0:1, 0:IC2]))
                            W.w(v, "v", CV, CV.n)
                            CV.tick(v.scalar_tensor_tensor(
                                rowf[0:1, 2 * IC2:3 * IC2], rowf[0:1, 2 * IC2:3 * IC2],
                                -1.0, rowf[0:1, IC2:2 * IC2], ALU.mult, ALU.add))
                            W.w(v, "v", CV, CV.n)
                            CV.tick(v.tensor_scalar(rowf[0:1, 2 * IC2:3 * IC2],
                                                    rowf[0:1, 2 * IC2:3 * IC2],
                                                    LN_EPS, None, ALU.add))
                        var_tag = CV.n
                        st_read = var_tag

                        @block.scalar
                        def _(s, ic=ic, var_tag=var_tag):
                            W.w(s, "s", CV, var_tag)
                            W.w(s, "s", CS, CS.n)
                            CS.tick(s.activation(rowf[0:1, 2 * IC2:3 * IC2],
                                                 rowf[0:1, 2 * IC2:3 * IC2], AF.Ln))
                            W.w(s, "s", CS, CS.n)
                            CS.tick(s.activation(rall[0:1, ic * 512: ic * 512 + IC2],
                                                 rowf[0:1, 2 * IC2:3 * IC2], AF.Exp,
                                                 scale=-0.5))
                        rstd_tag = CS.n

                        @block.vector
                        def _(v, ic=ic, rstd_tag=rstd_tag):
                            W.w(v, "v", CS, rstd_tag)
                            W.w(v, "v", CV, CV.n)
                            CV.tick(v.scalar_tensor_tensor(
                                rall[0:1, ic * 512 + IC2: ic * 512 + 2 * IC2],
                                rowf[0:1, 0:IC2], -1.0,
                                rall[0:1, ic * 512: ic * 512 + IC2],
                                ALU.mult, ALU.mult))
                    rall_done = CV.n

                    bc_read = 0
                    ng_used = 0
                    oev_tag = [0, 0]
                    gic_done = [0] * (NIC2 + 1)
                    gc_readv = [0] * (NIC2 + 1)

                    def emit_g(gi):
                        cs_sg0 = CS.reserve(DC)
                        gmm = [0] * DC
                        for gt in range(DC):
                            gpar = gt % 2

                            @block.tensor
                            def _(t, gi=gi, gt=gt, gpar=gpar):
                                W.w(t, "t", CV, wgo_ready)
                                if gt >= 2:
                                    W.w(t, "t", CS, cs_sg0 + gt - 1)
                                elif gi >= 1:
                                    W.w(t, "t", CS, gic_done[gi - 1])
                                for dc in range(DC):
                                    CT.tick(t.matmul(
                                        ps_g[gpar][:, 0:IC2],
                                        wg_bf[:, gt * D + dc * 128: gt * D + (dc + 1) * 128],
                                        xn_t[:, dc * T + tok0 + gi * IC2:
                                             dc * T + tok0 + (gi + 1) * IC2],
                                        start=(dc == 0), stop=(dc == DC - 1)))
                                gmm[gt] = CT.n

                        @block.scalar
                        def _(s, gi=gi):
                            W.w(s, "s", CV, gc_readv[gi - 1] if gi >= 1 else 0)
                            for gt in range(DC):
                                W.w(s, "s", CT, gmm[gt])
                                CS.tick_raw(s.activation(
                                    Gc[:, gt * IC2:(gt + 1) * IC2],
                                    ps_g[gt % 2][:, 0:IC2],
                                    AF.Sigmoid, bias=bg_t[:, gt:gt + 1]))
                        assert CS.n == cs_sg0 + DC
                        gic_done[gi] = cs_sg0 + DC

                    emit_g(0)
                    for ic in range(NIC2):
                        @block.tensor
                        def _(t, ic=ic):
                            W.w(t, "t", CV, rall_done)
                            W.w(t, "t", CV, bc_read)
                            CT.tick(t.matmul(ps_bc[:, 0:IC2], ones_r[0:1, :],
                                             rall[0:1, ic * 512: ic * 512 + IC2],
                                             start=True, stop=False))
                            CT.tick(t.matmul(ps_bc[:, IC2:2 * IC2], ones_r[0:1, :],
                                             rall[0:1, ic * 512 + IC2: ic * 512 + 2 * IC2],
                                             start=False, stop=True))
                        bc_mm = CT.n

                        @block.vector
                        def _(v, ic=ic, bc_mm=bc_mm):
                            W.w(v, "v", CT, bc_mm)
                            W.w(v, "v", CT, ng_used)
                            W.w(v, "v", CS, gic_done[ic])
                            for dc in range(DC):
                                W.w(v, "v", CV, CV.n)
                                CV.tick(v.scalar_tensor_tensor(
                                    t1b[:],
                                    retained[:, dc * S + ic * IC2: dc * S + (ic + 1) * IC2],
                                    gam_t[:, dc:dc + 1], ps_bc[:, 0:IC2],
                                    ALU.mult, ALU.mult))
                                W.w(v, "v", CV, CV.n)
                                CV.tick(v.scalar_tensor_tensor(
                                    t2c[:], ps_bc[:, IC2:2 * IC2], gam_t[:, dc:dc + 1],
                                    t1b[:], ALU.mult, ALU.add))
                                W.w(v, "v", CV, CV.n)
                                CV.tick(v.scalar_tensor_tensor(
                                    ngb[:, dc * IC2:(dc + 1) * IC2], t2c[:],
                                    bet_t[:, dc:dc + 1],
                                    Gc[:, dc * IC2:(dc + 1) * IC2],
                                    ALU.add, ALU.mult))
                        ng_tag = CV.n
                        bc_read = ng_tag
                        gc_readv[ic] = ng_tag

                        if ic + 1 < NIC2:
                            emit_g(ic + 1)

                        for tsub in range(IC2 // 128):
                            opar = tsub % 2
                            tt_loc = ic * (IC2 // 128) + tsub

                            @block.tensor
                            def _(t, ic=ic, tsub=tsub, opar=opar, ng_tag=ng_tag):
                                W.w(t, "t", CV, ng_tag)
                                W.w(t, "t", CV, wgo_ready)
                                W.w(t, "t", CS, oev_tag[opar])
                                for dc in range(DC):
                                    for n2 in range(2):
                                        CT.tick(t.matmul(
                                            ps_o[opar][:, n2 * 512:(n2 + 1) * 512],
                                            ngb[:, dc * IC2 + tsub * 128:
                                                dc * IC2 + (tsub + 1) * 128],
                                            wo_bf[:, dc * D + n2 * 512:
                                                  dc * D + (n2 + 1) * 512],
                                            start=(dc == 0), stop=False))
                                for n2 in range(2):
                                    CT.tick(t.matmul(
                                        ps_o[opar][:, n2 * 512:(n2 + 1) * 512],
                                        ones_r[0:1, :],
                                        bo_bf[0:1, n2 * 512:(n2 + 1) * 512],
                                        start=False, stop=True))
                            omm = CT.n
                            ng_used = max(ng_used, omm)

                            @block.scalar
                            def _(s, tt_loc=tt_loc, opar=opar, omm=omm):
                                W.w(s, "s", CT, omm)
                                W.w(s, "s", CS, CS.n)
                                CS.tick(s.activation(acc[:, tt_loc * D:(tt_loc + 1) * D],
                                                     ps_o[opar][:], AF.Copy))
                                oev_tag[opar] = CS.n
                    a2_done_s = CS.n
                    a2_done_v = CV.n

            f_floor = floor_now()
            stt_tag = [0] * NTB
            with ExitStack() as sF:
                w1c = [sF.enter_context(u_sb(f"w1c{i}", [128, DC * 512], BF16))
                       for i in range(2)]
                w2c = [sF.enter_context(u_sb(f"w2c{i}", [128, NJT * D], BF16))
                       for i in range(2)]
                stg1 = sF.enter_context(u_sb("stg1", [128, DC * 512], F32))
                stg2 = sF.enter_context(u_sb("stg2", [128, NJT * D], F32))
                hb = sF.enter_context(u_sb("hb", [128, NJT * S], BF16))
                b1t = sF.enter_context(u_sb("b1t", [128, NHT], F32))
                b1r = sF.enter_context(u_sb("b1r", [NHT, 128], F32))
                wT_bf = sF.enter_context(u_sb("wT_bf", [8, S], BF16))
                ps_h = [sF.enter_context(u_ps(f"ps_h{i}", [128, TH], F32))
                        for i in range(2)]
                ps_o2 = [sF.enter_context(u_ps(f"ps_o2{i}", [128, D], F32))
                         for i in range(2)]

                wT_last = 0
                for tt in range(NTB):
                    @block.tensor
                    def _(t, tt=tt, wT_last=wT_last):
                        W.w(t, "t", CV, p0_done_v)
                        W.w(t, "t", CS, a2_done_s)
                        W.w(t, "t", CV, a2_done_v)
                        W.w(t, "t", CV, wT_last)
                        CT.tick(t.transpose(ps_h[0][0:8, 0:128],
                                            wrt[:, (b * NTB + tt) * 8:(b * NTB + tt + 1) * 8],
                                            id_f[:]))
                    trt = CT.n

                    @block.vector
                    def _(v, tt=tt, trt=trt):
                        W.w(v, "v", CT, trt)
                        W.w(v, "v", CV, CV.n)
                        CV.tick(v.tensor_copy(wT_bf[0:8, tt * 128:(tt + 1) * 128],
                                              ps_h[0][0:8, 0:128]))
                    wT_last = CV.n
                wT_done = wT_last

                cast1 = {}
                cast2 = {}
                w1mm_last = {}
                w2mm_last = {}
                pso2_read = [0, 0]
                w2_pend = None
                hfree = [0, 0]
                hread = {}
                b1tr = [0] * E
                b1cp = [0] * E

                for e in range(E):
                    @block.scalar
                    def _(s, e=e):
                        wait_floor(s, "s", f_floor)
                        if e >= 1:
                            W.w(s, "s", CT, b1tr[e - 1])
                        W.w(s, "s", DX, DX.n)
                        DX.tick(s.dma_start(out=b1r[:],
                                            in_=b1_ap[e].rearrange("(a c) -> a c", c=128)))
                    dxh = DX.n

                    @block.tensor
                    def _(t, e=e, dxh=dxh):
                        W.w(t, "t", DX, dxh)
                        W.w(t, "t", CV, wT_done)
                        W.w(t, "t", CS, hfree[1])
                        if e >= 1:
                            W.w(t, "t", CV, b1cp[e - 1])
                        CT.tick(t.transpose(ps_h[1][:, 0:NHT], b1r[:], id_f[0:NHT, 0:NHT]))
                        b1tr[e] = CT.n

                    @block.vector
                    def _(v, e=e):
                        W.w(v, "v", CT, b1tr[e])
                        W.w(v, "v", CS, max(hfree))
                        W.w(v, "v", CV, CV.n)
                        CV.tick(v.tensor_copy(b1t[:], ps_h[1][:, 0:NHT]))
                        b1cp[e] = CV.n

                    for jc in range(NJC):
                        k = e * NJC + jc
                        par = k % 2

                        @block.sync
                        def _(sy, e=e, jc=jc, k=k):
                            wait_floor(sy, "sy", f_floor)
                            W.w(sy, "sy", CV, cast1.get(k - 1, 0))
                            W.w(sy, "sy", DW, DW.n)
                            DW.tick(sy.dma_start(
                                out=stg1[:].rearrange("p (dc j) -> p dc j", dc=DC),
                                in_=w1_ap[e][:, jc * JC:(jc + 1) * JC]
                                .rearrange("(dc p) j -> p dc j", p=128)))
                        dw1 = DW.n

                        @block.vector
                        def _(v, k=k, par=par, dw1=dw1):
                            W.w(v, "v", DW, dw1)
                            W.w(v, "v", CT, w1mm_last.get(k - 2, 0))
                            W.w(v, "v", CV, CV.n)
                            CV.tick(v.tensor_copy(w1c[par][:], stg1[:]))
                        cast1[k] = CV.n

                        @block.sync
                        def _(sy, e=e, jc=jc, k=k):
                            wait_floor(sy, "sy", f_floor)
                            W.w(sy, "sy", CV, cast2.get(k - 1, 0))
                            W.w(sy, "sy", DW, DW.n)
                            DW.tick(sy.dma_start(
                                out=stg2[:].rearrange("p (jt d) -> p jt d", jt=NJT),
                                in_=w2_ap[e][jc * JC:(jc + 1) * JC, :]
                                .rearrange("(jt p) d -> p jt d", p=128)))
                        dw2 = DW.n

                        @block.vector
                        def _(v, k=k, par=par, dw2=dw2):
                            W.w(v, "v", DW, dw2)
                            W.w(v, "v", CT, w2mm_last.get(k - 2, 0))
                            W.w(v, "v", CV, CV.n)
                            CV.tick(v.tensor_copy(w2c[par][:], stg2[:]))
                        cast2[k] = CV.n

                        NW1 = NJT * NTH
                        cs_g0 = CS.reserve(NW1)
                        w1_here = [0] * NW1

                        def emit_w1(idx, e=e, jc=jc, k=k, par=par, cs_g0=cs_g0):
                            jt, th = divmod(idx, NTH)
                            hpar = idx % 2
                            if True:

                                @block.tensor
                                def _(t, e=e, jc=jc, jt=jt, th=th, k=k, par=par,
                                      hpar=hpar, idx=idx):
                                    W.w(t, "t", CV, cast1[k])
                                    W.w(t, "t", CV, b1cp[e])
                                    if idx >= 2:
                                        W.w(t, "t", CS, cs_g0 + idx - 1)
                                    else:
                                        W.w(t, "t", CS, hfree[hpar])
                                    for dc in range(DC):
                                        for n2 in range(TH // 512):
                                            CT.tick(t.matmul(
                                                ps_h[hpar][:, n2 * 512:(n2 + 1) * 512],
                                                w1c[par][:, dc * 512 + jt * 128:
                                                     dc * 512 + (jt + 1) * 128],
                                                xn_t[:, dc * T + tok0 + th * TH + n2 * 512:
                                                     dc * T + tok0 + th * TH + (n2 + 1) * 512],
                                                start=(dc == 0), stop=(dc == DC - 1)))
                                    w1mm_last[k] = CT.n
                                    w1_here[idx] = CT.n

                        emit_w1(0)
                        emit_w1(1)
                        if w2_pend is not None:
                            emit_w2(*w2_pend)
                            w2_pend = None
                        for _i in range(2, NW1):
                            emit_w1(_i)

                        @block.scalar
                        def _(s, e=e, jc=jc, k=k):
                            for idx in range(NW1):
                                jt, th = divmod(idx, NTH)
                                ht = jc * NJT + jt
                                W.w(s, "s", CT, w1_here[idx])
                                if idx == 0:
                                    W.w(s, "s", CT, w2mm_last.get(k - 1, 0))
                                CS.tick_raw(s.activation(
                                    hb[:, jt * S + th * TH: jt * S + (th + 1) * TH],
                                    ps_h[idx % 2][:], AF.Gelu, bias=b1t[:, ht:ht + 1]))
                        assert CS.n == cs_g0 + NW1
                        hfree[0] = cs_g0 + NW1 - 1
                        hfree[1] = cs_g0 + NW1
                        if NW1 >= 2 and (NW1 - 2) % 2 == 1:
                            hfree[0], hfree[1] = cs_g0 + NW1, cs_g0 + NW1 - 1
                        gelu_all = cs_g0 + NW1

                        def emit_w2(e2, jc2, k2, par2, gelu_all2):
                            w2_here = [0] * NTB
                            for tth in range(NTB // 4):
                                stt_base = CV.reserve(4)

                                @block.tensor
                                def _(t, e=e2, jc=jc2, tth=tth, k=k2, par=par2,
                                      gelu_all=gelu_all2, stt_base=stt_base):
                                    W.w(t, "t", CS, gelu_all)
                                    W.w(t, "t", CV, cast2[k])
                                    for tt in range(tth * 4, (tth + 1) * 4):
                                        opar = tt % 2
                                        if tt - 2 >= tth * 4:
                                            W.w(t, "t", CV,
                                                stt_base + (tt - 2 - tth * 4) + 1)
                                        else:
                                            W.w(t, "t", CV, pso2_read[opar])
                                        for jt in range(NJT):
                                            last = (jt == NJT - 1) and not (e == 0 and jc == 0)
                                            for n2 in range(2):
                                                CT.tick(t.matmul(
                                                    ps_o2[opar][:, n2 * 512:(n2 + 1) * 512],
                                                    hb[:, jt * S + tt * 128: jt * S + (tt + 1) * 128],
                                                    w2c[par][:, jt * D + n2 * 512:
                                                         jt * D + (n2 + 1) * 512],
                                                    start=(jt == 0), stop=last))
                                        if e == 0 and jc == 0:
                                            for n2 in range(2):
                                                CT.tick(t.matmul(
                                                    ps_o2[opar][:, n2 * 512:(n2 + 1) * 512],
                                                    wT_bf[0:8, tt * 128:(tt + 1) * 128],
                                                    b2_bf[0:8, n2 * 512:(n2 + 1) * 512],
                                                    start=False, stop=True))
                                        w2mm_last[k] = CT.n
                                        hread[k] = CT.n
                                        w2_here[tt] = CT.n

                                for tt in range(tth * 4, (tth + 1) * 4):
                                    opar = tt % 2
                                    mytag = stt_base + (tt - tth * 4) + 1

                                    @block.vector
                                    def _(v, e=e2, tt=tt, opar=opar):
                                        W.w(v, "v", CT, w2_here[tt])
                                        W.w(v, "v", CS, a2_done_s)
                                        W.w(v, "v", CV, stt_tag[tt])
                                        CV.tick_raw(v.scalar_tensor_tensor(
                                            acc[:, tt * D:(tt + 1) * D], ps_o2[opar][:],
                                            wrt[:, (b * NTB + tt) * 8 + e:
                                                (b * NTB + tt) * 8 + e + 1],
                                            acc[:, tt * D:(tt + 1) * D], ALU.mult, ALU.add))
                                    stt_tag[tt] = mytag
                                    pso2_read[opar] = mytag

                        w2_pend = (e, jc, k, par, gelu_all)

                if w2_pend is not None:
                    emit_w2(*w2_pend)
                    w2_pend = None

                pre_out_floor = floor_now()
                add_tag = [0] * NTB
                odma_tag = [0] * NTB
                for tt in range(NTB):
                    @block.scalar
                    def _(s, tt=tt):
                        if tt >= 1:
                            W.w(s, "s", CV, add_tag[tt - 1])
                        if prev_add_last:
                            W.w(s, "s", CV, prev_add_last)
                        W.w(s, "s", DX, DX.n)
                        DX.tick(s.dma_start(out=xo[0][:],
                                            in_=x_ap[b, tt * 128:(tt + 1) * 128, :]))
                    dxh = DX.n

                    @block.vector
                    def _(v, tt=tt, dxh=dxh):
                        W.w(v, "v", DX, dxh)
                        W.w(v, "v", CV, stt_tag[tt])
                        if tt >= 1:
                            W.w(v, "v", DX, odma_tag[tt - 1])
                        elif prev_odma:
                            W.w(v, "v", DX, prev_odma)
                        W.w(v, "v", CV, CV.n)
                        CV.tick(v.tensor_add(oo[0][:], xo[0][:],
                                             acc[:, tt * D:(tt + 1) * D]))
                        add_tag[tt] = CV.n

                    @block.scalar
                    def _(s, tt=tt):
                        W.w(s, "s", CV, add_tag[tt])
                        W.w(s, "s", DX, DX.n)
                        DX.tick(s.dma_start(out=out_ap[b, tt * 128:(tt + 1) * 128, :],
                                            in_=oo[0][:]))
                        odma_tag[tt] = DX.n
                prev_add_last = add_tag[-1]
                prev_odma = odma_tag[-1]
            sB_prev[0] = sB

        if sB_prev[0] is not None:
            sB_prev[0].close()

        @block.scalar
        def _(s):
            s.wait_ge(DX.sem, DX.n)
            s.wait_ge(DG.sem, DG.n)

    return nc


def kernel(**inputs):
    P = dict(FULL)
    x = np.ascontiguousarray(np.asarray(inputs["L0_attn_x"], dtype=np.float32))
    nb = x.shape[0] // N_CORES
    base = {
        "wq": np.ascontiguousarray(np.asarray(inputs["L0_attn_Wq"], dtype=np.float32)),
        "bq": np.ascontiguousarray(np.asarray(inputs["L0_attn_bq"], dtype=np.float32)),
        "wk": np.ascontiguousarray(np.asarray(inputs["L0_attn_Wk"], dtype=np.float32)),
        "bk": np.ascontiguousarray(np.asarray(inputs["L0_attn_bk"], dtype=np.float32)),
        "wv": np.ascontiguousarray(np.asarray(inputs["L0_attn_Wv"], dtype=np.float32)),
        "bv": np.ascontiguousarray(np.asarray(inputs["L0_attn_bv"], dtype=np.float32)),
        "wg": np.ascontiguousarray(np.asarray(inputs["L0_attn_Wg"], dtype=np.float32)),
        "bg": np.ascontiguousarray(np.asarray(inputs["L0_attn_bg"], dtype=np.float32)),
        "wo": np.ascontiguousarray(np.asarray(inputs["L0_attn_Wo"], dtype=np.float32)),
        "bo": np.ascontiguousarray(np.asarray(inputs["L0_attn_bo"], dtype=np.float32)),
        "gam": np.ascontiguousarray(np.asarray(inputs["L0_attn_gamma"], dtype=np.float32)),
        "bet": np.ascontiguousarray(np.asarray(inputs["L0_attn_beta"], dtype=np.float32)),
        "w1": np.ascontiguousarray(np.asarray(inputs["L0_ffn_W1"], dtype=np.float32)),
        "b1": np.ascontiguousarray(np.asarray(inputs["L0_ffn_b1"], dtype=np.float32)),
        "w2": np.ascontiguousarray(np.asarray(inputs["L0_ffn_W2"], dtype=np.float32)),
        "b2": np.ascontiguousarray(np.asarray(inputs["L0_ffn_b2"], dtype=np.float32)),
        "wr": np.ascontiguousarray(np.asarray(inputs["L0_ffn_Wr"], dtype=np.float32)),
        "br": np.ascontiguousarray(np.asarray(inputs["L0_ffn_br"], dtype=np.float32)),
    }
    nc = build_program(P)
    in_maps = []
    for c in range(N_CORES):
        m = dict(base)
        m["x"] = np.ascontiguousarray(x[c * nb:(c + 1) * nb])
        in_maps.append(m)
    res = run_bass_kernel_spmd(nc, in_maps, core_ids=list(range(N_CORES)))
    return np.concatenate([r["out"] for r in res.results], axis=0)


# version 20 (final)
# speedup vs baseline: 1.1476x; 1.0603x over previous
import math
import os
import numpy as np
from contextlib import ExitStack

import concourse.bass as bass
import concourse.mybir as mybir
from concourse.bass_utils import run_bass_kernel_spmd

F32 = mybir.dt.float32
BF16 = mybir.dt.bfloat16
F16 = mybir.dt.float16
AF = mybir.ActivationFunctionType
ALU = mybir.AluOpType
AX = mybir.AxisListType

FULL = dict(B=2, S=2048, D=1024, H=16, E=8, HID=4096)
DECAY = 0.99
LN_EPS = 1e-5
N_CORES = 8


class Ctr:
    def __init__(self, es, nc, name, step=1):
        self.sem = es.enter_context(nc.semaphore(name))
        self.n = 0
        self.step = step

    def tick(self, instr):
        instr.then_inc(self.sem, self.step)
        self.n += self.step
        return self.n

    def reserve(self, k):
        base = self.n
        self.n += k * self.step
        return base

    def tick_raw(self, instr):
        instr.then_inc(self.sem, self.step)


class Waiter:

    def __init__(self):
        self.seen = {}

    def w(self, eng, key, ctr, tag):
        if tag <= 0:
            return
        k = (key, id(ctr))
        if tag <= self.seen.get(k, 0):
            return
        eng.wait_ge(ctr.sem, tag)
        self.seen[k] = tag


def build_program(P):
    B, S, D, H, E, HID = P["B"], P["S"], P["D"], P["H"], P["E"], P["HID"]
    T = B * S
    DC = D // 128
    NT = T // 128
    NTB = S // 128
    HP = H // 2
    JC = 512
    NJT = JC // 128
    NJC = HID // JC
    IC = 512
    NIC = S // IC
    IC2 = 256
    NIC2 = S // IC2
    TH = min(1024, S)
    NTH = S // TH
    NHT = HID // 128
    assert D == 1024 and DC == 8 and HP == 8 and S % IC == 0 and S % IC2 == 0

    nc = bass.Bass("TRN2", target_bir_lowering=False)

    x_ap = nc.dram_tensor("x", [B, S, D], F32, kind="ExternalInput").ap()
    wq_ap = nc.dram_tensor("wq", [D, D], F32, kind="ExternalInput").ap()
    bq_ap = nc.dram_tensor("bq", [D], F32, kind="ExternalInput").ap()
    wk_ap = nc.dram_tensor("wk", [D, D], F32, kind="ExternalInput").ap()
    bk_ap = nc.dram_tensor("bk", [D], F32, kind="ExternalInput").ap()
    wv_ap = nc.dram_tensor("wv", [D, D], F32, kind="ExternalInput").ap()
    bv_ap = nc.dram_tensor("bv", [D], F32, kind="ExternalInput").ap()
    wg_ap = nc.dram_tensor("wg", [D, D], F32, kind="ExternalInput").ap()
    bg_ap = nc.dram_tensor("bg", [D], F32, kind="ExternalInput").ap()
    wo_ap = nc.dram_tensor("wo", [D, D], F32, kind="ExternalInput").ap()
    bo_ap = nc.dram_tensor("bo", [D], F32, kind="ExternalInput").ap()
    gam_ap = nc.dram_tensor("gam", [D], F32, kind="ExternalInput").ap()
    bet_ap = nc.dram_tensor("bet", [D], F32, kind="ExternalInput").ap()
    w1_ap = nc.dram_tensor("w1", [E, D, HID], F32, kind="ExternalInput").ap()
    b1_ap = nc.dram_tensor("b1", [E, HID], F32, kind="ExternalInput").ap()
    w2_ap = nc.dram_tensor("w2", [E, HID, D], F32, kind="ExternalInput").ap()
    b2_ap = nc.dram_tensor("b2", [E, D], F32, kind="ExternalInput").ap()
    wr_ap = nc.dram_tensor("wr", [D, E], F32, kind="ExternalInput").ap()
    br_ap = nc.dram_tensor("br", [E], F32, kind="ExternalInput").ap()
    out_ap = nc.dram_tensor("out", [B, S, D], F32, kind="ExternalOutput").ap()
    DBG = bool(os.environ.get("KDBG"))
    if DBG:
        dbg_ret = nc.dram_tensor("dbg_ret", [128, DC * S], F32, kind="ExternalOutput").ap()
        dbg_qp = nc.dram_tensor("dbg_qp", [128, S], F32, kind="ExternalOutput").ap()
        dbg_kp = nc.dram_tensor("dbg_kp", [128, S], F32, kind="ExternalOutput").ap()
        dbg_cq = nc.dram_tensor("dbg_cq", [128, S], F32, kind="ExternalOutput").ap()
        dbg_ck = nc.dram_tensor("dbg_ck", [128, S], F32, kind="ExternalOutput").ap()
        dbg_xn = nc.dram_tensor("dbg_xn", [128, DC * T], F32, kind="ExternalOutput").ap()

    es = ExitStack()
    _uq = [0]

    def _un(n):
        _uq[0] += 1
        return f"{n}_{_uq[0]}"

    def u_sb(name, shape, dtype, **kw):
        return nc.sbuf_tensor(_un(name), shape, dtype, **kw)

    def u_ps(name, shape, dtype=F32, **kw):
        return nc.psum_tensor(_un(name), shape, dtype, **kw)

    def sb(name, shape, dtype, side=None):
        kw = {} if side is None else dict(side=side)
        return es.enter_context(u_sb(name, shape, dtype, **kw))

    xn_t = sb("xn_t", [128, DC * T], BF16)
    wrt = sb("wrt", [128, NT * 8], F32)
    cqrow = sb("cqrow", [1, S], BF16, side="right")
    ckcols = sb("ckcols", [128, S // 128], F32)
    iota_c = sb("iota_c", [128, S // 128], F32)
    bq_t = sb("bq_t", [128, DC], F32)
    bk_t = sb("bk_t", [128, DC], F32)
    bg_t = sb("bg_t", [128, DC], F32)
    gam_t = sb("gam_t", [128, DC], F32)
    bet_t = sb("bet_t", [128, DC], F32)
    bv_bf = sb("bv_bf", [1, D], BF16)
    bo_bf = sb("bo_bf", [1, D], BF16)
    br_bf = sb("br_bf", [1, E], BF16)
    b2_bf = sb("b2_bf", [E, D], BF16)
    id_bf = sb("id_bf", [128, 128], BF16)
    xo = [sb("xo0", [128, D], F32)]
    oo = [sb("oo0", [128, D], F32)]
    id_f = sb("id_f", [128, 128], F32)
    tri_bf = sb("tri_bf", [128, 128], BF16)
    ones_r = sb("ones_r", [1, 128], BF16)
    ones_c = sb("ones_c", [128, 1], BF16)

    block = es.enter_context(nc.Block())
    CT = Ctr(es, nc, "ct")
    CV = Ctr(es, nc, "cv")
    CS = Ctr(es, nc, "cs")
    CG = Ctr(es, nc, "cg")
    DW = Ctr(es, nc, "dw", 16)
    DX = Ctr(es, nc, "dx", 16)
    DG = Ctr(es, nc, "dg", 16)
    W = Waiter()

    LN99 = math.log(DECAY)
    LN8I = math.log(1.0 / math.sqrt(D // H))

    def floor_now():
        return (CT.n, CV.n, CS.n, DW.n, DX.n, CG.n, DG.n)

    def wait_floor(eng, key, f):
        W.w(eng, key, CT, f[0])
        W.w(eng, key, CV, f[1])
        W.w(eng, key, CS, f[2])
        W.w(eng, key, DW, f[3])
        W.w(eng, key, DX, f[4])
        W.w(eng, key, CG, f[5])
        W.w(eng, key, DG, f[6])

    with es:
        p0b = ExitStack()
        def sb0(name, shape, dtype):
            return p0b.enter_context(u_sb(name, shape, dtype))
        iota_f = sb0("iota_f", [128, 128], F32)
        iota_row = sb0("iota_row", [1, S], F32)
        rows_b = [sb0(f"rows_{i}", [8, 128], F32) for i in range(5)]
        bv_row = sb0("bv_row", [1, D], F32)
        bo_row = sb0("bo_row", [1, D], F32)
        br_row = sb0("br_row", [1, E], F32)
        b2_row = sb0("b2_row", [E, D], F32)
        wr_st = sb0("wr_st", [128, DC * E], F32)
        wr_bf = sb0("wr_bf", [128, DC * E], BF16)
        xin = [sb0(f"xin{i}", [128, 4 * D], F32) for i in range(2)]
        xnk = [sb0(f"xnk{i}", [128, 4 * D], BF16) for i in range(2)]
        sqb = sb0("sqb", [128, 4 * D], BF16)
        st1 = sb0("st1", [128, 24], F32)
        ltb = sb0("ltb", [128, 8], F32)
        eqb = sb0("eqb", [128, 8], F32)
        t2b = sb0("t2b", [128, 8], F32)
        kpb = sb0("kpb", [128, 8], F32)
        exb = sb0("exb", [128, 8], F32)
        ekb = sb0("ekb", [128, 8], F32)
        c_ln8i = sb0("c_ln8i", [1, 1], F32)

        @block.sync
        def _(sy):
            W.w(sy, "sy", DW, DW.n)
            DW.tick(sy.dma_start(out=wr_st[:].rearrange("p (dc e) -> p dc e", dc=DC),
                                 in_=wr_ap.rearrange("(dc p) e -> p dc e", p=128)))
            W.w(sy, "sy", DW, DW.n)
            DW.tick(sy.dma_start(out=rows_b[0][:], in_=bq_ap.rearrange("(a c) -> a c", c=128)))
            W.w(sy, "sy", DW, DW.n)
            DW.tick(sy.dma_start(out=rows_b[1][:], in_=bk_ap.rearrange("(a c) -> a c", c=128)))
            W.w(sy, "sy", DW, DW.n)
            DW.tick(sy.dma_start(out=rows_b[2][:], in_=bg_ap.rearrange("(a c) -> a c", c=128)))
            W.w(sy, "sy", DW, DW.n)
            DW.tick(sy.dma_start(out=rows_b[3][:], in_=gam_ap.rearrange("(a c) -> a c", c=128)))
            W.w(sy, "sy", DW, DW.n)
            DW.tick(sy.dma_start(out=rows_b[4][:], in_=bet_ap.rearrange("(a c) -> a c", c=128)))
            W.w(sy, "sy", DW, DW.n)
            DW.tick(sy.dma_start(out=bv_row[:], in_=bv_ap.rearrange("(o d) -> o d", o=1)))
            W.w(sy, "sy", DW, DW.n)
            DW.tick(sy.dma_start(out=bo_row[:], in_=bo_ap.rearrange("(o d) -> o d", o=1)))
            W.w(sy, "sy", DW, DW.n)
            DW.tick(sy.dma_start(out=br_row[:], in_=br_ap.rearrange("(o d) -> o d", o=1)))
            W.w(sy, "sy", DW, DW.n)
            DW.tick(sy.dma_start(out=b2_row[:], in_=b2_ap))
        dma_const = DW.n

        @block.gpsimd
        def _(g):
            CG.tick(g.iota(iota_f[:], [[1, 128]], channel_multiplier=-1,
                           allow_small_or_imprecise_dtypes=True))
            CG.tick(g.iota(iota_row[:], [[1, S]], channel_multiplier=0,
                           allow_small_or_imprecise_dtypes=True))
            CG.tick(g.iota(iota_c[:], [[128, S // 128]], channel_multiplier=1,
                           allow_small_or_imprecise_dtypes=True))
        t_iota = CG.n

        @block.vector
        def _(v):
            W.w(v, "v", CG, t_iota)
            W.w(v, "v", CV, CV.n)
            CV.tick(v.tensor_scalar(id_bf[:], iota_f[:], 0.0, None, ALU.is_equal))
            W.w(v, "v", CV, CV.n)
            CV.tick(v.tensor_scalar(id_f[:], iota_f[:], 0.0, None, ALU.is_equal))
            W.w(v, "v", CV, CV.n)
            CV.tick(v.tensor_scalar(tri_bf[:], iota_f[:], 0.0, None, ALU.is_ge))
            W.w(v, "v", CV, CV.n)
            CV.tick(v.memset(ones_r[:], 1.0))
            W.w(v, "v", CV, CV.n)
            CV.tick(v.memset(ones_c[:], 1.0))
            W.w(v, "v", DW, dma_const)
            W.w(v, "v", CV, CV.n)
            CV.tick(v.tensor_copy(wr_bf[:], wr_st[:]))
            W.w(v, "v", CV, CV.n)
            CV.tick(v.tensor_copy(bv_bf[:], bv_row[:]))
            W.w(v, "v", CV, CV.n)
            CV.tick(v.tensor_copy(bo_bf[:], bo_row[:]))
            W.w(v, "v", CV, CV.n)
            CV.tick(v.tensor_copy(br_bf[:], br_row[:]))
            W.w(v, "v", CV, CV.n)
            CV.tick(v.tensor_copy(b2_bf[:], b2_row[:]))
        @block.vector
        def _(v):
            W.w(v, "v", CV, CV.n)
            CV.tick(v.memset(c_ln8i[:], LN8I))
        t_c8 = CV.n
        t_consts_v = CV.n

        @block.scalar
        def _(s):
            W.w(s, "s", CG, t_iota)
            W.w(s, "s", CV, t_c8)
            W.w(s, "s", CS, CS.n)
            CS.tick(s.activation(cqrow[:], iota_row[:], AF.Exp,
                                 bias=c_ln8i[0:1, 0:1], scale=LN99))
            W.w(s, "s", CS, CS.n)
            CS.tick(s.activation(ckcols[:], iota_c[:], AF.Exp,
                                 bias=0.0, scale=-LN99))
        t_decay_s = CS.n

        with ExitStack() as p0s:
            ps_b = p0s.enter_context(u_ps("ps_b", [128, 512], F32))
            ps_t = p0s.enter_context(u_ps("ps_t", [128, 2048], BF16))
            ps_r = p0s.enter_context(u_ps("ps_r", [128, 512], F32))

            bias_t = [bq_t, bk_t, bg_t, gam_t, bet_t]

            @block.tensor
            def _(t):
                W.w(t, "t", CV, t_consts_v)
                for i in range(5):
                    CT.tick(t.transpose(ps_b[:, i * 8:(i + 1) * 8], rows_b[i][:],
                                        id_f[0:8, 0:8]))
            t_biasT = CT.n

            @block.vector
            def _(v):
                W.w(v, "v", CT, t_biasT)
                for i in range(5):
                    W.w(v, "v", CV, CV.n)
                    CV.tick(v.tensor_copy(bias_t[i][:], ps_b[:, i * 8:(i + 1) * 8]))
            t_biasT_v = CV.n

            NST = NT // 4
            apply_tag = [0] * NST
            stats_tag = [0] * NST
            trans_tag = [0] * NST
            xev_tag = [0] * NST
            rr_tag = [0] * NT
            w_tag = [0] * NT

            for st in range(NST):
                sl = st % 2
                b_i, r_i = divmod(st * 4, NTB)

                @block.scalar
                def _(s, st=st, sl=sl, b_i=b_i, r_i=r_i):
                    if st >= 2:
                        W.w(s, "s", CV, apply_tag[st - 2])
                        W.w(s, "s", CV, stats_tag[st - 2])
                    W.w(s, "s", DX, DX.n)
                    DX.tick(s.dma_start(
                        out=xin[sl][:].rearrange("p (a d) -> p a d", a=4),
                        in_=x_ap[b_i, r_i * 128:(r_i + 4) * 128, :]
                        .rearrange("(a p) d -> p a d", p=128)))
                dx_here = DX.n

                @block.vector
                def _(v, st=st, sl=sl):
                    W.w(v, "v", DX, dx_here)
                    xv = xin[sl][:].rearrange("p (a d) -> p a d", a=4)
                    W.w(v, "v", CV, CV.n)
                    CV.tick(v.tensor_mul(sqb[:], xin[sl][:], xin[sl][:]))
                    CV.tick(v.reduce_sum(st1[:, 0:4].rearrange("p (a o) -> p a o", o=1),
                                         xv, axis=AX.X))
                    W.w(v, "v", CV, CV.n)
                    CV.tick(v.reduce_sum(st1[:, 4:8].rearrange("p (a o) -> p a o", o=1),
                                         sqb[:].rearrange("p (a d) -> p a d", a=4),
                                         axis=AX.X))
                    W.w(v, "v", CV, CV.n)
                    CV.tick(v.tensor_scalar(st1[:, 8:12], st1[:, 0:4], 1.0 / D, None,
                                            ALU.mult))
                    CV.tick(v.tensor_scalar(st1[:, 12:16], st1[:, 4:8], 1.0 / D, None,
                                            ALU.mult))
                    W.w(v, "v", CV, CV.n)
                    CV.tick(v.tensor_mul(st1[:, 16:20], st1[:, 8:12], st1[:, 8:12]))
                    W.w(v, "v", CV, CV.n)
                    CV.tick(v.scalar_tensor_tensor(st1[:, 16:20], st1[:, 16:20], -1.0,
                                                   st1[:, 12:16], ALU.mult, ALU.add))
                    W.w(v, "v", CV, CV.n)
                    CV.tick(v.tensor_scalar(st1[:, 16:20], st1[:, 16:20], LN_EPS, None,
                                            ALU.add))
                vtag1 = CV.n

                @block.scalar
                def _(s, st=st, sl=sl):
                    W.w(s, "s", CS, CS.n)
                    W.w(s, "s", CV, vtag1)
                    CS.tick(s.activation(st1[:, 16:20], st1[:, 16:20], AF.Ln))
                    W.w(s, "s", CS, CS.n)
                    CS.tick(s.activation(st1[:, 20:24], st1[:, 16:20], AF.Exp,
                                         scale=-0.5))
                rstd_tag = CS.n

                @block.vector
                def _(v, st=st):
                    W.w(v, "v", CS, rstd_tag)
                    W.w(v, "v", CV, CV.n)
                    CV.tick(v.scalar_tensor_tensor(st1[:, 8:12], st1[:, 8:12], -1.0,
                                                   st1[:, 20:24], ALU.mult, ALU.mult))
                    stats_tag[st] = CV.n
                negm_tag = CV.n

                @block.scalar
                def _(s, st=st, sl=sl):
                    W.w(s, "s", CV, negm_tag)
                    if st >= 2:
                        W.w(s, "s", CT, trans_tag[st - 2])
                    for a in range(4):
                        W.w(s, "s", CS, CS.n)
                        CS.tick(s.activation(xnk[sl][:, a * D:(a + 1) * D],
                                             xin[sl][:, a * D:(a + 1) * D],
                                             AF.Identity, bias=st1[:, 8 + a:9 + a],
                                             scale=st1[:, 20 + a:21 + a]))
                    apply_tag[st] = CS.n

                sub_ev = [0] * 4
                for a in range(4):
                    @block.tensor
                    def _(t, st=st, sl=sl, a=a):
                        W.w(t, "t", CS, apply_tag[st])
                        if a >= 2:
                            W.w(t, "t", CV, sub_ev[a - 2])
                        elif st >= 1:
                            W.w(t, "t", CV, xev_tag[st - 1])
                        for dc in range(DC):
                            CT.tick(t.transpose(
                                ps_t[:, (a % 2) * 1024 + dc * 128:
                                     (a % 2) * 1024 + (dc + 1) * 128],
                                xnk[sl][:, a * D + dc * 128: a * D + (dc + 1) * 128],
                                id_bf[:]))
                    tr_here = CT.n

                    @block.vector
                    def _(v, st=st, a=a, tr_here=tr_here):
                        W.w(v, "v", CT, tr_here)
                        W.w(v, "v", CV, CV.n)
                        i = st * 4 + a
                        CV.tick(v.tensor_copy(
                            xn_t[:].rearrange("p (dc t) -> p dc t", dc=DC)
                            [:, :, i * 128:(i + 1) * 128],
                            ps_t[:, (a % 2) * 1024:(a % 2) * 1024 + 1024]
                            .rearrange("p (dc t) -> p dc t", dc=DC)))
                    sub_ev[a] = CV.n
                trans_tag[st] = CT.n
                xev_tag[st] = CV.n

                for a in range(4):
                    i = st * 4 + a

                    @block.tensor
                    def _(t, i=i, a=a):
                        W.w(t, "t", CV, sub_ev[a])
                        if i >= 1:
                            W.w(t, "t", CV, rr_tag[i - 1])
                        for dc in range(DC):
                            CT.tick(t.matmul(ps_r[:, 0:8],
                                             xn_t[:, dc * T + i * 128: dc * T + (i + 1) * 128],
                                             wr_bf[:, dc * 8:(dc + 1) * 8],
                                             start=(dc == 0), stop=False))
                        CT.tick(t.matmul(ps_r[:, 0:8], ones_r[0:1, :], br_bf[0:1, :],
                                         start=False, stop=True))
                    rmm_tag = CT.n

                    @block.vector
                    def _(v, i=i, rmm_tag=rmm_tag):
                        W.w(v, "v", CT, rmm_tag)
                        W.w(v, "v", CV, CV.n)
                        CV.tick(v.tensor_copy(ltb[:], ps_r[:, 0:8]))
                        rr_tag[i] = CV.n
                        W.w(v, "v", CV, CV.n)
                        CV.tick(v.reduce_max(eqb[:, 4:5], ltb[:], axis=AX.X))
                        W.w(v, "v", CV, CV.n)
                        CV.tick(v.tensor_scalar(eqb[:, 5:6], eqb[:, 4:5], -1.0, None, ALU.mult))
                        CV.tick(v.tensor_scalar(eqb[:, 6:7] if False else kpb[:], ltb[:],
                                                eqb[:, 4:5], None, ALU.is_equal))
                        W.w(v, "v", CV, CV.n)
                        CV.tick(v.scalar_tensor_tensor(t2b[:], kpb[:], -3.0e38, ltb[:],
                                                       ALU.mult, ALU.add))
                        W.w(v, "v", CV, CV.n)
                        CV.tick(v.reduce_max(eqb[:, 0:1], t2b[:], axis=AX.X))
                        W.w(v, "v", CV, CV.n)
                        CV.tick(v.tensor_scalar(kpb[:], ltb[:], eqb[:, 0:1], None, ALU.is_ge))
                    ex_tag = CV.n

                    @block.scalar
                    def _(s, i=i, ex_tag=ex_tag):
                        W.w(s, "s", CS, CS.n)
                        W.w(s, "s", CV, ex_tag)
                        CS.tick(s.activation(exb[:], ltb[:], AF.Exp, bias=eqb[:, 5:6]))
                    exs_tag = CS.n

                    @block.vector
                    def _(v, i=i, exs_tag=exs_tag):
                        W.w(v, "v", CS, exs_tag)
                        W.w(v, "v", CV, CV.n)
                        CV.tick(v.tensor_mul(ekb[:], exb[:], kpb[:]))
                        W.w(v, "v", CV, CV.n)
                        CV.tick(v.reduce_sum(eqb[:, 4:5], ekb[:], axis=AX.X))
                        W.w(v, "v", CV, CV.n)
                        CV.tick(v.reciprocal(eqb[:, 5:6], eqb[:, 4:5]))
                        W.w(v, "v", CV, CV.n)
                        CV.tick(v.tensor_scalar(wrt[:, i * 8:(i + 1) * 8], ekb[:],
                                                eqb[:, 5:6], None, ALU.mult))
                        w_tag[i] = CV.n

            p0_done_v = CV.n
        p0b.close()
        p0_floor = (CT.n, CV.n, CS.n, DW.n, DX.n)

        a2_done_s = 0
        a2_done_v = 0
        stt_tag = [0] * NTB
        prev_add_last = 0
        prev_odma = 0

        pre_out_floor = None
        sB_prev = [None]
        for b in range(B):
            tok0 = b * S
            sB = ExitStack()

            with ExitStack() as sA:
                retained = sA.enter_context(
                    u_sb("ret", [128, DC * S], BF16))
                s01 = sA.enter_context(ExitStack())
                Vt = s01.enter_context(u_sb("Vt", [128, NTB * D], BF16))

                ph_floor = pre_out_floor if pre_out_floor else floor_now()
                cs_floor = ph_floor[2]
                with ExitStack() as s0:
                    wv_bf = s0.enter_context(u_sb("wv_bf", [128, DC * D], BF16))
                    stgv = s0.enter_context(u_sb("stgv", [128, DC * 128], F32))
                    ps_v = [s0.enter_context(u_ps(f"ps_v{i}", [128, 1024], F32))
                            for i in range(2)]

                    vcast = []
                    for hf in range(8):
                        @block.sync
                        def _(sy, hf=hf):
                            wait_floor(sy, "sy", ph_floor)
                            if hf >= 1:
                                W.w(sy, "sy", CV, vcast[hf - 1])
                            W.w(sy, "sy", DW, DW.n)
                            DW.tick(sy.dma_start(
                                out=stgv[:].rearrange("p (dc n) -> p dc n", dc=DC),
                                in_=wv_ap[:, hf * 128:(hf + 1) * 128]
                                .rearrange("(dc p) n -> p dc n", p=128)))
                        dwh = DW.n

                        @block.vector
                        def _(v, hf=hf):
                            W.w(v, "v", DW, dwh)
                            W.w(v, "v", CV, CV.n)
                            CV.tick(v.tensor_copy(
                                wv_bf[:].rearrange("p (dc n) -> p dc n", dc=DC)
                                [:, :, hf * 128:(hf + 1) * 128],
                                stgv[:].rearrange("p (dc n) -> p dc n", dc=DC)))
                        vcast.append(CV.n)
                    wv_ready = vcast[-1]

                    cs_v0 = CS.reserve(NTB)
                    vmm = [0] * NTB
                    for tt in range(NTB):
                        par = tt % 2

                        @block.tensor
                        def _(t, tt=tt, par=par):
                            W.w(t, "t", CV, wv_ready)
                            W.w(t, "t", CS, cs_floor)
                            if tt >= 2:
                                W.w(t, "t", CS, cs_v0 + tt - 1)
                            for dc in range(DC):
                                for n2 in range(2):
                                    CT.tick(t.matmul(
                                        ps_v[par][:, n2 * 512:(n2 + 1) * 512],
                                        xn_t[:, dc * T + tok0 + tt * 128:
                                             dc * T + tok0 + (tt + 1) * 128],
                                        wv_bf[:, dc * D + n2 * 512: dc * D + (n2 + 1) * 512],
                                        start=(dc == 0), stop=False))
                            for n2 in range(2):
                                CT.tick(t.matmul(ps_v[par][:, n2 * 512:(n2 + 1) * 512],
                                                 ones_r[0:1, :],
                                                 bv_bf[0:1, n2 * 512:(n2 + 1) * 512],
                                                 start=False, stop=True))
                            vmm[tt] = CT.n

                    @block.scalar
                    def _(s):
                        for tt in range(NTB):
                            W.w(s, "s", CT, vmm[tt])
                            CS.tick_raw(s.activation(Vt[:, tt * D:(tt + 1) * D],
                                                     ps_v[tt % 2][:], AF.Copy))
                    assert CS.n == cs_v0 + NTB

                a1_floor = floor_now()
                with ExitStack() as s1:
                    qp = s1.enter_context(u_sb("qp", [128, S], BF16))
                    kp = s1.enter_context(u_sb("kp", [128, S], BF16))
                    cq = s1.enter_context(u_sb("cq", [128, S], BF16))
                    wqk_st = s1.enter_context(u_sb("wqk_st", [128, DC * 32], F32))
                    wq_bf = s1.enter_context(u_sb("wq_bf", [128, DC * 128], BF16))
                    wk_bf = s1.enter_context(u_sb("wk_bf", [128, DC * 128], BF16))
                    st_buf = s1.enter_context(u_sb("st_buf", [128, 3 * IC], BF16))
                    bmask = s1.enter_context(u_sb("bmask", [128, 4 * 512], BF16))
                    ps_qk = s1.enter_context(u_ps("ps_qk", [128, S], F32))
                    ps_s = [s1.enter_context(u_ps(f"ps_s{i}", [128, IC], F32))
                            for i in range(3)]
                    ps_ret = [s1.enter_context(u_ps(f"ps_r{i}", [64, IC], F32))
                              for i in range(1)]

                    @block.vector
                    def _(v):
                        W.w(v, "v", CT, a1_floor[0])
                        W.w(v, "v", CS, a1_floor[2])
                        for bp in range(4):
                            for isub in range(4):
                                dst = bmask[:, bp * 512 + isub * 128:
                                            bp * 512 + (isub + 1) * 128]
                                W.w(v, "v", CV, CV.n)
                                if isub < bp:
                                    CV.tick(v.memset(dst, 0.0))
                                elif isub == bp:
                                    CV.tick(v.tensor_copy(dst, tri_bf[:]))
                                else:
                                    CV.tick(v.memset(dst, 1.0))

                    dec_cp = [0, 0]
                    for which2, (dst2, rw) in enumerate(((cq, cqrow),)):
                        for c in range(S // 512):
                            spar = c % 2

                            @block.tensor
                            def _(t, which2=which2, dst2=dst2, rw=rw, c=c, spar=spar):
                                W.w(t, "t", CS, t_decay_s)
                                W.w(t, "t", CS, a1_floor[2])
                                W.w(t, "t", CV, dec_cp[spar])
                                W.w(t, "t", CV, a1_floor[1])
                                CT.tick(t.matmul(ps_s[spar][:],
                                                 ones_r[0:1, :],
                                                 rw[0:1, c * 512:(c + 1) * 512],
                                                 start=True, stop=True))
                            mm_here = CT.n

                            @block.vector
                            def _(v, dst2=dst2, c=c, spar=spar, mm_here=mm_here):
                                W.w(v, "v", CT, mm_here)
                                W.w(v, "v", CV, CV.n)
                                CV.tick(v.tensor_copy(dst2[:, c * 512:(c + 1) * 512],
                                                      ps_s[spar][:]))
                            dec_cp[spar] = CV.n
                    sev_init = list(dec_cp)

                    qcast = [0] * HP
                    kcast = [0] * HP
                    qmm = [0] * HP
                    kmm = [0] * HP
                    qev = [0] * HP
                    kev = [0] * HP
                    sev_last = sev_init + [0]
                    slot_av = [0, 0, 0]
                    ret_ev = [0, 0]
                    u_idx = 0

                    for hp in range(HP):
                        for which in range(2):
                            wap = wq_ap if which == 0 else wk_ap
                            wbf = wq_bf if which == 0 else wk_bf

                            for mh in range(4):
                                @block.sync
                                def _(sy, hp=hp, which=which, wap=wap, mh=mh):
                                    wait_floor(sy, "sy", a1_floor)
                                    prev = 0
                                    if mh >= 1:
                                        prev = half_cast
                                    elif which == 1:
                                        prev = qcast[hp]
                                    elif hp > 0:
                                        prev = kcast[hp - 1]
                                    W.w(sy, "sy", CV, prev)
                                    W.w(sy, "sy", DW, DW.n)
                                    DW.tick(sy.dma_start(
                                        out=wqk_st[:].rearrange("p (dc m) -> p dc m", dc=DC // 4),
                                        in_=wap[mh * 256:(mh + 1) * 256, hp * 128:(hp + 1) * 128]
                                        .rearrange("(dc p) m -> p dc m", p=128)))
                                dwh = DW.n

                                @block.vector
                                def _(v, hp=hp, which=which, wbf=wbf, mh=mh):
                                    W.w(v, "v", DW, dwh)
                                    if hp > 0:
                                        W.w(v, "v", CT, (qmm if which == 0 else kmm)[hp - 1])
                                    W.w(v, "v", CV, CV.n)
                                    CV.tick(v.tensor_copy(
                                        wbf[:].rearrange("p (dc m) -> p dc m", dc=DC)
                                        [:, mh * (DC // 4):(mh + 1) * (DC // 4), :],
                                        wqk_st[:].rearrange("p (dc m) -> p dc m", dc=DC // 4)))
                                half_cast = CV.n
                            if which == 0:
                                qcast[hp] = CV.n
                            else:
                                kcast[hp] = CV.n

                        @block.tensor
                        def _(t, hp=hp):
                            W.w(t, "t", CV, qcast[hp])
                            if hp > 0:
                                W.w(t, "t", CV, kev[hp - 1])
                            for dc in range(DC):
                                for ncs in range(S // 512):
                                    CT.tick(t.matmul(
                                        ps_qk[:, ncs * 512:(ncs + 1) * 512],
                                        wq_bf[:, dc * 128:(dc + 1) * 128],
                                        xn_t[:, dc * T + tok0 + ncs * 512:
                                             dc * T + tok0 + (ncs + 1) * 512],
                                        start=(dc == 0), stop=(dc == DC - 1)))
                            qmm[hp] = CT.n

                        @block.vector
                        def _(v, hp=hp):
                            W.w(v, "v", CT, qmm[hp])
                            W.w(v, "v", CV, CV.n)
                            CV.tick(v.scalar_tensor_tensor(qp[:], ps_qk[:],
                                                           bq_t[:, hp:hp + 1], cq[:],
                                                           ALU.add, ALU.mult))
                            qev[hp] = CV.n

                        @block.tensor
                        def _(t, hp=hp):
                            W.w(t, "t", CV, kcast[hp])
                            W.w(t, "t", CV, qev[hp])
                            for dc in range(DC):
                                for ncs in range(S // 512):
                                    CT.tick(t.matmul(
                                        ps_qk[:, ncs * 512:(ncs + 1) * 512],
                                        wk_bf[:, dc * 128:(dc + 1) * 128],
                                        xn_t[:, dc * T + tok0 + ncs * 512:
                                             dc * T + tok0 + (ncs + 1) * 512],
                                        start=(dc == 0), stop=(dc == DC - 1)))
                            kmm[hp] = CT.n

                        @block.vector
                        def _(v, hp=hp):
                            W.w(v, "v", CT, kmm[hp])
                            W.w(v, "v", CV, CV.n)
                            CV.tick(v.tensor_scalar(kp[:], ps_qk[:],
                                                    bk_t[:, hp:hp + 1], None, ALU.add))
                            kev[hp] = CV.n

                        for h2 in range(2):
                            po = 64 * h2
                            hgl = 2 * hp + h2
                            for ic in range(NIC):
                                njt = (ic + 1) * (IC // 128)
                                rpar = u_idx % 2
                                smm = [0] * njt
                                sev = [0] * njt
                                avt = [0] * njt

                                def S_sec(jt, hp=hp, po=po, ic=ic, smm=smm, sev=sev):
                                    @block.tensor
                                    def _(t, jt=jt):
                                        spar = jt % 3
                                        W.w(t, "t", CV,
                                            sev[jt - 3] if jt >= 3 else sev_last[spar])
                                        W.w(t, "t", CV, qev[hp])
                                        W.w(t, "t", CV, kev[hp])
                                        CT.tick(t.matmul(
                                            ps_s[spar][:],
                                            kp[po:po + 64, jt * 128:(jt + 1) * 128],
                                            qp[po:po + 64, ic * IC:(ic + 1) * IC],
                                            start=True, stop=True))
                                        smm[jt] = CT.n

                                def EV_sec(jt, ic=ic, smm=smm, sev=sev, avt=avt):
                                    @block.vector
                                    def _(v, jt=jt):
                                        spar = jt % 3
                                        W.w(v, "v", CT, smm[jt])
                                        W.w(v, "v", CT,
                                            avt[jt - 3] if jt >= 3 else slot_av[jt % 3])
                                        dst0 = (jt % 3) * IC
                                        band0 = ic * (IC // 128)
                                        ckc = ckcols[:, jt:jt + 1]
                                        W.w(v, "v", CV, CV.n)
                                        if jt < band0:
                                            CV.tick(v.tensor_scalar(
                                                st_buf[:, dst0:dst0 + IC],
                                                ps_s[spar][:], ckc, None, ALU.mult))
                                        else:
                                            bp = jt - band0
                                            CV.tick(v.scalar_tensor_tensor(
                                                st_buf[:, dst0:dst0 + IC],
                                                ps_s[spar][:], ckc,
                                                bmask[:, bp * 512:(bp + 1) * 512],
                                                ALU.mult, ALU.mult))
                                        sev[jt] = CV.n

                                def AV_sec(jt, hgl=hgl, njt=njt, rpar=rpar,
                                           sev=sev, avt=avt):
                                    @block.tensor
                                    def _(t, jt=jt):
                                        W.w(t, "t", CV, sev[jt])
                                        if jt == 0:
                                            W.w(t, "t", CV, ret_ev[0])
                                        CT.tick(t.matmul(
                                            ps_ret[0][:],
                                            Vt[:, jt * D + hgl * 64: jt * D + hgl * 64 + 64],
                                            st_buf[:, (jt % 3) * IC:(jt % 3 + 1) * IC],
                                            start=(jt == 0), stop=(jt == njt - 1)))
                                        avt[jt] = CT.n

                                S_sec(0)
                                if njt > 1:
                                    S_sec(1)
                                    EV_sec(0)
                                for jt in range(2, njt):
                                    S_sec(jt)
                                    EV_sec(jt - 1)
                                    AV_sec(jt - 2)
                                if njt > 1:
                                    EV_sec(njt - 1)
                                    AV_sec(njt - 2)
                                else:
                                    EV_sec(0)
                                AV_sec(njt - 1)

                                for jt in range(njt):
                                    slot_av[jt % 3] = max(slot_av[jt % 3], avt[jt])
                                for jt in range(njt):
                                    sev_last[jt % 2] = max(sev_last[jt % 2], sev[jt])
                                av_last = avt[njt - 1]

                                @block.vector
                                def _(v, hp=hp, po=po, ic=ic, av_last=av_last):
                                    W.w(v, "v", CT, av_last)
                                    W.w(v, "v", CV, CV.n)
                                    CV.tick(v.tensor_copy(
                                        retained[po:po + 64,
                                                 hp * S + ic * IC: hp * S + (ic + 1) * IC],
                                        ps_ret[0][:]))
                                    ret_ev[0] = CV.n
                                u_idx += 1
                    a1_done_v = CV.n
                    if DBG and b == 0:
                        dbf = s1.enter_context(u_sb("dbf", [128, S], F32))
                        dcp = [0]

                        @block.vector
                        def _(v):
                            W.w(v, "v", CV, a1_done_v)
                            W.w(v, "v", CV, CV.n)
                            CV.tick(v.tensor_copy(dbf[:], qp[:]))
                        dcp[0] = CV.n

                        @block.sync
                        def _(sy):
                            W.w(sy, "sy", CV, dcp[0])
                            DW.tick(sy.dma_start(out=dbg_qp, in_=dbf[:]))
                        dw_q = DW.n

                        @block.vector
                        def _(v):
                            W.w(v, "v", DW, dw_q)
                            W.w(v, "v", CV, CV.n)
                            CV.tick(v.tensor_copy(dbf[:], kp[:]))
                        dcp[0] = CV.n

                        @block.sync
                        def _(sy):
                            W.w(sy, "sy", CV, dcp[0])
                            DW.tick(sy.dma_start(out=dbg_kp, in_=dbf[:]))
                            sy.wait_ge(DW.sem, DW.n)

                        @block.vector
                        def _(v):
                            W.w(v, "v", CV, CV.n)
                            CV.tick(v.tensor_copy(dbf[:], cq[:]))
                        dcp[0] = CV.n

                        @block.sync
                        def _(sy):
                            W.w(sy, "sy", CV, dcp[0])
                            DW.tick(sy.dma_start(out=dbg_cq, in_=dbf[:]))
                            sy.wait_ge(DW.sem, DW.n)

                        @block.vector
                        def _(v):
                            W.w(v, "v", CV, CV.n)
                            CV.tick(v.tensor_copy(dbf[:], ck[:]))
                        dcp[0] = CV.n

                        @block.sync
                        def _(sy):
                            W.w(sy, "sy", CV, dcp[0])
                            DW.tick(sy.dma_start(out=dbg_ck, in_=dbf[:]))
                            sy.wait_ge(DW.sem, DW.n)

                        for dcx in range(DC):
                            @block.vector
                            def _(v, dcx=dcx):
                                W.w(v, "v", CV, CV.n)
                                CV.tick(v.tensor_copy(dbf[:, 0:S],
                                                      xn_t[:, dcx * T: dcx * T + S]))
                            dctx = CV.n

                            @block.sync
                            def _(sy, dcx=dcx):
                                W.w(sy, "sy", CV, dctx)
                                DW.tick(sy.dma_start(out=dbg_xn[:, dcx * T: dcx * T + S],
                                                     in_=dbf[:, 0:S]))
                                sy.wait_ge(DW.sem, DW.n)

                        for dc in range(DC):
                            @block.vector
                            def _(v, dc=dc):
                                W.w(v, "v", CV, CV.n)
                                CV.tick(v.tensor_copy(
                                    dbf[:].rearrange("p (a s) -> p a s", a=1) if False
                                    else dbf[:, 0:S],
                                    retained[:, dc * S:(dc + 1) * S]))
                            dct = CV.n

                            @block.sync
                            def _(sy, dc=dc):
                                W.w(sy, "sy", CV, dct)
                                DW.tick(sy.dma_start(out=dbg_ret[:, dc * S:(dc + 1) * S],
                                                     in_=dbf[:, 0:S]))
                                sy.wait_ge(DW.sem, DW.n)
                s01.close()

                a2_floor = floor_now()
                if sB_prev[0] is not None:
                    sB_prev[0].close()
                acc = sB.enter_context(
                    u_sb("acc", [128, NTB * D], F16, side="right"))
                with ExitStack() as s2:
                    wg_bf = s2.enter_context(u_sb("wg_bf", [128, DC * D], BF16))
                    wo_bf = s2.enter_context(u_sb("wo_bf", [128, DC * D], BF16))
                    Gc = s2.enter_context(u_sb("Gc", [128, DC * IC2], BF16))
                    ngb = s2.enter_context(u_sb("ngb", [128, DC * IC2], BF16))
                    sqf = s2.enter_context(u_sb("sqf", [128, DC * IC2], BF16))
                    t1b = s2.enter_context(u_sb("t1b", [128, IC2], BF16))
                    t2c = s2.enter_context(u_sb("t2c", [128, IC2], BF16))
                    rowf = s2.enter_context(u_sb("rowf", [1, 3 * IC2], F32))
                    rall = s2.enter_context(u_sb("rall", [1, NIC2 * 2 * IC2], BF16))
                    ps_st = s2.enter_context(u_ps("ps_st", [1, 512], F32))
                    ps_bc = s2.enter_context(u_ps("ps_bc", [128, 512], F32))
                    ps_g = [s2.enter_context(u_ps(f"ps_g{i}", [128, 512], F32))
                            for i in range(2)]
                    ps_o = [s2.enter_context(u_ps(f"ps_o{i}", [128, D], F32))
                            for i in range(2)]

                    s2s = s2.enter_context(ExitStack())
                    stg4 = s2s.enter_context(u_sb("stg4", [128, DC * 64], F32))
                    wcast = 0
                    for which, (wap2, wbf2) in enumerate(((wg_ap, wg_bf), (wo_ap, wo_bf))):
                        for q in range(16):
                            @block.sync
                            def _(sy, wap2=wap2, q=q):
                                wait_floor(sy, "sy", a2_floor)
                                W.w(sy, "sy", CV, wcast)
                                W.w(sy, "sy", DW, DW.n)
                                DW.tick(sy.dma_start(
                                    out=stg4[:].rearrange("p (dc n) -> p dc n", dc=DC),
                                    in_=wap2[:, q * 64:(q + 1) * 64]
                                    .rearrange("(dc p) n -> p dc n", p=128)))
                            dwh = DW.n

                            @block.vector
                            def _(v, which=which, wbf2=wbf2, q=q):
                                W.w(v, "v", DW, dwh)
                                W.w(v, "v", CV, CV.n)
                                if which == 0:
                                    gt, qc = divmod(q, 2)
                                    CV.tick(v.tensor_copy(
                                        wbf2[:, gt * D:(gt + 1) * D]
                                        .rearrange("p (dc c) -> p dc c", dc=DC)
                                        [:, :, qc * 64:(qc + 1) * 64],
                                        stg4[:].rearrange("p (dc n) -> p dc n", dc=DC)))
                                else:
                                    CV.tick(v.tensor_copy(
                                        wbf2[:].rearrange("p (dc n) -> p dc n", dc=DC)
                                        [:, :, q * 64:(q + 1) * 64],
                                        stg4[:].rearrange("p (dc n) -> p dc n", dc=DC)))
                            wcast = CV.n
                    wgo_ready = wcast

                    sq_read = 0
                    st_read = 0
                    sq_tagv = [0] * NIC2
                    stat_mmv = [0] * NIC2
                    rall_tag = [0] * NIC2

                    def emit_prepass(ic):
                        nonlocal st_read
                        @block.vector
                        def _(v, ic=ic):
                            W.w(v, "v", CV, a1_done_v)
                            W.w(v, "v", CT, stat_mmv[ic - 1] if ic >= 1 else 0)
                            for dc in range(DC):
                                W.w(v, "v", CV, CV.n)
                                CV.tick(v.tensor_mul(
                                    sqf[:, dc * IC2:(dc + 1) * IC2],
                                    retained[:, dc * S + ic * IC2: dc * S + (ic + 1) * IC2],
                                    retained[:, dc * S + ic * IC2: dc * S + (ic + 1) * IC2]))
                        sq_tagv[ic] = CV.n

                        @block.tensor
                        def _(t, ic=ic):
                            W.w(t, "t", CV, sq_tagv[ic])
                            W.w(t, "t", CV, st_read)
                            for dc in range(DC):
                                CT.tick(t.matmul(
                                    ps_st[0:1, 0:IC2], ones_c[:, 0:1],
                                    retained[:, dc * S + ic * IC2: dc * S + (ic + 1) * IC2],
                                    start=(dc == 0), stop=False))
                                CT.tick(t.matmul(
                                    ps_st[0:1, IC2:2 * IC2], ones_c[:, 0:1],
                                    sqf[:, dc * IC2:(dc + 1) * IC2],
                                    start=False, stop=(dc == DC - 1)))
                        stat_mmv[ic] = CT.n

                        @block.vector
                        def _(v, ic=ic):
                            W.w(v, "v", CT, stat_mmv[ic])
                            W.w(v, "v", CV, CV.n)
                            CV.tick(v.tensor_scalar(rowf[0:1, 0:IC2], ps_st[0:1, 0:IC2],
                                                    1.0 / D, None, ALU.mult))
                            CV.tick(v.tensor_scalar(rowf[0:1, IC2:2 * IC2],
                                                    ps_st[0:1, IC2:2 * IC2],
                                                    1.0 / D, None, ALU.mult))
                            W.w(v, "v", CV, CV.n)
                            CV.tick(v.tensor_mul(rowf[0:1, 2 * IC2:3 * IC2],
                                                 rowf[0:1, 0:IC2], rowf[0:1, 0:IC2]))
                            W.w(v, "v", CV, CV.n)
                            CV.tick(v.scalar_tensor_tensor(
                                rowf[0:1, 2 * IC2:3 * IC2], rowf[0:1, 2 * IC2:3 * IC2],
                                -1.0, rowf[0:1, IC2:2 * IC2], ALU.mult, ALU.add))
                            W.w(v, "v", CV, CV.n)
                            CV.tick(v.tensor_scalar(rowf[0:1, 2 * IC2:3 * IC2],
                                                    rowf[0:1, 2 * IC2:3 * IC2],
                                                    LN_EPS, None, ALU.add))
                        var_tag = CV.n
                        st_read = var_tag

                        @block.scalar
                        def _(s, ic=ic, var_tag=var_tag):
                            W.w(s, "s", CV, var_tag)
                            W.w(s, "s", CS, CS.n)
                            CS.tick(s.activation(rowf[0:1, 2 * IC2:3 * IC2],
                                                 rowf[0:1, 2 * IC2:3 * IC2], AF.Ln))
                            W.w(s, "s", CS, CS.n)
                            CS.tick(s.activation(rall[0:1, ic * 512: ic * 512 + IC2],
                                                 rowf[0:1, 2 * IC2:3 * IC2], AF.Exp,
                                                 scale=-0.5))
                        rstd_tag = CS.n

                        @block.vector
                        def _(v, ic=ic, rstd_tag=rstd_tag):
                            W.w(v, "v", CS, rstd_tag)
                            W.w(v, "v", CV, CV.n)
                            CV.tick(v.scalar_tensor_tensor(
                                rall[0:1, ic * 512 + IC2: ic * 512 + 2 * IC2],
                                rowf[0:1, 0:IC2], -1.0,
                                rall[0:1, ic * 512: ic * 512 + IC2],
                                ALU.mult, ALU.mult))
                        rall_tag[ic] = CV.n

                    emit_prepass(0)
                    emit_prepass(1)

                    bc_read = 0
                    ng_used = 0
                    oev_tag = [0, 0]
                    gic_done = [0] * (NIC2 + 1)
                    gc_readv = [0] * (NIC2 + 1)

                    def emit_g(gi):
                        cs_sg0 = CS.reserve(DC)
                        gmm = [0] * DC
                        for gt in range(DC):
                            gpar = gt % 2

                            @block.tensor
                            def _(t, gi=gi, gt=gt, gpar=gpar):
                                W.w(t, "t", CV, wgo_ready)
                                if gt >= 2:
                                    W.w(t, "t", CS, cs_sg0 + gt - 1)
                                elif gi >= 1:
                                    W.w(t, "t", CS, gic_done[gi - 1])
                                for dc in range(DC):
                                    CT.tick(t.matmul(
                                        ps_g[gpar][:, 0:IC2],
                                        wg_bf[:, gt * D + dc * 128: gt * D + (dc + 1) * 128],
                                        xn_t[:, dc * T + tok0 + gi * IC2:
                                             dc * T + tok0 + (gi + 1) * IC2],
                                        start=(dc == 0), stop=(dc == DC - 1)))
                                gmm[gt] = CT.n

                        @block.scalar
                        def _(s, gi=gi):
                            W.w(s, "s", CV, gc_readv[gi - 1] if gi >= 1 else 0)
                            for gt in range(DC):
                                W.w(s, "s", CT, gmm[gt])
                                CS.tick_raw(s.activation(
                                    Gc[:, gt * IC2:(gt + 1) * IC2],
                                    ps_g[gt % 2][:, 0:IC2],
                                    AF.Sigmoid, bias=bg_t[:, gt:gt + 1]))
                        assert CS.n == cs_sg0 + DC
                        gic_done[gi] = cs_sg0 + DC

                    emit_g(0)
                    for ic in range(NIC2):
                        @block.tensor
                        def _(t, ic=ic):
                            W.w(t, "t", CV, rall_tag[ic])
                            W.w(t, "t", CV, bc_read)
                            CT.tick(t.matmul(ps_bc[:, 0:IC2], ones_r[0:1, :],
                                             rall[0:1, ic * 512: ic * 512 + IC2],
                                             start=True, stop=False))
                            CT.tick(t.matmul(ps_bc[:, IC2:2 * IC2], ones_r[0:1, :],
                                             rall[0:1, ic * 512 + IC2: ic * 512 + 2 * IC2],
                                             start=False, stop=True))
                        bc_mm = CT.n

                        @block.vector
                        def _(v, ic=ic, bc_mm=bc_mm):
                            W.w(v, "v", CT, bc_mm)
                            W.w(v, "v", CT, ng_used)
                            W.w(v, "v", CS, gic_done[ic])
                            for dc in range(DC):
                                W.w(v, "v", CV, CV.n)
                                CV.tick(v.scalar_tensor_tensor(
                                    t1b[:],
                                    retained[:, dc * S + ic * IC2: dc * S + (ic + 1) * IC2],
                                    gam_t[:, dc:dc + 1], ps_bc[:, 0:IC2],
                                    ALU.mult, ALU.mult))
                                W.w(v, "v", CV, CV.n)
                                CV.tick(v.scalar_tensor_tensor(
                                    t2c[:], ps_bc[:, IC2:2 * IC2], gam_t[:, dc:dc + 1],
                                    t1b[:], ALU.mult, ALU.add))
                                W.w(v, "v", CV, CV.n)
                                CV.tick(v.scalar_tensor_tensor(
                                    ngb[:, dc * IC2:(dc + 1) * IC2], t2c[:],
                                    bet_t[:, dc:dc + 1],
                                    Gc[:, dc * IC2:(dc + 1) * IC2],
                                    ALU.add, ALU.mult))
                        ng_tag = CV.n
                        bc_read = ng_tag
                        gc_readv[ic] = ng_tag

                        if ic + 1 < NIC2:
                            emit_g(ic + 1)
                        if ic + 2 < NIC2:
                            emit_prepass(ic + 2)

                        for tsub in range(IC2 // 128):
                            opar = tsub % 2
                            tt_loc = ic * (IC2 // 128) + tsub

                            @block.tensor
                            def _(t, ic=ic, tsub=tsub, opar=opar, ng_tag=ng_tag):
                                W.w(t, "t", CV, ng_tag)
                                W.w(t, "t", CV, wgo_ready)
                                W.w(t, "t", CS, oev_tag[opar])
                                for dc in range(DC):
                                    for n2 in range(2):
                                        CT.tick(t.matmul(
                                            ps_o[opar][:, n2 * 512:(n2 + 1) * 512],
                                            ngb[:, dc * IC2 + tsub * 128:
                                                dc * IC2 + (tsub + 1) * 128],
                                            wo_bf[:, dc * D + n2 * 512:
                                                  dc * D + (n2 + 1) * 512],
                                            start=(dc == 0), stop=False))
                                for n2 in range(2):
                                    CT.tick(t.matmul(
                                        ps_o[opar][:, n2 * 512:(n2 + 1) * 512],
                                        ones_r[0:1, :],
                                        bo_bf[0:1, n2 * 512:(n2 + 1) * 512],
                                        start=False, stop=True))
                            omm = CT.n
                            ng_used = max(ng_used, omm)

                            @block.scalar
                            def _(s, tt_loc=tt_loc, opar=opar, omm=omm):
                                W.w(s, "s", CT, omm)
                                W.w(s, "s", CS, CS.n)
                                CS.tick(s.activation(acc[:, tt_loc * D:(tt_loc + 1) * D],
                                                     ps_o[opar][:], AF.Copy))
                                oev_tag[opar] = CS.n
                    a2_done_s = CS.n
                    a2_done_v = CV.n

            f_floor = floor_now()
            stt_tag = [0] * NTB
            with ExitStack() as sF:
                w1c = [sF.enter_context(u_sb(f"w1c{i}", [128, DC * 512], BF16))
                       for i in range(2)]
                w2c = [sF.enter_context(u_sb(f"w2c{i}", [128, NJT * D], BF16))
                       for i in range(2)]
                stg1 = sF.enter_context(u_sb("stg1", [128, DC * 512], F32))
                stg2 = sF.enter_context(u_sb("stg2", [128, NJT * D], F32))
                hb = sF.enter_context(u_sb("hb", [128, NJT * S], BF16))
                b1t = sF.enter_context(u_sb("b1t", [128, NHT], F32))
                b1r = sF.enter_context(u_sb("b1r", [NHT, 128], F32))
                wT_bf = sF.enter_context(u_sb("wT_bf", [8, S], BF16))
                ps_h = [sF.enter_context(u_ps(f"ps_h{i}", [128, TH], F32))
                        for i in range(2)]
                ps_o2 = [sF.enter_context(u_ps(f"ps_o2{i}", [128, D], F32))
                         for i in range(2)]

                wT_last = 0
                for tt in range(NTB):
                    @block.tensor
                    def _(t, tt=tt, wT_last=wT_last):
                        W.w(t, "t", CV, p0_done_v)
                        W.w(t, "t", CS, a2_done_s)
                        W.w(t, "t", CV, a2_done_v)
                        W.w(t, "t", CV, wT_last)
                        CT.tick(t.transpose(ps_h[0][0:8, 0:128],
                                            wrt[:, (b * NTB + tt) * 8:(b * NTB + tt + 1) * 8],
                                            id_f[:]))
                    trt = CT.n

                    @block.vector
                    def _(v, tt=tt, trt=trt):
                        W.w(v, "v", CT, trt)
                        W.w(v, "v", CV, CV.n)
                        CV.tick(v.tensor_copy(wT_bf[0:8, tt * 128:(tt + 1) * 128],
                                              ps_h[0][0:8, 0:128]))
                    wT_last = CV.n
                wT_done = wT_last

                cast1 = {}
                cast2 = {}
                w1mm_last = {}
                w2mm_last = {}
                pso2_read = [0, 0]
                w2_pend = None
                hfree = [0, 0]
                hread = {}
                b1tr = [0] * E
                b1cp = [0] * E

                for e in range(E):
                    @block.scalar
                    def _(s, e=e):
                        wait_floor(s, "s", f_floor)
                        if e >= 1:
                            W.w(s, "s", CT, b1tr[e - 1])
                        W.w(s, "s", DX, DX.n)
                        DX.tick(s.dma_start(out=b1r[:],
                                            in_=b1_ap[e].rearrange("(a c) -> a c", c=128)))
                    dxh = DX.n

                    @block.tensor
                    def _(t, e=e, dxh=dxh):
                        W.w(t, "t", DX, dxh)
                        W.w(t, "t", CV, wT_done)
                        W.w(t, "t", CS, hfree[1])
                        if e >= 1:
                            W.w(t, "t", CV, b1cp[e - 1])
                        CT.tick(t.transpose(ps_h[1][:, 0:NHT], b1r[:], id_f[0:NHT, 0:NHT]))
                        b1tr[e] = CT.n

                    @block.vector
                    def _(v, e=e):
                        W.w(v, "v", CT, b1tr[e])
                        W.w(v, "v", CS, max(hfree))
                        W.w(v, "v", CV, CV.n)
                        CV.tick(v.tensor_copy(b1t[:], ps_h[1][:, 0:NHT]))
                        b1cp[e] = CV.n

                    for jc in range(NJC):
                        k = e * NJC + jc
                        par = k % 2

                        @block.sync
                        def _(sy, e=e, jc=jc, k=k):
                            wait_floor(sy, "sy", f_floor)
                            W.w(sy, "sy", CV, cast1.get(k - 1, 0))
                            W.w(sy, "sy", DW, DW.n)
                            DW.tick(sy.dma_start(
                                out=stg1[:].rearrange("p (dc j) -> p dc j", dc=DC),
                                in_=w1_ap[e][:, jc * JC:(jc + 1) * JC]
                                .rearrange("(dc p) j -> p dc j", p=128)))
                        dw1 = DW.n

                        @block.vector
                        def _(v, k=k, par=par, dw1=dw1):
                            W.w(v, "v", DW, dw1)
                            W.w(v, "v", CT, w1mm_last.get(k - 2, 0))
                            W.w(v, "v", CV, CV.n)
                            CV.tick(v.tensor_copy(w1c[par][:], stg1[:]))
                        cast1[k] = CV.n

                        @block.sync
                        def _(sy, e=e, jc=jc, k=k):
                            wait_floor(sy, "sy", f_floor)
                            W.w(sy, "sy", CV, cast2.get(k - 1, 0))
                            W.w(sy, "sy", DW, DW.n)
                            DW.tick(sy.dma_start(
                                out=stg2[:].rearrange("p (jt d) -> p jt d", jt=NJT),
                                in_=w2_ap[e][jc * JC:(jc + 1) * JC, :]
                                .rearrange("(jt p) d -> p jt d", p=128)))
                        dw2 = DW.n

                        @block.vector
                        def _(v, k=k, par=par, dw2=dw2):
                            W.w(v, "v", DW, dw2)
                            W.w(v, "v", CT, w2mm_last.get(k - 2, 0))
                            W.w(v, "v", CV, CV.n)
                            CV.tick(v.tensor_copy(w2c[par][:], stg2[:]))
                        cast2[k] = CV.n

                        NW1 = NJT * NTH
                        cs_g0 = CS.reserve(NW1)
                        w1_here = [0] * NW1

                        def emit_w1(idx, e=e, jc=jc, k=k, par=par, cs_g0=cs_g0):
                            jt, th = divmod(idx, NTH)
                            hpar = idx % 2
                            if True:

                                @block.tensor
                                def _(t, e=e, jc=jc, jt=jt, th=th, k=k, par=par,
                                      hpar=hpar, idx=idx):
                                    W.w(t, "t", CV, cast1[k])
                                    W.w(t, "t", CV, b1cp[e])
                                    if idx >= 2:
                                        W.w(t, "t", CS, cs_g0 + idx - 1)
                                    else:
                                        W.w(t, "t", CS, hfree[hpar])
                                    for dc in range(DC):
                                        for n2 in range(TH // 512):
                                            CT.tick(t.matmul(
                                                ps_h[hpar][:, n2 * 512:(n2 + 1) * 512],
                                                w1c[par][:, dc * 512 + jt * 128:
                                                     dc * 512 + (jt + 1) * 128],
                                                xn_t[:, dc * T + tok0 + th * TH + n2 * 512:
                                                     dc * T + tok0 + th * TH + (n2 + 1) * 512],
                                                start=(dc == 0), stop=(dc == DC - 1)))
                                    w1mm_last[k] = CT.n
                                    w1_here[idx] = CT.n

                        emit_w1(0)
                        emit_w1(1)
                        if w2_pend is not None:
                            emit_w2(*w2_pend)
                            w2_pend = None
                        for _i in range(2, NW1):
                            emit_w1(_i)

                        @block.scalar
                        def _(s, e=e, jc=jc, k=k):
                            for idx in range(NW1):
                                jt, th = divmod(idx, NTH)
                                ht = jc * NJT + jt
                                W.w(s, "s", CT, w1_here[idx])
                                if idx == 0:
                                    W.w(s, "s", CT, w2mm_last.get(k - 1, 0))
                                CS.tick_raw(s.activation(
                                    hb[:, jt * S + th * TH: jt * S + (th + 1) * TH],
                                    ps_h[idx % 2][:], AF.Gelu, bias=b1t[:, ht:ht + 1]))
                        assert CS.n == cs_g0 + NW1
                        hfree[0] = cs_g0 + NW1 - 1
                        hfree[1] = cs_g0 + NW1
                        if NW1 >= 2 and (NW1 - 2) % 2 == 1:
                            hfree[0], hfree[1] = cs_g0 + NW1, cs_g0 + NW1 - 1
                        gelu_all = cs_g0 + NW1

                        def emit_w2(e2, jc2, k2, par2, gelu_all2):
                            w2_here = [0] * NTB
                            for tth in range(NTB // 4):
                                stt_base = CV.reserve(4)

                                @block.tensor
                                def _(t, e=e2, jc=jc2, tth=tth, k=k2, par=par2,
                                      gelu_all=gelu_all2, stt_base=stt_base):
                                    W.w(t, "t", CS, gelu_all)
                                    W.w(t, "t", CV, cast2[k])
                                    for tt in range(tth * 4, (tth + 1) * 4):
                                        opar = tt % 2
                                        if tt - 2 >= tth * 4:
                                            W.w(t, "t", CV,
                                                stt_base + (tt - 2 - tth * 4) + 1)
                                        else:
                                            W.w(t, "t", CV, pso2_read[opar])
                                        for jt in range(NJT):
                                            last = (jt == NJT - 1) and not (e == 0 and jc == 0)
                                            for n2 in range(2):
                                                CT.tick(t.matmul(
                                                    ps_o2[opar][:, n2 * 512:(n2 + 1) * 512],
                                                    hb[:, jt * S + tt * 128: jt * S + (tt + 1) * 128],
                                                    w2c[par][:, jt * D + n2 * 512:
                                                         jt * D + (n2 + 1) * 512],
                                                    start=(jt == 0), stop=last))
                                        if e == 0 and jc == 0:
                                            for n2 in range(2):
                                                CT.tick(t.matmul(
                                                    ps_o2[opar][:, n2 * 512:(n2 + 1) * 512],
                                                    wT_bf[0:8, tt * 128:(tt + 1) * 128],
                                                    b2_bf[0:8, n2 * 512:(n2 + 1) * 512],
                                                    start=False, stop=True))
                                        w2mm_last[k] = CT.n
                                        hread[k] = CT.n
                                        w2_here[tt] = CT.n

                                for tt in range(tth * 4, (tth + 1) * 4):
                                    opar = tt % 2
                                    mytag = stt_base + (tt - tth * 4) + 1

                                    @block.vector
                                    def _(v, e=e2, tt=tt, opar=opar):
                                        W.w(v, "v", CT, w2_here[tt])
                                        W.w(v, "v", CS, a2_done_s)
                                        W.w(v, "v", CV, stt_tag[tt])
                                        CV.tick_raw(v.scalar_tensor_tensor(
                                            acc[:, tt * D:(tt + 1) * D], ps_o2[opar][:],
                                            wrt[:, (b * NTB + tt) * 8 + e:
                                                (b * NTB + tt) * 8 + e + 1],
                                            acc[:, tt * D:(tt + 1) * D], ALU.mult, ALU.add))
                                    stt_tag[tt] = mytag
                                    pso2_read[opar] = mytag

                        w2_pend = (e, jc, k, par, gelu_all)

                if w2_pend is not None:
                    emit_w2(*w2_pend)
                    w2_pend = None

                pre_out_floor = floor_now()
                add_tag = [0] * NTB
                odma_tag = [0] * NTB
                for tt in range(NTB):
                    @block.scalar
                    def _(s, tt=tt):
                        if tt >= 1:
                            W.w(s, "s", CV, add_tag[tt - 1])
                        if prev_add_last:
                            W.w(s, "s", CV, prev_add_last)
                        W.w(s, "s", DX, DX.n)
                        DX.tick(s.dma_start(out=xo[0][:],
                                            in_=x_ap[b, tt * 128:(tt + 1) * 128, :]))
                    dxh = DX.n

                    @block.vector
                    def _(v, tt=tt, dxh=dxh):
                        W.w(v, "v", DX, dxh)
                        W.w(v, "v", CV, stt_tag[tt])
                        if tt >= 1:
                            W.w(v, "v", DX, odma_tag[tt - 1])
                        elif prev_odma:
                            W.w(v, "v", DX, prev_odma)
                        W.w(v, "v", CV, CV.n)
                        CV.tick(v.tensor_add(oo[0][:], xo[0][:],
                                             acc[:, tt * D:(tt + 1) * D]))
                        add_tag[tt] = CV.n

                    @block.scalar
                    def _(s, tt=tt):
                        W.w(s, "s", CV, add_tag[tt])
                        W.w(s, "s", DX, DX.n)
                        DX.tick(s.dma_start(out=out_ap[b, tt * 128:(tt + 1) * 128, :],
                                            in_=oo[0][:]))
                        odma_tag[tt] = DX.n
                prev_add_last = add_tag[-1]
                prev_odma = odma_tag[-1]
            sB_prev[0] = sB

        if sB_prev[0] is not None:
            sB_prev[0].close()

        @block.scalar
        def _(s):
            s.wait_ge(DX.sem, DX.n)
            s.wait_ge(DG.sem, DG.n)

    return nc


def kernel(**inputs):
    P = dict(FULL)
    x = np.ascontiguousarray(np.asarray(inputs["L0_attn_x"], dtype=np.float32))
    nb = x.shape[0] // N_CORES
    base = {
        "wq": np.ascontiguousarray(np.asarray(inputs["L0_attn_Wq"], dtype=np.float32)),
        "bq": np.ascontiguousarray(np.asarray(inputs["L0_attn_bq"], dtype=np.float32)),
        "wk": np.ascontiguousarray(np.asarray(inputs["L0_attn_Wk"], dtype=np.float32)),
        "bk": np.ascontiguousarray(np.asarray(inputs["L0_attn_bk"], dtype=np.float32)),
        "wv": np.ascontiguousarray(np.asarray(inputs["L0_attn_Wv"], dtype=np.float32)),
        "bv": np.ascontiguousarray(np.asarray(inputs["L0_attn_bv"], dtype=np.float32)),
        "wg": np.ascontiguousarray(np.asarray(inputs["L0_attn_Wg"], dtype=np.float32)),
        "bg": np.ascontiguousarray(np.asarray(inputs["L0_attn_bg"], dtype=np.float32)),
        "wo": np.ascontiguousarray(np.asarray(inputs["L0_attn_Wo"], dtype=np.float32)),
        "bo": np.ascontiguousarray(np.asarray(inputs["L0_attn_bo"], dtype=np.float32)),
        "gam": np.ascontiguousarray(np.asarray(inputs["L0_attn_gamma"], dtype=np.float32)),
        "bet": np.ascontiguousarray(np.asarray(inputs["L0_attn_beta"], dtype=np.float32)),
        "w1": np.ascontiguousarray(np.asarray(inputs["L0_ffn_W1"], dtype=np.float32)),
        "b1": np.ascontiguousarray(np.asarray(inputs["L0_ffn_b1"], dtype=np.float32)),
        "w2": np.ascontiguousarray(np.asarray(inputs["L0_ffn_W2"], dtype=np.float32)),
        "b2": np.ascontiguousarray(np.asarray(inputs["L0_ffn_b2"], dtype=np.float32)),
        "wr": np.ascontiguousarray(np.asarray(inputs["L0_ffn_Wr"], dtype=np.float32)),
        "br": np.ascontiguousarray(np.asarray(inputs["L0_ffn_br"], dtype=np.float32)),
    }
    nc = build_program(P)
    in_maps = []
    for c in range(N_CORES):
        m = dict(base)
        m["x"] = np.ascontiguousarray(x[c * nb:(c + 1) * nb])
        in_maps.append(m)
    res = run_bass_kernel_spmd(nc, in_maps, core_ids=list(range(N_CORES)))
    return np.concatenate([r["out"] for r in res.results], axis=0)
